# version 01
# baseline (speedup 1.0000x reference)
import numpy as np
import ml_dtypes
from contextlib import ExitStack

import concourse.bass as bass
import concourse.mybir as mybir

F32 = mybir.dt.float32
BF16 = mybir.dt.bfloat16
AF = mybir.ActivationFunctionType
ALU = mybir.AluOpType
AX = mybir.AxisListType
BF = ml_dtypes.bfloat16

MAGIC = float(1.5 * 2 ** 23)
CQ = 0.501953125
SOFTCAP = 30.0
ROPE_THETA = 10000.0
LN_EPS = 1e-5
RMS_EPS = float(np.finfo(np.float32).eps)
NEG = -1e30

REAL = dict(T=4096, D=2048, H=16, G=4, HFF=8192)
MINI = dict(T=1024, D=256, H=4, G=2, HFF=512)
MIDI2 = dict(T=1024, D=1536, H=4, G=2, HFF=512, NRING_CAP=2)
MIDI4 = dict(T=4096, D=256, H=2, G=1, HFF=512)


def derived(cfg):
    d = dict(cfg)
    d["DC"] = cfg["D"] // 128
    d["NQT"] = cfg["T"] // 128
    d["NSLOT"] = d["NQT"] // 4
    d["NA_SLOTS"] = d["NSLOT"] // 2
    d["NB_SLOTS"] = d["NSLOT"] - d["NA_SLOTS"]
    d["QWA"] = 128 * d["NA_SLOTS"]
    d["QWB"] = 128 * d["NB_SLOTS"]
    d["NTA"] = 4 * d["NA_SLOTS"]
    d["NTB"] = 4 * d["NSLOT"]
    d["OWN"] = 128 * d["NSLOT"]
    d["HB"] = cfg["H"] // 2
    d["W1R"] = 2 * cfg["HFF"]
    d["NG1"] = cfg["D"] // 64
    d["NG2"] = cfg["HFF"] // 64
    d["NSTRIP"] = d["W1R"] // 512
    d["NH2"] = cfg["HFF"] // 128
    d["REP"] = cfg["H"] // cfg["G"]
    return d


def own_qtiles(cfg, lane):
    d = derived(cfg)
    return [lane + 4 * k for k in range(d["NSLOT"])]


def qcols(cfg, lane, qc):
    d = derived(cfg)
    slots = (range(0, d["NA_SLOTS"]) if qc == 0
             else range(d["NA_SLOTS"], d["NSLOT"]))
    cols = []
    for k in slots:
        t = lane + 4 * k
        cols.extend(range(t * 128, t * 128 + 128))
    return np.array(cols, dtype=np.int64)



def host_inputs(cfg, ins):
    d = derived(cfg)
    T, D, H, G, HFF = cfg["T"], cfg["D"], cfg["H"], cfg["G"], cfg["HFF"]
    x = np.ascontiguousarray(np.asarray(ins["x"], dtype=np.float32))
    assert x.shape == (2, T, D)

    inv = (1.0 / ROPE_THETA ** (np.arange(0, 128, 2, dtype=np.float64) / 128))
    pos = np.arange(T, dtype=np.float64)
    ang = pos[None, :] * np.repeat(inv, 2)[:, None]
    cos_fm = np.cos(ang).astype(np.float32)
    sin_fm = np.sin(ang).astype(np.float32)

    ident = np.eye(128, dtype=BF)
    ident_f = np.eye(128, dtype=np.float32)
    ones_col = np.ones((128, 1), dtype=BF)
    ones_row_f = np.ones((1, 128), dtype=np.float32)
    PT = np.zeros((128, 128), dtype=np.float32)
    for i in range(64):
        PT[2 * i, 2 * i + 1] = 1.0
        PT[2 * i + 1, 2 * i] = -1.0
    PT = PT.astype(BF)

    CSv = 1.0 / (np.float64(30.0) * np.sqrt(np.float64(128.0)))
    slopes = [np.float64(np.float32(2.0) ** np.float32(-(h + 1) * 8.0 / H))
              for h in range(H)]
    abias = np.zeros((128, H * d["NTB"]), dtype=np.float32)
    for h in range(H):
        for s_ in range(d["NTB"]):
            abias[:, h * d["NTB"] + s_] = np.float32(CSv * slopes[h] * (s_ * 128.0))

    bo_fm = np.ascontiguousarray(
        np.asarray(ins["bo"], dtype=np.float32).reshape(d["DC"], 128).T)

    in_maps = []
    for c in range(8):
        b, lane = c // 4, c % 4
        qa, qb = qcols(cfg, lane, 0), qcols(cfg, lane, 1)
        qall = np.concatenate([qa, qb])
        p = np.arange(128, dtype=np.int64)
        DQ = (p[:, None] - qall[None, :]).astype(np.float32)
        MA = np.zeros((d["NTA"], 128, d["QWA"]), dtype=BF)
        for t in range(d["NTA"]):
            MA[t] = ((t * 128 + p)[:, None] <= qa[None, :]).astype(BF)
        MB = np.zeros((d["NTB"], 128, d["QWB"]), dtype=BF)
        for t in range(d["NTB"]):
            MB[t] = ((t * 128 + p)[:, None] <= qb[None, :]).astype(BF)
        im = dict(
            x_all=x[b],
            wq=np.asarray(ins["Wq"], dtype=np.float32),
            wk=np.asarray(ins["Wk"], dtype=np.float32),
            wv=np.asarray(ins["Wv"], dtype=np.float32),
            wo=np.asarray(ins["Wo"], dtype=np.float32),
            bo_fm=bo_fm,
            w1=np.asarray(ins["w_gate_up"], dtype=np.float32),
            w2=np.asarray(ins["w_proj"], dtype=np.float32),
            cos_fm=cos_fm, sin_fm=sin_fm,
            cos_own=np.ascontiguousarray(cos_fm[:, qall]),
            sin_own=np.ascontiguousarray(sin_fm[:, qall]),
            ident=ident, ident_f=ident_f, ones_col=ones_col,
            ones_row_f=ones_row_f, rot_pt=PT,
            x_own=np.ascontiguousarray(x[b][qall, :]),
            abias=abias,
            dq=DQ, ma=MA.reshape(d["NTA"] * 128, d["QWA"]),
            mb=MB.reshape(d["NTB"] * 128, d["QWB"]),
        )
        in_maps.append(im)
    return in_maps


def assemble(cfg, results):
    d = derived(cfg)
    T, D = cfg["T"], cfg["D"]
    out = np.empty((2, T, D), dtype=np.float32)
    for c in range(8):
        b, lane = c // 4, c % 4
        qall = np.concatenate([qcols(cfg, lane, 0), qcols(cfg, lane, 1)])
        out[b, qall, :] = results[c]["out_fm"].T
    return out



class PB:

    def __init__(self, nc, ctx):
        self.nc = nc
        self.ctx = ctx
        self.q = {k: [] for k in "tsvgy"}
        self.semh = {}
        self.cnt = {}

    def sem(self, name):
        if name not in self.semh:
            self.semh[name] = self.ctx.enter_context(self.nc.semaphore(name))
            self.cnt[name] = 0
        return name

    def emit(self, e, fn, waits=(), inc=None, k=1):
        ws = tuple(t for t in waits if t is not None and t[1] > 0)
        tok = None
        if inc is not None:
            self.sem(inc)
            self.cnt[inc] += k
            tok = (inc, self.cnt[inc])
        self.q[e].append((fn, ws, inc, k))
        return tok

    def dma(self, e, fn, waits=(), inc=None):
        return self.emit(e, fn, waits, inc, 16)

    def replay(self, block):
        semh = self.semh

        def runner(e):
            ops = self.q[e]

            def body(eng):
                clock = {}
                for fn, ws, inc, k in ops:
                    for s, v in ws:
                        if v > clock.get(s, 0):
                            eng.wait_ge(semh[s], v)
                            clock[s] = v
                    inst = fn(eng)
                    if inc is not None:
                        inst.then_inc(semh[inc], k)
            return body

        block.tensor(runner("t"))
        block.scalar(runner("s"))
        block.vector(runner("v"))
        block.gpsimd(runner("g"))
        block.sync(runner("y"))


class Slot:

    def __init__(self, P, name):
        self.P = P
        self.name = P.sem(name)
        self.consumers = []

    def load(self, e, fn, extra=()):
        waits = tuple(self.consumers) + tuple(extra)
        self.consumers = []
        return self.P.dma(e, fn, waits, inc=self.name)

    def done(self, tok):
        if tok is not None:
            self.consumers.append(tok)


class Kern:
    def __init__(self, cfg, taps=False, stop_after=None):
        self.cfg = derived(cfg)
        self.taps = taps
        self.stop_after = stop_after
        self.nc = bass.Bass("TRN2", target_bir_lowering=False)
        self.top = ExitStack()
        self.P = PB(self.nc, self.top)
        self.sb = {}
        self.stacks = {}

    def arena(self, key):
        if key not in self.stacks:
            self.stacks[key] = ExitStack()
        return self.stacks[key]

    def close_arena(self, key):
        if key in self.stacks:
            self.stacks.pop(key).close()

    ARENA_SIDE = {"P": "left", "Q": "left", "Lkv": "left", "Lqf": "left",
                  "L3b": "left", "L3": "left", "L4": "left",
                  "L1": "right", "Lro": "right", "L1q": "right",
                  "Lctx": "right", "L2": "right"}

    def sbuf(self, key, name, shape, dtype, **kw):
        h = self.arena(key).enter_context(
            self.nc.sbuf_tensor(name, shape, dtype,
                                side=self.ARENA_SIDE[key], **kw))
        self.sb[name] = h
        return h

    def psum(self, name, shape, dtype):
        h = self.top.enter_context(self.nc.psum_tensor(name, shape, dtype))
        self.sb[name] = h
        return h

    def build(self):
        c = self.cfg
        nc = self.nc
        T, D, DC, OWN = c["T"], c["D"], c["DC"], c["OWN"]
        H, G, HFF = c["H"], c["G"], c["HFF"]

        ap = {}
        EI, EO = "ExternalInput", "ExternalOutput"
        for n_, sh, dt_, kd in [
            ("x_all", [T, D], F32, EI), ("x_own", [OWN, D], F32, EI),
            ("wq", [D, H * 128], F32, EI), ("wk", [D, G * 128], F32, EI),
            ("wv", [D, G * 128], F32, EI), ("wo", [H * 128, D], F32, EI),
            ("bo_fm", [128, DC], F32, EI),
            ("w1", [c["W1R"], D], F32, EI), ("w2", [D, HFF], F32, EI),
            ("cos_fm", [128, T], F32, EI), ("sin_fm", [128, T], F32, EI),
            ("cos_own", [128, OWN], F32, EI), ("sin_own", [128, OWN], F32, EI),
            ("ident", [128, 128], BF16, EI), ("ident_f", [128, 128], F32, EI),
            ("ones_col", [128, 1], BF16, EI), ("ones_row_f", [1, 128], F32, EI),
            ("rot_pt", [128, 128], BF16, EI), ("dq", [128, OWN], F32, EI),
            ("abias", [128, c["H"] * c["NTB"]], F32, EI),
            ("ma", [c["NTA"] * 128, c["QWA"]], BF16, EI),
            ("mb", [c["NTB"] * 128, c["QWB"]], BF16, EI),
            ("out_fm", [D, OWN], F32, EO),
            ("w1h", [c["W1R"], D], BF16, "Internal"),
            ("w2h", [D, HFF], BF16, "Internal"),
            ("x1s", [D, OWN], F32, "Internal"),
        ]:
            ap[n_] = nc.dram_tensor(n_, sh, dt_, kind=kd).ap()
        if self.taps:
            for n_, sh, dt_ in [
                ("t_k", [128, G * T], BF16), ("t_v", [128, (T // 128) * G * 128], BF16),
                ("t_q", [128, H * OWN], BF16), ("t_ctx", [128, H * OWN], BF16),
                ("t_x1", [D, OWN], F32), ("t_ln2", [128, DC * OWN], BF16),
                ("t_w1h", [c["W1R"], D], BF16), ("t_w2h", [D, HFF], BF16),
                ("t_hn", [128, c["NH2"] * OWN], BF16),
            ]:
                ap[n_] = nc.dram_tensor(n_, sh, dt_, kind=EO).ap()
        self.ap = ap

        for i in range(6):
            self.psum(f"pf{i}", [128, 512], F32)
        self.psum("pbA", [128, 1024], BF16)
        self.psum("pbB", [128, 1024], BF16)

        self.sbuf("P", "ident_sb", [128, 128], BF16)
        self.sbuf("P", "ident_f_sb", [128, 128], F32)
        self.sbuf("P", "ones_col_sb", [128, 1], BF16)
        self.sbuf("P", "orf_sb", [1, 128], F32)
        self.sbuf("P", "rot_pt_sb", [128, 128], BF16)
        self.sbuf("P", "bo_fm_sb", [128, DC], F32)
        P = self.P
        P.sem("cl")
        for src, dst in [("ident", "ident_sb"), ("ident_f", "ident_f_sb"),
                         ("ones_col", "ones_col_sb"), ("ones_row_f", "orf_sb"),
                         ("rot_pt", "rot_pt_sb"), ("bo_fm", "bo_fm_sb")]:
            P.dma("y", (lambda e, s=src, d=dst: e.dma_start(
                out=self.sb[d][:], in_=self.ap[s])), inc="cl")
        self.cl = ("cl", P.cnt["cl"])

        self.quant_init()
        self.sbuf("Lkv", "kfm", [128, G * T], BF16)
        self.sbuf("Lkv", "vtm", [128, (T // 128) * G * 128], BF16)
        done = False
        for name, fn in [("p1a", self.phase1a), ("p1b", self.phase1b),
                         ("attn", self.attention), ("wo", self.wo_ln2),
                         ("ffn", self.ffn)]:
            fn()
            if self.stop_after == name:
                done = True
                break
        if done and self.stop_after not in ("ffn", "f1", "f2"):
            self.finish_early()

        block = self.top.enter_context(self.nc.Block())
        P.replay(block)
        for k in reversed(list(self.stacks)):
            self.close_arena(k)
        self.top.close()
        return self.nc

    def finish_early(self):
        P = self.P
        assert self.taps, "finish_early requires taps"
        P.sem("tap")
        if self.stop_after in ("p1a", "p1b") and "kfm" in self.sb:
            P.dma("y", lambda e: e.dma_start(out=self.ap["t_k"],
                                             in_=self.sb["kfm"][:]),
                  waits=[self.kv_done[-1]], inc="tap")
            P.dma("y", lambda e: e.dma_start(out=self.ap["t_v"],
                                             in_=self.sb["vtm"][:]),
                  waits=[self.kv_done[-1]], inc="tap")
        if self.stop_after == "p1b":
            P.dma("y", lambda e: e.dma_start(out=self.ap["t_q"],
                                             in_=self.sb["qfm"][:]),
                  waits=[self.q_done[-1]], inc="tap")
        if self.stop_after == "attn":
            P.dma("y", lambda e: e.dma_start(out=self.ap["t_ctx"],
                                             in_=self.sb["ctxb"][:]),
                  waits=[self.ctx_done[-1]], inc="tap")
        fw = [("tap", P.cnt["tap"])]
        for s in ("qst0", "qst1"):
            if s in P.cnt:
                fw.append((s, P.cnt[s]))
        P.emit("y", lambda e: e.nop(), waits=fw)

    def quant_init(self):
        c = self.cfg
        D, HFF, W1R = c["D"], c["HFF"], c["W1R"]
        WB2 = min(2048, HFF)
        W = max(D, WB2)
        self.qW, self.qWB2 = W, WB2
        ng = W // 64
        self.qw = [self.sbuf("Q", f"qw{i}", [128, W], BF16) for i in (0, 1)]
        self.qu = [self.sbuf("Q", f"qu{i}", [128, W], BF16) for i in (0, 1)]
        self.qh = [self.sbuf("Q", f"qh{i}", [128, W], BF16) for i in (0, 1)]
        self.qsums = [self.sbuf("Q", f"qsums{i}", [128, ng], F32) for i in (0, 1)]
        self.qscb = [self.sbuf("Q", f"qscb{i}", [128, ng], BF16) for i in (0, 1)]
        self.qrcp = [self.sbuf("Q", f"qrcp{i}", [128, ng], F32) for i in (0, 1)]
        self.qwsl = [Slot(self.P, f"qwl{i}") for i in (0, 1)]
        self.qssl = [Slot(self.P, f"qst{i}") for i in (0, 1)]
        sched = []
        for j in range(c["NSTRIP"] // 2):
            for half in (0, HFF):
                for rb in range(4):
                    r = half + j * 512 + rb * 128
                    sched.append(dict(src="w1", dst="w1h", key=("w1h", r // 128),
                                      r=r, cc=0, w=D))
        for r in range(0, D, 128):
            for cc in range(0, HFF, WB2):
                sched.append(dict(src="w2", dst="w2h",
                                  key=("w2h", r // 128, cc // WB2),
                                  r=r, cc=cc, w=WB2))
        self.qsched = sched
        self.qpos = 0
        self.qgps_left = max(8, min(64, len(sched) // 3))
        self.qquota = {"p1b": 8, "attn": 32, "wo": 16}
        self.qtok = {}
        self.qprev = dict(u=[[], []], small=[[], []], qh=[[], []])
        self.qidx = 0

    def qsched_at(self, stage):
        if stage == "all":
            return len(self.qsched) - self.qpos
        q = self.qquota.get(stage, 0)
        if q > 0:
            self.qquota[stage] = q - 1
            return 1
        return 0

    def emit_quant(self, n, eng=None):
        P = self.P
        for _ in range(n):
            if self.qpos >= len(self.qsched):
                return
            blk = self.qsched[self.qpos]
            self.qpos += 1
            i = self.qidx
            self.qidx += 1
            par = i % 2
            E = eng
            if E is None:
                if self.qgps_left > 0 and (i % 3 == 2):
                    E = "g"
                else:
                    E = "v"
            if E == "g":
                if self.qgps_left <= 0:
                    E = "v"
                else:
                    self.qgps_left -= 1
            w, ng = blk["w"], blk["w"] // 64
            r, cc = blk["r"], blk["cc"]
            src, dst = self.ap[blk["src"]], self.ap[blk["dst"]]
            qw, qu, qh = self.qw[par], self.qu[par], self.qh[par]
            qsums, qscb, qrcp = self.qsums[par], self.qscb[par], self.qrcp[par]
            ld = self.qwsl[par].load("g", lambda e, qw=qw, src=src, r=r, cc=cc,
                                     w=w: e.dma_start(
                out=qw[:, 0:w], in_=src[r:r + 128, cc:cc + w]))
            e_s = "vp" if E == "v" else "gp"
            red = P.emit("v", lambda e, qw=qw, qsums=qsums, w=w, ng=ng: e.reduce_sum(
                qsums[:, 0:ng].unsqueeze(2),
                qw[:, 0:w].rearrange("p (g k) -> p g k", k=64), axis=AX.X,
                apply_absolute_value=True),
                waits=[ld] + self.qprev["small"][par], inc="vp")
            sc1 = P.emit(E, lambda e, qscb=qscb, qsums=qsums, ng=ng: e.tensor_scalar(
                qscb[:, 0:ng], qsums[:, 0:ng], 1.0 / 64, None, ALU.mult),
                waits=[red], inc=e_s)
            sc2 = P.emit(E, lambda e, qscb=qscb, ng=ng: e.tensor_scalar_max(
                qscb[:, 0:ng], qscb[:, 0:ng], float(np.float32(BF(1e-8)))),
                waits=[sc1], inc=e_s)
            rc = P.emit("v", lambda e, qrcp=qrcp, qscb=qscb, ng=ng: e.reciprocal(
                qrcp[:, 0:ng], qscb[:, 0:ng]), waits=[sc2], inc="vp")
            u = P.emit(E, lambda e, qu=qu, qw=qw, qrcp=qrcp, w=w, ng=ng:
                       e.tensor_tensor(
                qu[:, 0:w].rearrange("p (g k) -> p g k", k=64),
                qw[:, 0:w].rearrange("p (g k) -> p g k", k=64),
                qrcp[:, 0:ng].unsqueeze(2).broadcast_to([128, ng, 64]),
                ALU.mult), waits=[rc] + self.qprev["u"][par], inc=e_s)
            self.qwsl[par].done(u)
            cl_ = P.emit(E, lambda e, qu=qu, w=w: e.tensor_scalar(
                qu[:, 0:w], qu[:, 0:w], 1.0, -1.0, ALU.min, ALU.max),
                waits=[u], inc=e_s)
            mg = P.emit(E, lambda e, qu=qu, w=w: e.tensor_scalar(
                qu[:, 0:w], qu[:, 0:w], MAGIC, -MAGIC, ALU.add, ALU.add),
                waits=[cl_], inc=e_s)
            wh = P.emit(E, lambda e, qh=qh, qu=qu, qscb=qscb, w=w, ng=ng:
                        e.tensor_tensor(
                qh[:, 0:w].rearrange("p (g k) -> p g k", k=64),
                qu[:, 0:w].rearrange("p (g k) -> p g k", k=64),
                qscb[:, 0:ng].unsqueeze(2).broadcast_to([128, ng, 64]),
                ALU.mult), waits=[mg] + self.qprev["qh"][par], inc=e_s)
            self.qprev["u"][par] = [wh]
            self.qprev["small"][par] = [u, wh]
            stok = self.qssl[par].load("y", lambda e, qh=qh, dst=dst, r=r, cc=cc,
                                       w=w: e.dma_start(
                out=dst[r:r + 128, cc:cc + w], in_=qh[:, 0:w]), extra=[wh])
            self.qssl[par].done(stok)
            self.qprev["qh"][par] = [stok]
            self.qtok[blk["key"]] = stok

    def ln_tile(self, x_ap, out_ap, jk, st, sc, t, NT, extra, prev):
        P, D = self.P, self.cfg["D"]
        a1 = P.emit("s", lambda e: e.activation(
            out=jk[0][:], in_=x_ap, func=AF.Copy,
            accum_out=st[:, t:t + 1]),
            waits=[prev["a1"]] + extra, inc="sp")
        a2 = P.emit("s", lambda e: e.activation(
            out=jk[1][:], in_=x_ap, func=AF.Square,
            accum_out=st[:, NT + t:NT + t + 1]),
            waits=[prev["a2"], a1], inc="sp")
        prev["a1"], prev["a2"] = a1, a2
        b = 8 * (t % 2)
        pv = prev["sc"][t % 2]
        s1 = P.emit("v", lambda e: e.tensor_scalar(
            sc[:, b + 2:b + 3], st[:, t:t + 1], 1.0 / D, None, ALU.mult),
            waits=[a1, pv], inc="vp")
        s2 = P.emit("v", lambda e: e.tensor_mul(
            sc[:, b + 3:b + 4], sc[:, b + 2:b + 3], sc[:, b + 2:b + 3]),
            waits=[s1], inc="vp")
        s3 = P.emit("v", lambda e: e.tensor_scalar(
            sc[:, b + 4:b + 5], st[:, NT + t:NT + t + 1], 1.0 / D, LN_EPS,
            ALU.mult, ALU.add), waits=[a2], inc="vp")
        s4 = P.emit("v", lambda e: e.tensor_sub(
            sc[:, b + 4:b + 5], sc[:, b + 4:b + 5], sc[:, b + 3:b + 4]),
            waits=[s2, s3], inc="vp")
        s5 = P.emit("s", lambda e: e.activation(
            out=sc[:, b + 5:b + 6], in_=sc[:, b + 4:b + 5], func=AF.Ln),
            waits=[s4], inc="sp")
        s6 = P.emit("s", lambda e: e.activation(
            out=sc[:, b + 6:b + 7], in_=sc[:, b + 5:b + 6], func=AF.Exp,
            scale=-0.5), waits=[s5], inc="sp")
        s7 = P.emit("v", lambda e: e.scalar_tensor_tensor(
            sc[:, b + 7:b + 8], sc[:, b + 2:b + 3], -1.0,
            sc[:, b + 6:b + 7], ALU.mult, ALU.mult),
            waits=[s6, s2], inc="vp")
        prev["sc"][t % 2] = s7
        ap_ = P.emit("s", lambda e: e.activation(
            out=out_ap, in_=x_ap, func=AF.Identity,
            scale=sc[:, b + 6:b + 7], bias=sc[:, b + 7:b + 8]),
            waits=[s6, s7, prev["apply"]], inc="sp")
        prev["apply"] = ap_
        return ap_

    def phase1a(self):
        c, P = self.cfg, self.P
        T, D, DC, G = c["T"], c["D"], c["DC"], c["G"]
        NT, NCH = T // 128, T // 512
        GW = G * 128

        xt = [self.sbuf("L1", f"xt{i}", [128, D], F32) for i in (0, 1)]
        ln1tm = [self.sbuf("L1", f"ln1tm{i}", [128, D], BF16) for i in (0, 1)]
        lnfm = [self.sbuf("L1", f"lnfm{i}", [128, DC * 512], BF16) for i in (0, 1)]
        csb = [self.sbuf("L1", f"csb{i}", [128, 512], F32) for i in (0, 1)]
        ssb = [self.sbuf("L1", f"ssb{i}", [128, 512], F32) for i in (0, 1)]
        jk = [self.sbuf("L1", f"jk{i}", [128, D], BF16) for i in (0, 1)]
        st = self.sbuf("L1", "st", [128, 2 * NT], F32)
        sc = self.sbuf("L1", "sc", [128, 16], F32)
        kraw = [self.sbuf("L1", f"kraw{i}", [128, 512], BF16) for i in (0, 1)]
        rt1 = self.sbuf("L1", "rt1", [128, 512], F32)
        rt2 = self.sbuf("L1", "rt2", [128, 512], F32)
        wkb = self.sbuf("L1", "wkb", [128, DC * GW], BF16)
        wvb = self.sbuf("L1", "wvb", [128, DC * GW], BF16)

        kfm, vtm = self.sb["kfm"], self.sb["vtm"]
        pf = [self.sb[f"pf{i}"] for i in range(6)]
        pbA = self.sb["pbA"]

        P.sem("wkv")
        P.dma("g", lambda e: e.dma_start(
            out=wkb[:].rearrange("p (dc n) -> p dc n", n=GW),
            in_=self.ap["wk"].rearrange("(dc p) n -> p dc n", p=128)), inc="wkv")
        P.dma("g", lambda e: e.dma_start(
            out=wvb[:].rearrange("p (dc n) -> p dc n", n=GW),
            in_=self.ap["wv"].rearrange("(dc p) n -> p dc n", p=128)), inc="wkv")
        wkv_tok = ("wkv", 32)

        xsl = [Slot(P, "xls0"), Slot(P, "xls1")]
        cssl = [Slot(P, "csl0"), Slot(P, "csl1")]
        t_ms = P.emit("v", lambda e: e.memset(st[:], 0.0), inc="vp")

        prev = dict(a1=None, a2=None, apply=None, sc=[None, None],
                    lnfm=[[], []], kraw=[[], []], rt=None, pf=[None] * 6, pbA=[])
        kv_done = []

        for ch in range(NCH):
            par = ch % 2
            tcs = cssl[par].load("y", lambda e, ch=ch, par=par: e.dma_start(
                out=csb[par][:], in_=self.ap["cos_fm"][:, ch * 512:(ch + 1) * 512]))
            tss = cssl[par].load("y", lambda e, ch=ch, par=par: e.dma_start(
                out=ssb[par][:], in_=self.ap["sin_fm"][:, ch * 512:(ch + 1) * 512]))
            lnfm_rdy = []
            for tl in range(4):
                t = ch * 4 + tl
                tp2 = t % 2
                tlx = xsl[tp2].load("g", lambda e, t=t, tp2=tp2: e.dma_start(
                    out=xt[tp2][:], in_=self.ap["x_all"][t * 128:(t + 1) * 128, :]))
                ap_ = self.ln_tile(xt[tp2][:], ln1tm[tp2][:], jk, st, sc, t, NT,
                                   [tlx, t_ms], prev)
                xsl[tp2].done(ap_)
                nb = (DC + 7) // 8
                for j in range(nb):
                    dcs = list(range(8 * j, min(8 * j + 8, DC)))
                    n8 = len(dcs)
                    tt = None
                    for ii, dc in enumerate(dcs):
                        tt = P.emit("t", lambda e, tp2=tp2, dc=dc, ii=ii: e.transpose(
                            pbA[:, ii * 128:(ii + 1) * 128],
                            ln1tm[tp2][:, dc * 128:(dc + 1) * 128],
                            self.sb["ident_sb"][:]),
                            waits=([ap_, self.cl] + prev["pbA"]) if ii == 0 else [],
                            inc="tp" if ii == n8 - 1 else None)
                    ev = P.emit("v", lambda e, par=par, tl=tl, j=j, n8=n8:
                                e.tensor_copy(
                        out=lnfm[par][:].rearrange("p (dc n) -> p dc n", n=512)
                            [:, 8 * j:8 * j + n8, tl * 128:(tl + 1) * 128],
                        in_=pbA[:].rearrange("p (k n) -> p k n", n=128)[:, 0:n8, :]),
                        waits=[tt] + prev["lnfm"][par], inc="vp")
                    lnfm_rdy.append(ev)
                    prev["pbA"] = [ev]
            prev["lnfm"][par] = []
            for g in range(G):
                bank = g % 2
                mmt = None
                for dc in range(DC):
                    mmt = P.emit("t", lambda e, g=g, dc=dc, par=par, bank=bank:
                                 e.matmul(
                        pf[bank][:, 0:512],
                        wkb[:, dc * GW + g * 128: dc * GW + g * 128 + 128],
                        lnfm[par][:, dc * 512:(dc + 1) * 512],
                        start=(dc == 0), stop=(dc == DC - 1)),
                        waits=([wkv_tok, prev["pf"][bank]] + lnfm_rdy)
                              if dc == 0 else [],
                        inc="tp" if dc == DC - 1 else None)
                kb = g % 2
                e1 = P.emit("s", lambda e, bank=bank, kb=kb: e.activation(
                    out=kraw[kb][:], in_=pf[bank][:, 0:512], func=AF.Copy),
                    waits=[mmt] + prev["kraw"][kb], inc="sp")
                prev["pf"][bank] = e1
                rmt = P.emit("t", lambda e, kb=kb: e.matmul(
                    pf[2][:, 0:512], self.sb["rot_pt_sb"][:], kraw[kb][:],
                    start=True, stop=True),
                    waits=[e1, prev["pf"][2], self.cl], inc="tp")
                d1 = P.emit("v", lambda e, par=par, kb=kb: e.tensor_mul(
                    rt1[:], kraw[kb][:], csb[par][:]),
                    waits=[e1, tss, prev["rt"]], inc="vp")
                d2 = P.emit("v", lambda e, par=par: e.tensor_mul(
                    rt2[:], pf[2][:, 0:512], ssb[par][:]),
                    waits=[rmt, tss, prev["rt"]], inc="vp")
                prev["pf"][2] = d2
                d3 = P.emit("v", lambda e, g=g, ch=ch: e.tensor_add(
                    kfm[:, g * T + ch * 512: g * T + ch * 512 + 512],
                    rt1[:], rt2[:]), waits=[d1, d2], inc="vp")
                prev["rt"] = d3
                prev["kraw"][kb] = [rmt, d1]
                kv_done.append(d3)
            cssl[par].done(kv_done[-1])
            last_v = None
            for tl in range(4):
                t = ch * 4 + tl
                mmt = None
                for dc in range(DC):
                    mmt = P.emit("t", lambda e, tl=tl, dc=dc, par=par: e.matmul(
                        pf[4][:, 0:GW],
                        lnfm[par][:, dc * 512 + tl * 128: dc * 512 + tl * 128 + 128],
                        wvb[:, dc * GW:(dc + 1) * GW],
                        start=(dc == 0), stop=(dc == DC - 1)),
                        waits=[wkv_tok, prev["pf"][4]] if dc == 0 else [],
                        inc="tp" if dc == DC - 1 else None)
                ev = P.emit("s", lambda e, t=t: e.activation(
                    out=vtm[:, t * GW:(t + 1) * GW], in_=pf[4][:, 0:GW],
                    func=AF.Copy), waits=[mmt], inc="sp")
                prev["pf"][4] = ev
                kv_done.append(ev)
                last_v = ev
            prev["lnfm"][par] = [last_v]
        self.kv_done = kv_done
        self.close_arena("L1")

    def phase1b(self):
        c, P = self.cfg, self.P
        D, DC, OWN, H, T = c["D"], c["DC"], c["OWN"], c["H"], c["T"]
        NSLOT = c["NSLOT"]
        NTC = (OWN + 511) // 512
        self.sbuf("Lqf", "qfm", [128, H * OWN], BF16)
        self.sbuf("Lqf", "dqsb", [128, OWN], F32)
        cosq = self.sbuf("Lro", "cosq", [128, OWN], F32)
        sinq = self.sbuf("Lro", "sinq", [128, OWN], F32)
        q_in = self.sbuf("L1q", "q_in", [128, DC * OWN], BF16)
        xo = self.sbuf("L1q", "xoq", [128, D], F32)
        lntm = [self.sbuf("L1q", f"lntm{i}", [128, D], BF16) for i in (0, 1)]
        jkq = self.sbuf("L1q", "jkq0", [128, D], BF16)
        jk = [jkq, jkq]
        st = self.sbuf("L1q", "stq", [128, 2 * NSLOT], F32)
        sc = self.sbuf("L1q", "scq", [128, 16], F32)
        wqh = [self.sbuf("L1q", f"wqh{i}", [128, DC * 128], BF16) for i in (0, 1)]
        qraw = self.sbuf("L1q", "qraw", [128, OWN], BF16)
        rq1 = self.sbuf("L1q", "rq1", [128, 512], F32)
        rq2 = self.sbuf("L1q", "rq2", [128, 512], F32)
        qfm, dqsb = self.sb["qfm"], self.sb["dqsb"]
        pf = [self.sb[f"pf{i}"] for i in range(6)]
        pbA = self.sb["pbA"]

        P.sem("roL")
        pgate = [self.kv_done[-1]]
        for src, dst in [("cos_own", cosq), ("sin_own", sinq), ("dq", dqsb)]:
            P.dma("y", lambda e, s=src, d=dst: e.dma_start(out=d[:], in_=self.ap[s]),
                  waits=pgate, inc="roL")
        ro_tok = ("roL", P.cnt["roL"])

        xsl = Slot(P, "xoq0")
        wsl = [Slot(P, "wql0"), Slot(P, "wql1")]
        t_ms = P.emit("v", lambda e: e.memset(st[:], 0.0), waits=pgate, inc="vp")
        prev = dict(a1=None, a2=None, apply=None, sc=[None, None], pbA=[],
                    pf=[None] * 6, qraw=[], rq=None)
        qin_rdy = []
        for k in range(NSLOT):
            par = k % 2
            tlx = xsl.load("g", lambda e, k=k: e.dma_start(
                out=xo[:], in_=self.ap["x_own"][k * 128:(k + 1) * 128, :]),
                extra=pgate if k == 0 else [])
            ap_ = self.ln_tile(xo[:], lntm[par][:], jk, st, sc, k, NSLOT,
                               [tlx, t_ms], prev)
            xsl.done(ap_)
            nb = (DC + 7) // 8
            for j in range(nb):
                dcs = list(range(8 * j, min(8 * j + 8, DC)))
                n8 = len(dcs)
                tt = None
                for ii, dc in enumerate(dcs):
                    tt = P.emit("t", lambda e, par=par, dc=dc, ii=ii: e.transpose(
                        pbA[:, ii * 128:(ii + 1) * 128],
                        lntm[par][:, dc * 128:(dc + 1) * 128],
                        self.sb["ident_sb"][:]),
                        waits=([ap_, self.cl] + prev["pbA"]) if ii == 0 else [],
                        inc="tp" if ii == n8 - 1 else None)
                ev = P.emit("v", lambda e, k=k, j=j, n8=n8: e.tensor_copy(
                    out=q_in[:].rearrange("p (dc n) -> p dc n", n=OWN)
                        [:, 8 * j:8 * j + n8, k * 128:(k + 1) * 128],
                    in_=pbA[:].rearrange("p (k n) -> p k n", n=128)[:, 0:n8, :]),
                    waits=[tt], inc="vp")
                prev["pbA"] = [ev]
                qin_rdy.append(ev)
        self.q_done = []
        for h in range(H):
            par = h % 2
            tw_ = wsl[par].load("g", lambda e, h=h, par=par: e.dma_start(
                out=wqh[par][:].rearrange("p (dc n) -> p dc n", n=128),
                in_=self.ap["wq"][:, h * 128:(h + 1) * 128]
                    .rearrange("(dc p) n -> p dc n", p=128)),
                extra=pgate if h < 2 else [])
            for tc in range(NTC):
                off = tc * 512
                tww = min(512, OWN - off)
                bank = tc % 2
                mmt = None
                for dc in range(DC):
                    mmt = P.emit("t", lambda e, dc=dc, par=par, bank=bank,
                                 off=off, tww=tww: e.matmul(
                        pf[bank][:, 0:tww],
                        wqh[par][:, dc * 128:(dc + 1) * 128],
                        q_in[:, dc * OWN + off: dc * OWN + off + tww],
                        start=(dc == 0), stop=(dc == DC - 1)),
                        waits=([tw_, prev["pf"][bank]] + qin_rdy)
                              if dc == 0 else [],
                        inc="tp" if dc == DC - 1 else None)
                e1 = P.emit("s", lambda e, bank=bank, off=off, tww=tww: e.activation(
                    out=qraw[:, off:off + tww], in_=pf[bank][:, 0:tww],
                    func=AF.Copy), waits=[mmt] + prev["qraw"], inc="sp")
                prev["pf"][bank] = e1
                rmt = P.emit("t", lambda e, off=off, tww=tww: e.matmul(
                    pf[2][:, 0:tww], self.sb["rot_pt_sb"][:],
                    qraw[:, off:off + tww], start=True, stop=True),
                    waits=[e1, prev["pf"][2]], inc="tp")
                d1 = P.emit("v", lambda e, off=off, tww=tww: e.tensor_mul(
                    rq1[:, 0:tww], qraw[:, off:off + tww],
                    cosq[:, off:off + tww]),
                    waits=[e1, ro_tok, prev["rq"]], inc="vp")
                d2 = P.emit("v", lambda e, off=off, tww=tww: e.tensor_mul(
                    rq2[:, 0:tww], pf[2][:, 0:tww],
                    sinq[:, off:off + tww]), waits=[rmt, prev["rq"]], inc="vp")
                prev["pf"][2] = d2
                d3 = P.emit("v", lambda e, h=h, off=off, tww=tww: e.tensor_add(
                    qfm[:, h * OWN + off: h * OWN + off + tww],
                    rq1[:, 0:tww], rq2[:, 0:tww]),
                    waits=[d1, d2], inc="vp")
                prev["rq"] = d3
                prev["qraw"] = [rmt, d1]
                self.q_done.append(d3)
            wsl[par].done(self.q_done[-1])
            self.emit_quant(self.qsched_at("p1b"), eng="g")
        self.close_arena("L1q")
        self.close_arena("Lro")

    def attention(self):
        c, P = self.cfg, self.P
        T, OWN, H, G = c["T"], c["OWN"], c["H"], c["G"]
        REP, GW = c["REP"], c["G"] * 128
        kfm, vtm, qfm = self.sb["kfm"], self.sb["vtm"], self.sb["qfm"]
        dqsb = self.sb["dqsb"]
        ctxb = self.sbuf("Lctx", "ctxb", [128, H * OWN], BF16)
        pf = [self.sb[f"pf{i}"] for i in range(6)]
        CS = 1.0 / (SOFTCAP * float(np.sqrt(128.0)))
        gate = [self.q_done[-1], self.kv_done[-1]]

        MW = max(c["NTA"] * c["QWA"], c["NTB"] * c["QWB"])
        mbuf = self.sbuf("L2", "mbuf", [128, MW], BF16)
        u0 = [self.sbuf("L2", f"u0{i}", [128, 512], F32) for i in (0, 1)]
        pt = [self.sbuf("L2", f"pt{i}", [128, 512], BF16) for i in range(4)]
        dsb = [self.sbuf("L2", f"dsb{i}", [1, 512], F32) for i in (0, 1)]
        abias = self.sbuf("L2", "abias_sb", [128, c["H"] * c["NTB"]], F32)
        orf = self.sb["orf_sb"]
        P.sem("abl")
        P.dma("y", lambda e: e.dma_start(out=abias[:], in_=self.ap["abias"]),
              waits=gate, inc="abl")
        ab_tok = ("abl", 16)

        self.tap_kv = None
        if self.taps:
            P.sem("tap")
            P.dma("y", lambda e: e.dma_start(out=self.ap["t_k"], in_=kfm[:]),
                  waits=[self.kv_done[-1]], inc="tap")
            P.dma("y", lambda e: e.dma_start(out=self.ap["t_v"], in_=vtm[:]),
                  waits=[self.kv_done[-1]], inc="tap")
        msl = Slot(P, "mal")
        slopes = [float(np.float32(2.0) ** np.float32(-(h + 1) * 8.0 / H))
                  for h in range(H)]
        prev = dict(sc=[None, None], u0=[[], []], ctx=[None, None],
                    den=[None, None], dsb=[[], []])
        pt_gate = [[] for _ in range(4)]
        self.ctx_done = []
        for qc in (0, 1):
            TW = c["QWA"] if qc == 0 else c["QWB"]
            NTq = c["NTA"] if qc == 0 else c["NTB"]
            qoff = 0 if qc == 0 else c["QWA"]
            src_m = "ma" if qc == 0 else "mb"
            m_tok = msl.load("y", lambda e, src_m=src_m, TW=TW, NTq=NTq: e.dma_start(
                out=mbuf[:, 0:NTq * TW].rearrange("p (t f) -> p t f", f=TW),
                in_=self.ap[src_m].rearrange("(t p) f -> p t f", p=128)),
                extra=gate)
            mask_consumers = []
            for hb in range(H // 2):
                h0, h1 = 2 * hb, 2 * hb + 1
                g = h0 // REP
                assert h1 // REP == g
                last_pv = [None, None]
                last_den = [None, None]
                for s in range(NTq):
                    for hi, h in enumerate((h0, h1)):
                        sb_, cb_, db_ = pf[hi], pf[2 + hi], pf[4 + hi]
                        pti = 2 * hi + (s % 2)
                        qk = P.emit("t", lambda e, sb_=sb_, g=g, s=s, h=h,
                                    qoff=qoff, TW=TW: e.matmul(
                            sb_[:, 0:TW],
                            kfm[:, g * T + s * 128: g * T + s * 128 + 128],
                            qfm[:, h * OWN + qoff: h * OWN + qoff + TW],
                            start=True, stop=True),
                            waits=gate + [prev["sc"][hi]], inc="tp")
                        stt = P.emit("v", lambda e, sb_=sb_, h=h, qoff=qoff,
                                     TW=TW, hi=hi: e.scalar_tensor_tensor(
                            u0[hi][:, 0:TW], dqsb[:, qoff:qoff + TW], slopes[h],
                            sb_[:, 0:TW], ALU.mult, ALU.add),
                            waits=[qk] + prev["u0"][hi], inc="vp")
                        prev["sc"][hi] = stt
                        bidx = h * c["NTB"] + s
                        th = P.emit("s", lambda e, hi=hi, TW=TW, bidx=bidx:
                                    e.activation(
                            out=u0[hi][:, 0:TW], in_=u0[hi][:, 0:TW], func=AF.Tanh,
                            scale=CS, bias=abias[:, bidx:bidx + 1]),
                            waits=[stt, ab_tok], inc="sp")
                        ex = P.emit("s", lambda e, hi=hi, pti=pti, TW=TW:
                                    e.activation(
                            out=pt[pti][:, 0:TW], in_=u0[hi][:, 0:TW], func=AF.Exp,
                            scale=SOFTCAP),
                            waits=[th] + pt_gate[pti], inc="sp")
                        prev["u0"][hi] = [ex]
                        mk = P.emit("v", lambda e, pti=pti, s=s, TW=TW:
                                    e.tensor_mul(
                            pt[pti][:, 0:TW], pt[pti][:, 0:TW],
                            mbuf[:, s * TW:(s + 1) * TW]),
                            waits=[ex, m_tok], inc="vp")
                        mask_consumers.append(mk)
                        pv = P.emit("t", lambda e, cb_=cb_, g=g, s=s, pti=pti,
                                    TW=TW, NTq=NTq: e.matmul(
                            cb_[:, 0:TW],
                            vtm[:, s * GW + g * 128: s * GW + g * 128 + 128],
                            pt[pti][:, 0:TW], start=(s == 0), stop=(s == NTq - 1)),
                            waits=[mk] + ([prev["ctx"][hi]] if s == 0 else []),
                            inc="tp")
                        dn = P.emit("t", lambda e, db_=db_, pti=pti, TW=TW, s=s,
                                    NTq=NTq: e.matmul(
                            db_[0:1, 0:TW], self.sb["ones_col_sb"][:],
                            pt[pti][:, 0:TW], start=(s == 0), stop=(s == NTq - 1)),
                            waits=([prev["den"][hi]] if s == 0 else []),
                            inc="tp")
                        pt_gate[pti] = [pv, dn]
                        last_pv[hi], last_den[hi] = pv, dn
                for hi, h in enumerate((h0, h1)):
                    sb_, cb_, db_ = pf[hi], pf[2 + hi], pf[4 + hi]
                    dc_ = P.emit("v", lambda e, hi=hi, db_=db_, TW=TW: e.tensor_copy(
                        out=dsb[hi][:, 0:TW], in_=db_[0:1, 0:TW]),
                        waits=[last_den[hi]] + prev["dsb"][hi], inc="vp")
                    prev["den"][hi] = dc_
                    rcd = P.emit("v", lambda e, hi=hi, TW=TW: e.reciprocal(
                        dsb[hi][:, 0:TW], dsb[hi][:, 0:TW]),
                        waits=[dc_], inc="vp")
                    bc = P.emit("t", lambda e, sb_=sb_, hi=hi, TW=TW: e.matmul(
                        sb_[:, 0:TW], orf[:], dsb[hi][:, 0:TW],
                        start=True, stop=True),
                        waits=[rcd, prev["sc"][hi], self.cl], inc="tp")
                    ev = P.emit("s", lambda e, cb_=cb_, h=h, qoff=qoff, TW=TW:
                                e.activation(
                        out=ctxb[:, h * OWN + qoff: h * OWN + qoff + TW],
                        in_=cb_[:, 0:TW], func=AF.Copy),
                        waits=[last_pv[hi]], inc="sp")
                    prev["ctx"][hi] = ev
                    dv = P.emit("v", lambda e, sb_=sb_, h=h, qoff=qoff, TW=TW:
                                e.tensor_tensor(
                        ctxb[:, h * OWN + qoff: h * OWN + qoff + TW],
                        ctxb[:, h * OWN + qoff: h * OWN + qoff + TW],
                        sb_[:, 0:TW], ALU.mult),
                        waits=[bc, ev], inc="vp")
                    prev["dsb"][hi] = [bc]
                    prev["sc"][hi] = dv
                    self.ctx_done.append(dv)
                self.emit_quant(self.qsched_at("attn") * 2, eng="g")
            msl.done(mask_consumers[-1])
        if self.taps:
            P.dma("y", lambda e: e.dma_start(out=self.ap["t_q"],
                                             in_=self.sb["qfm"][:]),
                  waits=[self.q_done[-1]], inc="tap")
            self.tap_kv = ("tap", P.cnt["tap"])
        self.close_arena("L2")
        self.close_arena("Lqf")
        self.close_arena("Lkv")

    def wo_ln2(self):
        c, P = self.cfg, self.P
        D, DC, OWN, H = c["D"], c["DC"], c["OWN"], c["H"]
        NSLOT, NTC = c["NSLOT"], (OWN + 511) // 512
        pf = [self.sb[f"pf{i}"] for i in range(6)]
        ctxb = self.sb["ctxb"]
        gate = [self.ctx_done[-1]]
        if self.taps and self.tap_kv:
            gate.append(self.tap_kv)

        ln2x = self.sbuf("L3b", "ln2x", [128, DC * OWN], BF16)
        x1 = self.sbuf("L3", "x1", [128, DC * OWN], F32)
        xo = self.sbuf("L3", "xw", [128, D], F32)
        wob = [self.sbuf("L3", f"wob{i}", [128, H * 128], BF16) for i in (0, 1)]
        x1b = self.sbuf("L3", "x1b", [128, 512], BF16)
        x1q = self.sbuf("L3", "x1q", [128, 512], BF16)
        smm = self.sbuf("L3", "smm", [1, 512], F32)
        smv = self.sbuf("L3", "smv", [1, 512], F32)
        smr = self.sbuf("L3", "smr", [1, 512], F32)
        tmp = self.sbuf("L3", "tmpw", [128, 512], F32)

        xsl = Slot(P, "xw0")
        wsl = [Slot(P, "wol0"), Slot(P, "wol1")]
        orf = self.sb["orf_sb"]

        prev_pf5 = None
        prev_ev = None
        for k in range(NSLOT):
            tlx = xsl.load("g", lambda e, k=k: e.dma_start(
                out=xo[:], in_=self.ap["x_own"][k * 128:(k + 1) * 128, :]),
                extra=gate if k == 0 else [])
            for j in range((DC + 3) // 4):
                dcs = list(range(4 * j, min(4 * j + 4, DC)))
                n4 = len(dcs)
                tt = None
                for ii, dc in enumerate(dcs):
                    tt = P.emit("t", lambda e, dc=dc, ii=ii: e.transpose(
                        pf[5][:, ii * 128:(ii + 1) * 128],
                        xo[:, dc * 128:(dc + 1) * 128],
                        self.sb["ident_f_sb"][:]),
                        waits=([tlx, prev_pf5, self.cl] if ii == 0 else []),
                        inc="tp" if ii == n4 - 1 else None)
                ev = P.emit("s", lambda e, k=k, j=j, n4=n4: e.copy(
                    out=x1[:].rearrange("p (dc n) -> p dc n", n=OWN)
                        [:, 4 * j:4 * j + n4, k * 128:(k + 1) * 128],
                    in_=pf[5][:].rearrange("p (q n) -> p q n", n=128)[:, 0:n4, :]),
                    waits=[tt, prev_ev], inc="sp")
                prev_pf5 = ev
                prev_ev = ev
            xsl.done(prev_ev)
        xfm_tok = prev_ev
        self.x1st = Slot(P, "x1st")
        ln2_done = []
        prev_bank = [None, None]
        prev_statmm = [None, None]
        prev_apply = None
        prev_x1bq = None
        prev_small = None
        for tc in range(NTC):
            off = tc * 512
            tw = min(512, OWN - off)
            bank = tc % 2
            stat_mm = [None, None]
            for dc in range(DC):
                par = dc % 2
                twl = wsl[par].load("g", lambda e, dc=dc, par=par: e.dma_start(
                    out=wob[par][:].rearrange("p (hc n) -> p hc n", n=128),
                    in_=self.ap["wo"][:, dc * 128:(dc + 1) * 128]
                        .rearrange("(hc p) n -> p hc n", p=128)),
                    extra=gate if (tc == 0 and dc < 2) else [])
                mmt = None
                for hc in range(H):
                    mmt = P.emit("t", lambda e, bank=bank, par=par, hc=hc, off=off,
                                 tw=tw: e.matmul(
                        pf[bank][:, 0:tw],
                        wob[par][:, hc * 128:(hc + 1) * 128],
                        ctxb[:, hc * OWN + off: hc * OWN + off + tw],
                        start=(hc == 0), stop=(hc == H - 1)),
                        waits=([twl, prev_bank[bank], xfm_tok] + gate)
                              if hc == 0 else [],
                        inc="tp" if hc == H - 1 else None)
                wsl[par].done(mmt)
                stt = P.emit("v", lambda e, bank=bank, dc=dc, off=off, tw=tw:
                             e.scalar_tensor_tensor(
                    x1[:, dc * OWN + off: dc * OWN + off + tw],
                    pf[bank][:, 0:tw], self.sb["bo_fm_sb"][:, dc:dc + 1],
                    x1[:, dc * OWN + off: dc * OWN + off + tw],
                    ALU.add, ALU.add),
                    waits=[mmt, self.cl], inc="vp")
                prev_bank[bank] = stt
                c1 = P.emit("s", lambda e, dc=dc, off=off, tw=tw: e.activation(
                    out=x1b[:, 0:tw],
                    in_=x1[:, dc * OWN + off: dc * OWN + off + tw],
                    func=AF.Copy), waits=[stt, prev_x1bq], inc="sp")
                c2 = P.emit("s", lambda e, dc=dc, off=off, tw=tw: e.activation(
                    out=x1q[:, 0:tw],
                    in_=x1[:, dc * OWN + off: dc * OWN + off + tw],
                    func=AF.Square), waits=[stt], inc="sp")
                m1 = P.emit("t", lambda e, dc=dc, tw=tw: e.matmul(
                    pf[2][0:1, 0:tw], self.sb["ones_col_sb"][:], x1b[:, 0:tw],
                    start=(dc == 0), stop=(dc == DC - 1)),
                    waits=[c1] + ([prev_statmm[0]] if dc == 0 else []), inc="tp")
                m2 = P.emit("t", lambda e, dc=dc, tw=tw: e.matmul(
                    pf[3][0:1, 0:tw], self.sb["ones_col_sb"][:], x1q[:, 0:tw],
                    start=(dc == 0), stop=(dc == DC - 1)),
                    waits=[c2] + ([prev_statmm[1]] if dc == 0 else []), inc="tp")
                stat_mm = [m1, m2]
                prev_x1bq = m2
            s1 = P.emit("v", lambda e, tw=tw: e.tensor_scalar(
                smm[:, 0:tw], pf[2][0:1, 0:tw], 1.0 / D, None, ALU.mult),
                waits=[stat_mm[0], prev_small], inc="vp")
            s2 = P.emit("v", lambda e, tw=tw: e.tensor_scalar(
                smv[:, 0:tw], pf[3][0:1, 0:tw], 1.0 / D, LN_EPS,
                ALU.mult, ALU.add), waits=[stat_mm[1]], inc="vp")
            s3 = P.emit("v", lambda e, tw=tw: e.tensor_mul(
                smr[:, 0:tw], smm[:, 0:tw], smm[:, 0:tw]), waits=[s1], inc="vp")
            s4 = P.emit("v", lambda e, tw=tw: e.tensor_sub(
                smv[:, 0:tw], smv[:, 0:tw], smr[:, 0:tw]), waits=[s2, s3], inc="vp")
            s5 = P.emit("s", lambda e, tw=tw: e.activation(
                out=smr[:, 0:tw], in_=smv[:, 0:tw], func=AF.Ln),
                waits=[s4], inc="sp")
            s6 = P.emit("s", lambda e, tw=tw: e.activation(
                out=smr[:, 0:tw], in_=smr[:, 0:tw], func=AF.Exp, scale=-0.5),
                waits=[s5], inc="sp")
            b1 = P.emit("t", lambda e, tw=tw: e.matmul(
                pf[2][:, 0:tw], orf[:], smm[:, 0:tw], start=True, stop=True),
                waits=[s6, s1], inc="tp")
            b2 = P.emit("t", lambda e, tw=tw: e.matmul(
                pf[3][:, 0:tw], orf[:], smr[:, 0:tw], start=True, stop=True),
                waits=[s6], inc="tp")
            prev_small = s6
            ap_t = prev_apply
            for dc in range(DC):
                a1 = P.emit("v", lambda e, dc=dc, off=off, tw=tw: e.tensor_tensor(
                    tmp[:, 0:tw], x1[:, dc * OWN + off: dc * OWN + off + tw],
                    pf[2][:, 0:tw], ALU.subtract),
                    waits=[b1, prev_bank[0], prev_bank[1], ap_t], inc="vp")
                ap_t = P.emit("v", lambda e, dc=dc, off=off, tw=tw: e.tensor_tensor(
                    ln2x[:, dc * OWN + off: dc * OWN + off + tw],
                    tmp[:, 0:tw], pf[3][:, 0:tw], ALU.mult),
                    waits=[b2, a1], inc="vp")
                ln2_done.append(ap_t)
            prev_apply = ap_t
            prev_statmm = [ap_t, ap_t]
            self.emit_quant(self.qsched_at("wo") * 8)
        x1_stores = []
        for dc in range(DC):
            tk = self.x1st.load("y", lambda e, dc=dc: e.dma_start(
                out=self.ap["x1s"][dc * 128:(dc + 1) * 128, :],
                in_=x1[:, dc * OWN:(dc + 1) * OWN]), extra=[ln2_done[-1]])
            x1_stores.append(tk)
        self.ln2_done = ln2_done
        if self.taps:
            P.sem("tap")
            P.dma("y", lambda e: e.dma_start(out=self.ap["t_ctx"], in_=ctxb[:]),
                  waits=[self.ctx_done[-1]], inc="tap")
            for dc in range(DC):
                P.dma("y", lambda e, dc=dc: e.dma_start(
                    out=self.ap["t_x1"][dc * 128:(dc + 1) * 128, :],
                    in_=x1[:, dc * OWN:(dc + 1) * OWN]),
                    waits=[ln2_done[-1]], inc="tap")
            P.dma("y", lambda e: e.dma_start(out=self.ap["t_ln2"], in_=ln2x[:]),
                  waits=[ln2_done[-1]], inc="tap")
        self.close_arena("Lctx")
        self.tap_x1_gate = ("tap", P.cnt.get("tap", 0)) if self.taps else None
        self.close_arena("L3")

    def ffn(self):
        c, P = self.cfg, self.P
        D, DC, OWN, HFF, NH2 = c["D"], c["DC"], c["OWN"], c["HFF"], c["NH2"]
        NTH = (OWN + 511) // 512
        NST2 = c["NSTRIP"] // 2
        pf = [self.sb[f"pf{i}"] for i in range(6)]
        ln2x = self.sb["ln2x"]
        orf = self.sb["orf_sb"]
        DBW = min(512, D)
        NDB = D // DBW
        NRING = min(c.get("NRING_CAP", 16), NH2)
        WB2 = self.qWB2

        hn = self.sbuf("L4", "hn", [128, NH2 * 512], BF16)
        w1t = [self.sbuf("L4", f"w1t{i}", [128, DC * 512], BF16,
                         is_transpose_dest=True) for i in (0, 1)]
        sg = self.sbuf("L4", "sg", [128, 4 * 512], BF16)
        sq = [self.sbuf("L4", f"sq{i}", [128, 512], BF16) for i in (0, 1)]
        srr = self.sbuf("L4", "srr", [1, 512], F32)
        rbc = self.sbuf("L4", "rbc", [128, 512], F32)
        ring = self.sbuf("L4", "ring", [128, NRING * DBW], BF16,
                         is_transpose_dest=True)
        x1l = [self.sbuf("L4", f"x1l{i}", [128, 512], F32) for i in range(4)]
        obuf = [self.sbuf("L4", f"obuf{i}", [128, 512], F32) for i in (0, 1)]

        wsl = [Slot(P, "f1l0"), Slot(P, "f1l1")]
        rsl = [Slot(P, f"rg{i}") for i in range(NRING)]
        x1sl = [Slot(P, f"x1l{i}") for i in range(4)]
        P.sem("outst0")
        P.sem("outst1")
        x1full = ("x1st", P.cnt["x1st"])
        gate0 = [x1full, self.ln2_done[-1]]
        if self.taps:
            gate0.append(self.tap_x1_gate)

        prev_bank = [None] * 4
        prev_sq = [None, None]
        prev_acc = [None] * 4
        prev_ob = [None, None]
        prev_rbc = None
        hn_last = None
        for th in range(NTH):
            hoff = th * 512
            hw = min(512, OWN - hoff)
            prev_sg = [None] * 4
            for j2 in range(NST2):
                self.emit_quant(8)
                for half in (0, 1):
                    sidx = j2 * 2 + half
                    par = sidx % 2
                    r0 = half * HFF + j2 * 512
                    qts = [self.qtok[("w1h", (r0 + rb * 128) // 128)]
                           for rb in range(4)]
                    ltok = None
                    for dc in range(DC):
                        ltok = wsl[par].load("y", lambda e, r0=r0, dc=dc, par=par:
                                             e.dma_start(
                            out=w1t[par][:, dc * 512:(dc + 1) * 512],
                            in_=self.ap["w1h"][r0:r0 + 512,
                                               dc * 128:(dc + 1) * 128],
                            transpose=True),
                            extra=(qts + gate0) if dc == 0 else [])
                    for hc in range(4):
                        hcg = j2 * 4 + hc
                        bank = hc % 4
                        mmt = None
                        for dc in range(DC):
                            mmt = P.emit("t", lambda e, par=par, hc=hc, dc=dc,
                                         hoff=hoff, hw=hw, bank=bank: e.matmul(
                                pf[bank][:, 0:hw],
                                w1t[par][:, dc * 512 + hc * 128:
                                         dc * 512 + hc * 128 + 128],
                                ln2x[:, dc * OWN + hoff: dc * OWN + hoff + hw],
                                start=(dc == 0), stop=(dc == DC - 1)),
                                waits=([ltok, prev_bank[bank],
                                        self.ln2_done[-1]] if dc == 0 else []),
                                inc="tp" if dc == DC - 1 else None)
                        if half == 0:
                            evt = P.emit("s", lambda e, hc=hc, hw=hw, bank=bank:
                                         e.activation(
                                out=sg[:, hc * 512: hc * 512 + hw],
                                in_=pf[bank][:, 0:hw], func=AF.Silu),
                                waits=[mmt, prev_sg[hc]], inc="sp")
                        else:
                            evt = P.emit("v", lambda e, hcg=hcg, hc=hc, hw=hw,
                                         bank=bank: e.tensor_mul(
                                hn[:, hcg * 512: hcg * 512 + hw],
                                sg[:, hc * 512: hc * 512 + hw],
                                pf[bank][:, 0:hw]),
                                waits=[mmt, prev_sg[hc]], inc="vp")
                            hn_last = evt
                        prev_bank[bank] = evt
                        prev_sg[hc] = evt
                    wsl[par].done(prev_bank[3])
            self.emit_quant(self.qsched_at("all"))
            if self.stop_after == "f1" and th == 0:
                P.sem("tap")
                P.dma("y", lambda e, hw=hw: e.dma_start(
                    out=self.ap["t_hn"].rearrange("p (c n) -> p c n", n=OWN)
                        [:, :, 0:hw],
                    in_=hn[:].rearrange("p (c n) -> p c n", n=512)[:, :, 0:hw]),
                    waits=[hn_last], inc="tap")
                fw = [("tap", P.cnt["tap"])]
                for s_ in ("qst0", "qst1"):
                    if s_ in P.cnt:
                        fw.append((s_, P.cnt[s_]))
                P.emit("y", lambda e: e.nop(), waits=fw)
                return
            smm = None
            for hc in range(NH2):
                par = hc % 2
                a1 = P.emit("s", lambda e, hc=hc, par=par, hw=hw: e.activation(
                    out=sq[par][:, 0:hw], in_=hn[:, hc * 512: hc * 512 + hw],
                    func=AF.Square),
                    waits=[hn_last, prev_sq[par]], inc="sp")
                smm = P.emit("t", lambda e, hc=hc, par=par, hw=hw: e.matmul(
                    pf[4][0:1, 0:hw], self.sb["ones_col_sb"][:],
                    sq[par][:, 0:hw],
                    start=(hc == 0), stop=(hc == NH2 - 1)),
                    waits=[a1] + ([prev_rbc] if hc == 0 else []), inc="tp")
                prev_sq[par] = smm
            s1 = P.emit("v", lambda e, hw=hw: e.tensor_scalar(
                srr[:, 0:hw], pf[4][0:1, 0:hw], 1.0 / HFF, RMS_EPS,
                ALU.mult, ALU.add), waits=[smm], inc="vp")
            s2 = P.emit("s", lambda e, hw=hw: e.activation(
                out=srr[:, 0:hw], in_=srr[:, 0:hw], func=AF.Ln),
                waits=[s1], inc="sp")
            s3 = P.emit("s", lambda e, hw=hw: e.activation(
                out=srr[:, 0:hw], in_=srr[:, 0:hw], func=AF.Exp, scale=-0.5),
                waits=[s2], inc="sp")
            b1 = P.emit("t", lambda e, hw=hw: e.matmul(
                pf[4][:, 0:hw], orf[:], srr[:, 0:hw], start=True, stop=True),
                waits=[s3, s1], inc="tp")
            cp = P.emit("v", lambda e, hw=hw: e.tensor_copy(
                out=rbc[:, 0:hw], in_=pf[4][:, 0:hw]), waits=[b1], inc="vp")
            prev_rbc = cp
            for db in range(NDB):
                r0 = db * DBW
                nds = DBW // 128
                txs = []
                for ds in range(nds):
                    dci = db * nds + ds
                    txs.append(x1sl[ds].load(
                        "y", lambda e, dci=dci, ds=ds, hoff=hoff, hw=hw:
                        e.dma_start(
                            out=x1l[ds][:, 0:hw],
                            in_=self.ap["x1s"][dci * 128:(dci + 1) * 128,
                                               hoff:hoff + hw]),
                        extra=[x1full]))
                last_mm = [None] * nds
                for hc in range(NH2):
                    sl = hc % NRING
                    qks = [self.qtok[("w2h", rb, (hc * 128) // WB2)]
                           for rb in range(r0 // 128, (r0 + DBW) // 128)]
                    lt = rsl[sl].load("y", lambda e, r0=r0, hc=hc, sl=sl:
                                      e.dma_start(
                        out=ring[:, sl * DBW:(sl + 1) * DBW],
                        in_=self.ap["w2h"][r0:r0 + DBW,
                                           hc * 128:(hc + 1) * 128],
                        transpose=True),
                        extra=qks +
                              (gate0 if (th == 0 and db == 0) else []))
                    for ds in range(nds):
                        bank = ds % 4
                        mm = P.emit("t", lambda e, sl=sl, ds=ds, hw=hw,
                                    bank=bank, hc=hc: e.matmul(
                            pf[bank][:, 0:hw],
                            ring[:, sl * DBW + ds * 128:
                                 sl * DBW + ds * 128 + 128],
                            hn[:, hc * 512: hc * 512 + hw],
                            start=(hc == 0), stop=(hc == NH2 - 1)),
                            waits=[lt] + ([prev_acc[bank], hn_last]
                                          if hc == 0 else []),
                            inc="tp" if (hc == NH2 - 1 or ds == nds - 1)
                                else None)
                        last_mm[ds] = mm
                    rsl[sl].done(last_mm[nds - 1])
                for ds in range(nds):
                    dci = db * nds + ds
                    bank = ds % 4
                    opar = dci % 2
                    e1 = P.emit("v", lambda e, bank=bank, hw=hw, opar=opar:
                                e.tensor_tensor(
                        obuf[opar][:, 0:hw], pf[bank][:, 0:hw],
                        rbc[:, 0:hw], ALU.mult),
                        waits=[last_mm[ds], cp, prev_ob[opar]], inc="vp")
                    prev_acc[bank] = e1
                    e2 = P.emit("v", lambda e, ds=ds, hw=hw, opar=opar:
                                e.tensor_add(
                        obuf[opar][:, 0:hw], obuf[opar][:, 0:hw],
                        x1l[ds][:, 0:hw]),
                        waits=[txs[ds], e1], inc="vp")
                    ot = P.dma("y", lambda e, dci=dci, hoff=hoff, hw=hw,
                               opar=opar: e.dma_start(
                        out=self.ap["out_fm"][dci * 128:(dci + 1) * 128,
                                              hoff:hoff + hw],
                        in_=obuf[opar][:, 0:hw]),
                        waits=[e2], inc=f"outst{opar}")
                    prev_ob[opar] = ot
                    x1sl[ds].done(ot)
            if self.stop_after == "f2" and th == 0:
                fw = [("outst0", P.cnt["outst0"]), ("outst1", P.cnt["outst1"])]
                for s_ in ("qst0", "qst1"):
                    if s_ in P.cnt:
                        fw.append((s_, P.cnt[s_]))
                P.emit("y", lambda e: e.nop(), waits=fw)
                return
        if self.taps and self.taps >= 2:
            qs0 = ("qst0", P.cnt.get("qst0", 0))
            qs1 = ("qst1", P.cnt.get("qst1", 0))
            P.dma("y", lambda e: e.dma_start(out=self.ap["t_w1h"],
                                             in_=self.ap["w1h"]),
                  waits=[qs0, qs1], inc="tap")
            P.dma("y", lambda e: e.dma_start(out=self.ap["t_w2h"],
                                             in_=self.ap["w2h"]),
                  waits=[qs0, qs1], inc="tap")
        final_waits = [("outst0", P.cnt["outst0"]),
                       ("outst1", P.cnt["outst1"])]
        if self.taps:
            final_waits.append(("tap", P.cnt["tap"]))
        P.emit("y", lambda e: e.nop(), waits=final_waits)
        self.close_arena("L4")
        self.close_arena("L3b")
        self.close_arena("Q")


def build(cfg, taps=False, stop_after=None):
    return Kern(cfg, taps=taps, stop_after=stop_after).build()



def kernel(L0_attn_hidden_states, L0_attn_Wq, L0_attn_Wk, L0_attn_Wv,
           L0_attn_Wo, L0_attn_bo, L0_ffn_w_gate_up, L0_ffn_w_proj):
    from concourse.bass_utils import run_bass_kernel_spmd
    cfg = REAL
    ins = dict(
        x=np.asarray(L0_attn_hidden_states, dtype=np.float32),
        Wq=np.asarray(L0_attn_Wq, dtype=np.float32),
        Wk=np.asarray(L0_attn_Wk, dtype=np.float32),
        Wv=np.asarray(L0_attn_Wv, dtype=np.float32),
        Wo=np.asarray(L0_attn_Wo, dtype=np.float32),
        bo=np.asarray(L0_attn_bo, dtype=np.float32),
        w_gate_up=np.asarray(L0_ffn_w_gate_up, dtype=np.float32),
        w_proj=np.asarray(L0_ffn_w_proj, dtype=np.float32),
    )
    in_maps = host_inputs(cfg, ins)
    nc = build(cfg, taps=False)
    res = run_bass_kernel_spmd(nc, in_maps, core_ids=list(range(8)))
    return assemble(cfg, res.results)


# version 03
# speedup vs baseline: 1.5832x; 1.5832x over previous
import numpy as np
import ml_dtypes
from contextlib import ExitStack

import concourse.bass as bass
import concourse.mybir as mybir

F32 = mybir.dt.float32
BF16 = mybir.dt.bfloat16
AF = mybir.ActivationFunctionType
ALU = mybir.AluOpType
AX = mybir.AxisListType
BF = ml_dtypes.bfloat16

MAGIC = float(1.5 * 2 ** 23)
CQ = 0.501953125
SOFTCAP = 30.0
ROPE_THETA = 10000.0
LN_EPS = 1e-5
RMS_EPS = float(np.finfo(np.float32).eps)
NEG = -1e30

REAL = dict(T=4096, D=2048, H=16, G=4, HFF=8192)
MINI = dict(T=1024, D=256, H=4, G=2, HFF=512)
MIDI2 = dict(T=1024, D=1536, H=4, G=2, HFF=512, NRING_CAP=2)
MIDI4 = dict(T=4096, D=256, H=2, G=1, HFF=512)


def derived(cfg):
    d = dict(cfg)
    d["DC"] = cfg["D"] // 128
    d["NQT"] = cfg["T"] // 128
    d["NSLOT"] = d["NQT"] // 4
    d["NA_SLOTS"] = d["NSLOT"] // 2
    d["NB_SLOTS"] = d["NSLOT"] - d["NA_SLOTS"]
    d["QWA"] = 128 * d["NA_SLOTS"]
    d["QWB"] = 128 * d["NB_SLOTS"]
    d["NTA"] = 4 * d["NA_SLOTS"]
    d["NTB"] = 4 * d["NSLOT"]
    d["OWN"] = 128 * d["NSLOT"]
    d["HB"] = cfg["H"] // 2
    d["W1R"] = 2 * cfg["HFF"]
    d["NG1"] = cfg["D"] // 64
    d["NG2"] = cfg["HFF"] // 64
    d["NSTRIP"] = d["W1R"] // 512
    d["NH2"] = cfg["HFF"] // 128
    d["REP"] = cfg["H"] // cfg["G"]
    return d


def own_qtiles(cfg, lane):
    d = derived(cfg)
    return [lane + 4 * k for k in range(d["NSLOT"])]


def qcols(cfg, lane, qc):
    d = derived(cfg)
    slots = (range(0, d["NA_SLOTS"]) if qc == 0
             else range(d["NA_SLOTS"], d["NSLOT"]))
    cols = []
    for k in slots:
        t = lane + 4 * k
        cols.extend(range(t * 128, t * 128 + 128))
    return np.array(cols, dtype=np.int64)



def host_inputs(cfg, ins):
    d = derived(cfg)
    T, D, H, G, HFF = cfg["T"], cfg["D"], cfg["H"], cfg["G"], cfg["HFF"]
    x = np.ascontiguousarray(np.asarray(ins["x"], dtype=np.float32))
    assert x.shape == (2, T, D)

    inv = (1.0 / ROPE_THETA ** (np.arange(0, 128, 2, dtype=np.float64) / 128))
    pos = np.arange(T, dtype=np.float64)
    ang = pos[None, :] * np.repeat(inv, 2)[:, None]
    cos_fm = np.cos(ang).astype(np.float32)
    sin_fm = np.sin(ang).astype(np.float32)

    ident = np.eye(128, dtype=BF)
    ident_f = np.eye(128, dtype=np.float32)
    ones_col = np.ones((128, 1), dtype=BF)
    ones_row_f = np.ones((1, 128), dtype=np.float32)
    PT = np.zeros((128, 128), dtype=np.float32)
    for i in range(64):
        PT[2 * i, 2 * i + 1] = 1.0
        PT[2 * i + 1, 2 * i] = -1.0
    PT = PT.astype(BF)

    CSv = 1.0 / (np.float64(30.0) * np.sqrt(np.float64(128.0)))
    slopes = [np.float64(np.float32(2.0) ** np.float32(-(h + 1) * 8.0 / H))
              for h in range(H)]
    NBB = d["NTB"] // 4
    abias = np.zeros((128, H * NBB), dtype=np.float32)
    for h in range(H):
        for b_ in range(NBB):
            abias[:, h * NBB + b_] = np.float32(CSv * slopes[h] * (b_ * 512.0))

    bo_fm = np.ascontiguousarray(
        np.asarray(ins["bo"], dtype=np.float32).reshape(d["DC"], 128).T)

    in_maps = []
    for c in range(8):
        b, lane = c // 4, c % 4
        qa, qb = qcols(cfg, lane, 0), qcols(cfg, lane, 1)
        qall = np.concatenate([qa, qb])
        p = np.arange(128, dtype=np.int64)
        MA = np.zeros((d["NTA"], 128, d["QWA"]), dtype=BF)
        for t in range(d["NTA"]):
            MA[t] = ((t * 128 + p)[:, None] <= qa[None, :]).astype(BF)
        NMB = d["NTB"] - d["NTA"]
        MB = np.zeros((NMB, 128, d["QWB"]), dtype=BF)
        for t in range(NMB):
            MB[t] = (((t + d["NTA"]) * 128 + p)[:, None] <= qb[None, :]).astype(BF)
        DQB_A = np.zeros((128, 4 * d["QWA"]), dtype=np.float32)
        DQB_B = np.zeros((128, 4 * d["QWB"]), dtype=np.float32)
        for i in range(4):
            DQB_A[:, i * d["QWA"]:(i + 1) * d["QWA"]] = \
                (p[:, None] - qa[None, :] + 128 * i)
            DQB_B[:, i * d["QWB"]:(i + 1) * d["QWB"]] = \
                (p[:, None] - qb[None, :] + 128 * i)
        im = dict(
            x_all=x[b],
            wq=np.asarray(ins["Wq"], dtype=np.float32),
            wk=np.asarray(ins["Wk"], dtype=np.float32),
            wv=np.asarray(ins["Wv"], dtype=np.float32),
            wo=np.asarray(ins["Wo"], dtype=np.float32),
            bo_fm=bo_fm,
            w1=np.asarray(ins["w_gate_up"], dtype=np.float32),
            w2=np.asarray(ins["w_proj"], dtype=np.float32),
            cos_fm=cos_fm, sin_fm=sin_fm,
            cos_own=np.ascontiguousarray(cos_fm[:, qall]),
            sin_own=np.ascontiguousarray(sin_fm[:, qall]),
            ident=ident, ident_f=ident_f, ones_col=ones_col,
            ones_row_f=ones_row_f, rot_pt=PT,
            x_own=np.ascontiguousarray(x[b][qall, :]),
            abias=abias,
            ma=MA.reshape(d["NTA"] * 128, d["QWA"]),
            mb=MB.reshape(NMB * 128, d["QWB"]),
            dqb_a=DQB_A, dqb_b=DQB_B,
        )
        in_maps.append(im)
    return in_maps


def assemble(cfg, results):
    d = derived(cfg)
    T, D = cfg["T"], cfg["D"]
    out = np.empty((2, T, D), dtype=np.float32)
    for c in range(8):
        b, lane = c // 4, c % 4
        qall = np.concatenate([qcols(cfg, lane, 0), qcols(cfg, lane, 1)])
        out[b, qall, :] = results[c]["out_fm"].T
    return out



class PB:

    def __init__(self, nc, ctx):
        self.nc = nc
        self.ctx = ctx
        self.q = {k: [] for k in "tsvgy"}
        self.semh = {}
        self.cnt = {}

    def sem(self, name):
        if name not in self.semh:
            self.semh[name] = self.ctx.enter_context(self.nc.semaphore(name))
            self.cnt[name] = 0
        return name

    def emit(self, e, fn, waits=(), inc=None, k=1):
        ws = tuple(t for t in waits if t is not None and t[1] > 0)
        tok = None
        if inc is not None:
            self.sem(inc)
            self.cnt[inc] += k
            tok = (inc, self.cnt[inc])
        self.q[e].append((fn, ws, inc, k))
        return tok

    def dma(self, e, fn, waits=(), inc=None):
        return self.emit(e, fn, waits, inc, 16)

    def replay(self, block):
        semh = self.semh

        def runner(e):
            ops = self.q[e]

            def body(eng):
                clock = {}
                for fn, ws, inc, k in ops:
                    for s, v in ws:
                        if v > clock.get(s, 0):
                            eng.wait_ge(semh[s], v)
                            clock[s] = v
                    inst = fn(eng)
                    if inc is not None:
                        inst.then_inc(semh[inc], k)
            return body

        block.tensor(runner("t"))
        block.scalar(runner("s"))
        block.vector(runner("v"))
        block.gpsimd(runner("g"))
        block.sync(runner("y"))


class Slot:

    def __init__(self, P, name):
        self.P = P
        self.name = P.sem(name)
        self.consumers = []

    def load(self, e, fn, extra=()):
        waits = tuple(self.consumers) + tuple(extra)
        self.consumers = []
        return self.P.dma(e, fn, waits, inc=self.name)

    def done(self, tok):
        if tok is not None:
            self.consumers.append(tok)


class Kern:
    def __init__(self, cfg, taps=False, stop_after=None):
        self.cfg = derived(cfg)
        self.taps = taps
        self.stop_after = stop_after
        self.nc = bass.Bass("TRN2", target_bir_lowering=False)
        self.top = ExitStack()
        self.P = PB(self.nc, self.top)
        self.sb = {}
        self.stacks = {}

    def arena(self, key):
        if key not in self.stacks:
            self.stacks[key] = ExitStack()
        return self.stacks[key]

    def close_arena(self, key):
        if key in self.stacks:
            self.stacks.pop(key).close()

    ARENA_SIDE = {"P": "left", "Q": "left", "Lkv": "left", "Lqf": "left",
                  "L3b": "left", "L3": "left", "L4": "left",
                  "L1": "right", "Lro": "right", "L1q": "right",
                  "Lctx": "right", "L2": "right", "L4b": "left", "L4c": "left"}

    def sbuf(self, key, name, shape, dtype, **kw):
        h = self.arena(key).enter_context(
            self.nc.sbuf_tensor(name, shape, dtype,
                                side=self.ARENA_SIDE[key], **kw))
        self.sb[name] = h
        return h

    def psum(self, name, shape, dtype):
        h = self.top.enter_context(self.nc.psum_tensor(name, shape, dtype))
        self.sb[name] = h
        return h

    def build(self):
        c = self.cfg
        nc = self.nc
        T, D, DC, OWN = c["T"], c["D"], c["DC"], c["OWN"]
        H, G, HFF = c["H"], c["G"], c["HFF"]

        ap = {}
        EI, EO = "ExternalInput", "ExternalOutput"
        for n_, sh, dt_, kd in [
            ("x_all", [T, D], F32, EI), ("x_own", [OWN, D], F32, EI),
            ("wq", [D, H * 128], F32, EI), ("wk", [D, G * 128], F32, EI),
            ("wv", [D, G * 128], F32, EI), ("wo", [H * 128, D], F32, EI),
            ("bo_fm", [128, DC], F32, EI),
            ("w1", [c["W1R"], D], F32, EI), ("w2", [D, HFF], F32, EI),
            ("cos_fm", [128, T], F32, EI), ("sin_fm", [128, T], F32, EI),
            ("cos_own", [128, OWN], F32, EI), ("sin_own", [128, OWN], F32, EI),
            ("ident", [128, 128], BF16, EI), ("ident_f", [128, 128], F32, EI),
            ("ones_col", [128, 1], BF16, EI), ("ones_row_f", [1, 128], F32, EI),
            ("rot_pt", [128, 128], BF16, EI),
            ("abias", [128, c["H"] * (c["NTB"] // 4)], F32, EI),
            ("dqb_a", [128, 4 * c["QWA"]], F32, EI),
            ("dqb_b", [128, 4 * c["QWB"]], F32, EI),
            ("ma", [c["NTA"] * 128, c["QWA"]], BF16, EI),
            ("mb", [(c["NTB"] - c["NTA"]) * 128, c["QWB"]], BF16, EI),
            ("out_fm", [D, OWN], F32, EO),
            ("w1h", [c["W1R"], D], BF16, "Internal"),
            ("w2h", [D, HFF], BF16, "Internal"),
            ("x1s", [D, OWN], F32, "Internal"),
        ]:
            ap[n_] = nc.dram_tensor(n_, sh, dt_, kind=kd).ap()
        if self.taps:
            for n_, sh, dt_ in [
                ("t_k", [128, G * T], BF16), ("t_v", [128, (T // 128) * G * 128], BF16),
                ("t_q", [128, H * OWN], BF16), ("t_ctx", [128, H * OWN], BF16),
                ("t_x1", [D, OWN], F32), ("t_ln2", [128, DC * OWN], BF16),
                ("t_w1h", [c["W1R"], D], BF16), ("t_w2h", [D, HFF], BF16),
                ("t_hn", [128, c["NH2"] * OWN], BF16),
            ]:
                ap[n_] = nc.dram_tensor(n_, sh, dt_, kind=EO).ap()
        self.ap = ap

        pq4 = self.psum("pq4", [128, 2048], F32)
        p4 = self.psum("pf4t", [128, 512], F32)
        p5 = self.psum("pf5t", [128, 512], F32)
        self.psum("pbA", [128, 1024], BF16)
        self.psum("pbB", [128, 1024], BF16)
        self.pf = [pq4[:, 0:512], pq4[:, 512:1024], pq4[:, 1024:1536],
                   pq4[:, 1536:2048], p4[:], p5[:]]
        self.pq4 = pq4

        self.sbuf("P", "ident_sb", [128, 128], BF16)
        self.sbuf("P", "ident_f_sb", [128, 128], F32)
        self.sbuf("P", "ones_col_sb", [128, 1], BF16)
        self.sbuf("P", "orf_sb", [1, 128], F32)
        self.sbuf("P", "rot_pt_sb", [128, 128], BF16)
        self.sbuf("P", "bo_fm_sb", [128, DC], F32)
        P = self.P
        P.sem("cl")
        for src, dst in [("ident", "ident_sb"), ("ident_f", "ident_f_sb"),
                         ("ones_col", "ones_col_sb"), ("ones_row_f", "orf_sb"),
                         ("rot_pt", "rot_pt_sb"), ("bo_fm", "bo_fm_sb")]:
            P.dma("y", (lambda e, s=src, d=dst: e.dma_start(
                out=self.sb[d][:], in_=self.ap[s])), inc="cl")
        self.cl = ("cl", P.cnt["cl"])

        self.quant_init()
        self.sbuf("Lkv", "kfm", [128, G * T], BF16)
        self.sbuf("Lkv", "vtm", [128, (T // 128) * G * 128], BF16)
        done = False
        for name, fn in [("p1a", self.phase1a), ("p1b", self.phase1b),
                         ("attn", self.attention), ("wo", self.wo_ln2),
                         ("ffn", self.ffn)]:
            fn()
            if self.stop_after == name:
                done = True
                break
        if done and self.stop_after not in ("ffn", "f1", "f2"):
            self.finish_early()

        block = self.top.enter_context(self.nc.Block())
        P.replay(block)
        for k in reversed(list(self.stacks)):
            self.close_arena(k)
        self.top.close()
        return self.nc

    def finish_early(self):
        P = self.P
        assert self.taps, "finish_early requires taps"
        P.sem("tap")
        if self.stop_after in ("p1a", "p1b") and "kfm" in self.sb:
            P.dma("y", lambda e: e.dma_start(out=self.ap["t_k"],
                                             in_=self.sb["kfm"][:]),
                  waits=[self.kv_done[-1]], inc="tap")
            P.dma("y", lambda e: e.dma_start(out=self.ap["t_v"],
                                             in_=self.sb["vtm"][:]),
                  waits=[self.kv_done[-1]], inc="tap")
        if self.stop_after == "p1b":
            P.dma("y", lambda e: e.dma_start(out=self.ap["t_q"],
                                             in_=self.sb["qfm"][:]),
                  waits=[self.q_done[-1]], inc="tap")
        if self.stop_after == "attn":
            P.dma("y", lambda e: e.dma_start(out=self.ap["t_ctx"],
                                             in_=self.sb["ctxb"][:]),
                  waits=[self.ctx_done[-1]], inc="tap")
        fw = [("tap", P.cnt["tap"])]
        for s in ("qst0", "qst1"):
            if s in P.cnt:
                fw.append((s, P.cnt[s]))
        P.emit("y", lambda e: e.nop(), waits=fw)

    def quant_init(self):
        c = self.cfg
        D, HFF, W1R = c["D"], c["HFF"], c["W1R"]
        WB2 = min(2048, HFF)
        W = max(D, WB2)
        self.qW, self.qWB2 = W, WB2
        ng = W // 64
        qw0 = self.sbuf("Q", "qw0", [128, W], BF16)
        qu0 = self.sbuf("Q", "qu0", [128, W], BF16)
        self.qw = [qw0, qw0]
        self.qu = [qu0, qu0]
        self.qh = [qw0, qw0]
        qs0 = self.sbuf("Q", "qsums0", [128, ng], F32)
        qb0 = self.sbuf("Q", "qscb0", [128, ng], BF16)
        qr0 = self.sbuf("Q", "qrcp0", [128, ng], F32)
        self.qsums = [qs0, qs0]
        self.qscb = [qb0, qb0]
        self.qrcp = [qr0, qr0]
        _sl = Slot(self.P, "qwl0")
        _st = Slot(self.P, "qst0")
        self.qwsl = [_sl, _sl]
        self.qssl = [_st, _st]
        sched = []
        for j in range(c["NSTRIP"] // 2):
            for half in (0, HFF):
                for rb in range(4):
                    r = half + j * 512 + rb * 128
                    sched.append(dict(src="w1", dst="w1h", key=("w1h", r // 128),
                                      r=r, cc=0, w=D))
        for r in range(0, D, 128):
            for cc in range(0, HFF, WB2):
                sched.append(dict(src="w2", dst="w2h",
                                  key=("w2h", r // 128, cc // WB2),
                                  r=r, cc=cc, w=WB2))
        self.qsched = sched
        self.qpos = 0
        self.qgps_left = max(8, min(64, len(sched) // 3))
        self.qquota = {"p1b": 8, "attn": 32, "wo": 16}
        self.qtok = {}
        self.qprev = dict(u=[[], []], small=[[], []], qh=[[], []])
        self.qidx = 0

    def qsched_at(self, stage):
        if stage == "all":
            return len(self.qsched) - self.qpos
        q = self.qquota.get(stage, 0)
        if q > 0:
            self.qquota[stage] = q - 1
            return 1
        return 0

    def emit_quant(self, n, eng=None):
        P = self.P
        for _ in range(n):
            if self.qpos >= len(self.qsched):
                return
            blk = self.qsched[self.qpos]
            self.qpos += 1
            i = self.qidx
            self.qidx += 1
            par = i % 2
            E = eng or "v"
            w, ng = blk["w"], blk["w"] // 64
            r, cc = blk["r"], blk["cc"]
            src, dst = self.ap[blk["src"]], self.ap[blk["dst"]]
            qw, qu, qh = self.qw[par], self.qu[par], self.qh[par]
            qsums, qscb, qrcp = self.qsums[par], self.qscb[par], self.qrcp[par]
            ld = self.qwsl[par].load("g", lambda e, qw=qw, src=src, r=r, cc=cc,
                                     w=w: e.dma_start(
                out=qw[:, 0:w], in_=src[r:r + 128, cc:cc + w]))
            e_s = "vp" if E == "v" else "gp"
            red = P.emit("v", lambda e, qw=qw, qsums=qsums, w=w, ng=ng: e.reduce_sum(
                qsums[:, 0:ng].unsqueeze(2),
                qw[:, 0:w].rearrange("p (g k) -> p g k", k=64), axis=AX.X,
                apply_absolute_value=True),
                waits=[ld] + self.qprev["small"][par], inc="vp")
            sc1 = P.emit(E, lambda e, qscb=qscb, qsums=qsums, ng=ng: e.tensor_scalar(
                qscb[:, 0:ng], qsums[:, 0:ng], 1.0 / 64, None, ALU.mult),
                waits=[red], inc=e_s)
            sc2 = P.emit(E, lambda e, qscb=qscb, ng=ng: e.tensor_scalar_max(
                qscb[:, 0:ng], qscb[:, 0:ng], float(np.float32(BF(1e-8)))),
                waits=[sc1], inc=e_s)
            rc = P.emit("v", lambda e, qrcp=qrcp, qscb=qscb, ng=ng: e.reciprocal(
                qrcp[:, 0:ng], qscb[:, 0:ng]), waits=[sc2], inc="vp")
            u = P.emit(E, lambda e, qu=qu, qw=qw, qrcp=qrcp, w=w, ng=ng:
                       e.tensor_tensor(
                qu[:, 0:w].rearrange("p (g k) -> p g k", k=64),
                qw[:, 0:w].rearrange("p (g k) -> p g k", k=64),
                qrcp[:, 0:ng].unsqueeze(2).broadcast_to([128, ng, 64]),
                ALU.mult), waits=[rc] + self.qprev["u"][par], inc=e_s)
            self.qwsl[par].done(u)
            cl_ = P.emit(E, lambda e, qu=qu, w=w: e.tensor_scalar(
                qu[:, 0:w], qu[:, 0:w], 1.0, -1.0, ALU.min, ALU.max),
                waits=[u], inc=e_s)
            mg = P.emit(E, lambda e, qu=qu, w=w: e.tensor_scalar(
                qu[:, 0:w], qu[:, 0:w], MAGIC, -MAGIC, ALU.add, ALU.add),
                waits=[cl_], inc=e_s)
            wh = P.emit(E, lambda e, qh=qh, qu=qu, qscb=qscb, w=w, ng=ng:
                        e.tensor_tensor(
                qh[:, 0:w].rearrange("p (g k) -> p g k", k=64),
                qu[:, 0:w].rearrange("p (g k) -> p g k", k=64),
                qscb[:, 0:ng].unsqueeze(2).broadcast_to([128, ng, 64]),
                ALU.mult), waits=[mg] + self.qprev["qh"][par], inc=e_s)
            self.qprev["u"][0] = self.qprev["u"][1] = [wh]
            self.qprev["small"][0] = self.qprev["small"][1] = [u, wh]
            stok = self.qssl[par].load("y", lambda e, qh=qh, dst=dst, r=r, cc=cc,
                                       w=w: e.dma_start(
                out=dst[r:r + 128, cc:cc + w], in_=qh[:, 0:w]), extra=[wh])
            self.qssl[par].done(stok)
            self.qwsl[par].done(stok)
            self.qprev["qh"][0] = self.qprev["qh"][1] = [stok]
            self.qtok[blk["key"]] = stok

    def ln_tile(self, x_ap, out_ap, jk, st, sc, t, NT, extra, prev):
        P, D = self.P, self.cfg["D"]
        a1 = P.emit("s", lambda e: e.activation(
            out=jk[0][:], in_=x_ap, func=AF.Copy,
            accum_out=st[:, t:t + 1]),
            waits=[prev["a1"]] + extra, inc="sp")
        a2 = P.emit("s", lambda e: e.activation(
            out=jk[1][:], in_=x_ap, func=AF.Square,
            accum_out=st[:, NT + t:NT + t + 1]),
            waits=[prev["a2"], a1], inc="sp")
        prev["a1"], prev["a2"] = a1, a2
        b = 8 * (t % 2)
        pv = prev["sc"][t % 2]
        s1 = P.emit("v", lambda e: e.tensor_scalar(
            sc[:, b + 2:b + 3], st[:, t:t + 1], 1.0 / D, None, ALU.mult),
            waits=[a1, pv], inc="vp")
        s2 = P.emit("v", lambda e: e.tensor_mul(
            sc[:, b + 3:b + 4], sc[:, b + 2:b + 3], sc[:, b + 2:b + 3]),
            waits=[s1], inc="vp")
        s3 = P.emit("v", lambda e: e.tensor_scalar(
            sc[:, b + 4:b + 5], st[:, NT + t:NT + t + 1], 1.0 / D, LN_EPS,
            ALU.mult, ALU.add), waits=[a2], inc="vp")
        s4 = P.emit("v", lambda e: e.tensor_sub(
            sc[:, b + 4:b + 5], sc[:, b + 4:b + 5], sc[:, b + 3:b + 4]),
            waits=[s2, s3], inc="vp")
        s5 = P.emit("s", lambda e: e.activation(
            out=sc[:, b + 5:b + 6], in_=sc[:, b + 4:b + 5], func=AF.Ln),
            waits=[s4], inc="sp")
        s6 = P.emit("s", lambda e: e.activation(
            out=sc[:, b + 6:b + 7], in_=sc[:, b + 5:b + 6], func=AF.Exp,
            scale=-0.5), waits=[s5], inc="sp")
        s7 = P.emit("v", lambda e: e.scalar_tensor_tensor(
            sc[:, b + 7:b + 8], sc[:, b + 2:b + 3], -1.0,
            sc[:, b + 6:b + 7], ALU.mult, ALU.mult),
            waits=[s6, s2], inc="vp")
        prev["sc"][t % 2] = s7
        ap_ = P.emit("s", lambda e: e.activation(
            out=out_ap, in_=x_ap, func=AF.Identity,
            scale=sc[:, b + 6:b + 7], bias=sc[:, b + 7:b + 8]),
            waits=[s6, s7, prev["apply"]], inc="sp")
        prev["apply"] = ap_
        return ap_

    def phase1a(self):
        c, P = self.cfg, self.P
        T, D, DC, G = c["T"], c["D"], c["DC"], c["G"]
        NT, NCH = T // 128, T // 512
        GW = G * 128

        xt = [self.sbuf("L1", f"xt{i}", [128, D], F32) for i in (0, 1)]
        ln1tm = [self.sbuf("L1", f"ln1tm{i}", [128, D], BF16) for i in (0, 1)]
        lnfm = [self.sbuf("L1", f"lnfm{i}", [128, DC * 512], BF16) for i in (0, 1)]
        csb = [self.sbuf("L1", f"csb{i}", [128, 512], F32) for i in (0, 1)]
        ssb = [self.sbuf("L1", f"ssb{i}", [128, 512], F32) for i in (0, 1)]
        jk = [self.sbuf("L1", f"jk{i}", [128, D], BF16) for i in (0, 1)]
        st = self.sbuf("L1", "st", [128, 2 * NT], F32)
        sc = self.sbuf("L1", "sc", [128, 16], F32)
        kraw = [self.sbuf("L1", f"kraw{i}", [128, 512], BF16) for i in (0, 1)]
        rt1 = self.sbuf("L1", "rt1", [128, 512], F32)
        rt2 = self.sbuf("L1", "rt2", [128, 512], F32)
        wkb = self.sbuf("L1", "wkb", [128, DC * GW], BF16)
        wvb = self.sbuf("L1", "wvb", [128, DC * GW], BF16)

        kfm, vtm = self.sb["kfm"], self.sb["vtm"]
        pf = self.pf
        pbA = self.sb["pbA"]

        P.sem("wkv")
        P.dma("g", lambda e: e.dma_start(
            out=wkb[:].rearrange("p (dc n) -> p dc n", n=GW),
            in_=self.ap["wk"].rearrange("(dc p) n -> p dc n", p=128)), inc="wkv")
        P.dma("g", lambda e: e.dma_start(
            out=wvb[:].rearrange("p (dc n) -> p dc n", n=GW),
            in_=self.ap["wv"].rearrange("(dc p) n -> p dc n", p=128)), inc="wkv")
        wkv_tok = ("wkv", 32)

        xsl = [Slot(P, "xls0"), Slot(P, "xls1")]
        cssl = [Slot(P, "csl0"), Slot(P, "csl1")]
        t_ms = P.emit("v", lambda e: e.memset(st[:], 0.0), inc="vp")

        prev = dict(a1=None, a2=None, apply=None, sc=[None, None],
                    lnfm=[[], []], kraw=[[], []], rt=None, pf=[None] * 6, pbA=[])
        kv_done = []

        for ch in range(NCH):
            par = ch % 2
            tcs = cssl[par].load("y", lambda e, ch=ch, par=par: e.dma_start(
                out=csb[par][:], in_=self.ap["cos_fm"][:, ch * 512:(ch + 1) * 512]))
            tss = cssl[par].load("y", lambda e, ch=ch, par=par: e.dma_start(
                out=ssb[par][:], in_=self.ap["sin_fm"][:, ch * 512:(ch + 1) * 512]))
            lnfm_rdy = []
            for tl in range(4):
                t = ch * 4 + tl
                tp2 = t % 2
                tlx = xsl[tp2].load("g", lambda e, t=t, tp2=tp2: e.dma_start(
                    out=xt[tp2][:], in_=self.ap["x_all"][t * 128:(t + 1) * 128, :]))
                ap_ = self.ln_tile(xt[tp2][:], ln1tm[tp2][:], jk, st, sc, t, NT,
                                   [tlx, t_ms], prev)
                xsl[tp2].done(ap_)
                nb = (DC + 7) // 8
                for j in range(nb):
                    dcs = list(range(8 * j, min(8 * j + 8, DC)))
                    n8 = len(dcs)
                    tt = None
                    for ii, dc in enumerate(dcs):
                        tt = P.emit("t", lambda e, tp2=tp2, dc=dc, ii=ii: e.transpose(
                            pbA[:, ii * 128:(ii + 1) * 128],
                            ln1tm[tp2][:, dc * 128:(dc + 1) * 128],
                            self.sb["ident_sb"][:]),
                            waits=([ap_, self.cl] + prev["pbA"]) if ii == 0 else [],
                            inc="tp" if ii == n8 - 1 else None)
                    ev = P.emit("v", lambda e, par=par, tl=tl, j=j, n8=n8:
                                e.tensor_copy(
                        out=lnfm[par][:].rearrange("p (dc n) -> p dc n", n=512)
                            [:, 8 * j:8 * j + n8, tl * 128:(tl + 1) * 128],
                        in_=pbA[:].rearrange("p (k n) -> p k n", n=128)[:, 0:n8, :]),
                        waits=[tt] + prev["lnfm"][par], inc="vp")
                    lnfm_rdy.append(ev)
                    prev["pbA"] = [ev]
                self.emit_quant(2)
            prev["lnfm"][par] = []
            for g in range(G):
                bank = g % 2
                mmt = None
                for dc in range(DC):
                    mmt = P.emit("t", lambda e, g=g, dc=dc, par=par, bank=bank:
                                 e.matmul(
                        pf[bank][:, 0:512],
                        wkb[:, dc * GW + g * 128: dc * GW + g * 128 + 128],
                        lnfm[par][:, dc * 512:(dc + 1) * 512],
                        start=(dc == 0), stop=(dc == DC - 1)),
                        waits=([wkv_tok, prev["pf"][bank]] + lnfm_rdy)
                              if dc == 0 else [],
                        inc="tp" if dc == DC - 1 else None)
                kb = g % 2
                e1 = P.emit("s", lambda e, bank=bank, kb=kb: e.activation(
                    out=kraw[kb][:], in_=pf[bank][:, 0:512], func=AF.Copy),
                    waits=[mmt] + prev["kraw"][kb], inc="sp")
                prev["pf"][bank] = e1
                rmt = P.emit("t", lambda e, kb=kb: e.matmul(
                    pf[2][:, 0:512], self.sb["rot_pt_sb"][:], kraw[kb][:],
                    start=True, stop=True),
                    waits=[e1, prev["pf"][2], self.cl], inc="tp")
                d1 = P.emit("v", lambda e, par=par, kb=kb: e.tensor_mul(
                    rt1[:], kraw[kb][:], csb[par][:]),
                    waits=[e1, tss, prev["rt"]], inc="vp")
                d2 = P.emit("v", lambda e, par=par: e.tensor_mul(
                    rt2[:], pf[2][:, 0:512], ssb[par][:]),
                    waits=[rmt, tss, prev["rt"]], inc="vp")
                prev["pf"][2] = d2
                d3 = P.emit("v", lambda e, g=g, ch=ch: e.tensor_add(
                    kfm[:, g * T + ch * 512: g * T + ch * 512 + 512],
                    rt1[:], rt2[:]), waits=[d1, d2], inc="vp")
                prev["rt"] = d3
                prev["kraw"][kb] = [rmt, d1]
                kv_done.append(d3)
            cssl[par].done(kv_done[-1])
            last_v = None
            for tl in range(4):
                t = ch * 4 + tl
                mmt = None
                for dc in range(DC):
                    mmt = P.emit("t", lambda e, tl=tl, dc=dc, par=par: e.matmul(
                        pf[4][:, 0:GW],
                        lnfm[par][:, dc * 512 + tl * 128: dc * 512 + tl * 128 + 128],
                        wvb[:, dc * GW:(dc + 1) * GW],
                        start=(dc == 0), stop=(dc == DC - 1)),
                        waits=[wkv_tok, prev["pf"][4]] if dc == 0 else [],
                        inc="tp" if dc == DC - 1 else None)
                ev = P.emit("s", lambda e, t=t: e.activation(
                    out=vtm[:, t * GW:(t + 1) * GW], in_=pf[4][:, 0:GW],
                    func=AF.Copy), waits=[mmt], inc="sp")
                prev["pf"][4] = ev
                kv_done.append(ev)
                last_v = ev
            prev["lnfm"][par] = [last_v]
        self.kv_done = kv_done
        self.close_arena("L1")

    def phase1b(self):
        c, P = self.cfg, self.P
        D, DC, OWN, H, T = c["D"], c["DC"], c["OWN"], c["H"], c["T"]
        NSLOT = c["NSLOT"]
        NTC = (OWN + 511) // 512
        self.sbuf("Lqf", "qfm", [128, H * OWN], BF16)
        cosq = self.sbuf("Lro", "cosq", [128, OWN], F32)
        sinq = self.sbuf("Lro", "sinq", [128, OWN], F32)
        q_in = self.sbuf("L1q", "q_in", [128, DC * OWN], BF16)
        xo = self.sbuf("L1q", "xoq", [128, D], F32)
        lntm = [self.sbuf("L1q", f"lntm{i}", [128, D], BF16) for i in (0, 1)]
        jkq = self.sbuf("L1q", "jkq0", [128, D], BF16)
        jk = [jkq, jkq]
        st = self.sbuf("L1q", "stq", [128, 2 * NSLOT], F32)
        sc = self.sbuf("L1q", "scq", [128, 16], F32)
        wqh = [self.sbuf("L1q", f"wqh{i}", [128, DC * 128], BF16) for i in (0, 1)]
        qraw = self.sbuf("L1q", "qraw", [128, OWN], BF16)
        rq1 = self.sbuf("L1q", "rq1", [128, 512], F32)
        rq2 = self.sbuf("L1q", "rq2", [128, 512], F32)
        qfm = self.sb["qfm"]
        pf = self.pf
        pbA = self.sb["pbA"]

        P.sem("roL")
        pgate = [self.kv_done[-1]]
        for src, dst in [("cos_own", cosq), ("sin_own", sinq)]:
            P.dma("y", lambda e, s=src, d=dst: e.dma_start(out=d[:], in_=self.ap[s]),
                  waits=pgate, inc="roL")
        ro_tok = ("roL", P.cnt["roL"])

        xsl = Slot(P, "xoq0")
        wsl = [Slot(P, "wql0"), Slot(P, "wql1")]
        t_ms = P.emit("v", lambda e: e.memset(st[:], 0.0), waits=pgate, inc="vp")
        prev = dict(a1=None, a2=None, apply=None, sc=[None, None], pbA=[],
                    pf=[None] * 6, qraw=[], rq=None)
        qin_rdy = []
        for k in range(NSLOT):
            par = k % 2
            tlx = xsl.load("g", lambda e, k=k: e.dma_start(
                out=xo[:], in_=self.ap["x_own"][k * 128:(k + 1) * 128, :]),
                extra=pgate if k == 0 else [])
            ap_ = self.ln_tile(xo[:], lntm[par][:], jk, st, sc, k, NSLOT,
                               [tlx, t_ms], prev)
            xsl.done(ap_)
            nb = (DC + 7) // 8
            for j in range(nb):
                dcs = list(range(8 * j, min(8 * j + 8, DC)))
                n8 = len(dcs)
                tt = None
                for ii, dc in enumerate(dcs):
                    tt = P.emit("t", lambda e, par=par, dc=dc, ii=ii: e.transpose(
                        pbA[:, ii * 128:(ii + 1) * 128],
                        lntm[par][:, dc * 128:(dc + 1) * 128],
                        self.sb["ident_sb"][:]),
                        waits=([ap_, self.cl] + prev["pbA"]) if ii == 0 else [],
                        inc="tp" if ii == n8 - 1 else None)
                ev = P.emit("v", lambda e, k=k, j=j, n8=n8: e.tensor_copy(
                    out=q_in[:].rearrange("p (dc n) -> p dc n", n=OWN)
                        [:, 8 * j:8 * j + n8, k * 128:(k + 1) * 128],
                    in_=pbA[:].rearrange("p (k n) -> p k n", n=128)[:, 0:n8, :]),
                    waits=[tt], inc="vp")
                prev["pbA"] = [ev]
                qin_rdy.append(ev)
        self.q_done = []
        for h in range(H):
            par = h % 2
            tw_ = wsl[par].load("g", lambda e, h=h, par=par: e.dma_start(
                out=wqh[par][:].rearrange("p (dc n) -> p dc n", n=128),
                in_=self.ap["wq"][:, h * 128:(h + 1) * 128]
                    .rearrange("(dc p) n -> p dc n", p=128)),
                extra=pgate if h < 2 else [])
            for tc in range(NTC):
                off = tc * 512
                tww = min(512, OWN - off)
                bank = tc % 2
                mmt = None
                for dc in range(DC):
                    mmt = P.emit("t", lambda e, dc=dc, par=par, bank=bank,
                                 off=off, tww=tww: e.matmul(
                        pf[bank][:, 0:tww],
                        wqh[par][:, dc * 128:(dc + 1) * 128],
                        q_in[:, dc * OWN + off: dc * OWN + off + tww],
                        start=(dc == 0), stop=(dc == DC - 1)),
                        waits=([tw_, prev["pf"][bank]] + qin_rdy)
                              if dc == 0 else [],
                        inc="tp" if dc == DC - 1 else None)
                e1 = P.emit("s", lambda e, bank=bank, off=off, tww=tww: e.activation(
                    out=qraw[:, off:off + tww], in_=pf[bank][:, 0:tww],
                    func=AF.Copy), waits=[mmt] + prev["qraw"], inc="sp")
                prev["pf"][bank] = e1
                rmt = P.emit("t", lambda e, off=off, tww=tww: e.matmul(
                    pf[2][:, 0:tww], self.sb["rot_pt_sb"][:],
                    qraw[:, off:off + tww], start=True, stop=True),
                    waits=[e1, prev["pf"][2]], inc="tp")
                d1 = P.emit("v", lambda e, off=off, tww=tww: e.tensor_mul(
                    rq1[:, 0:tww], qraw[:, off:off + tww],
                    cosq[:, off:off + tww]),
                    waits=[e1, ro_tok, prev["rq"]], inc="vp")
                d2 = P.emit("v", lambda e, off=off, tww=tww: e.tensor_mul(
                    rq2[:, 0:tww], pf[2][:, 0:tww],
                    sinq[:, off:off + tww]), waits=[rmt, prev["rq"]], inc="vp")
                prev["pf"][2] = d2
                d3 = P.emit("v", lambda e, h=h, off=off, tww=tww: e.tensor_add(
                    qfm[:, h * OWN + off: h * OWN + off + tww],
                    rq1[:, 0:tww], rq2[:, 0:tww]),
                    waits=[d1, d2], inc="vp")
                prev["rq"] = d3
                prev["qraw"] = [rmt, d1]
                self.q_done.append(d3)
            wsl[par].done(self.q_done[-1])
            self.emit_quant(1)
        self.close_arena("L1q")
        self.close_arena("Lro")

    def attention(self):
        c, P = self.cfg, self.P
        T, OWN, H, G = c["T"], c["OWN"], c["H"], c["G"]
        REP, GW = c["REP"], c["G"] * 128
        kfm, vtm, qfm = self.sb["kfm"], self.sb["vtm"], self.sb["qfm"]
        ctxb = self.sbuf("Lctx", "ctxb", [128, H * OWN], BF16)
        pq4 = self.pq4
        pf4, pf5 = self.pf[4], self.pf[5]
        CS = 1.0 / (SOFTCAP * float(np.sqrt(128.0)))
        gate = [self.q_done[-1], self.kv_done[-1]]

        NMA = c["NTA"]
        NMB = c["NTB"] - c["NTA"]
        MW = max(NMA * c["QWA"], NMB * c["QWB"])
        mbuf = self.sbuf("L2", "mbuf", [128, MW], BF16)
        u4 = [self.sbuf("L2", f"u4{i}", [128, 2048], F32) for i in (0, 1)]
        p4b = [self.sbuf("L2", f"p4b{i}", [128, 2048], BF16) for i in (0, 1)]
        dsb = self.sbuf("L2", "dsb0", [1, 512], F32)
        abias = self.sbuf("L2", "abias_sb", [128, c["H"] * (c["NTB"] // 4)], F32)
        dqba = self.sbuf("L2", "dqba", [128, 4 * c["QWA"]], F32)
        dqbb = self.sbuf("L2", "dqbb", [128, 4 * c["QWB"]], F32)
        orf = self.sb["orf_sb"]

        self.tap_kv = None
        if self.taps:
            P.sem("tap")
            P.dma("y", lambda e: e.dma_start(out=self.ap["t_k"], in_=kfm[:]),
                  waits=[self.kv_done[-1], self.q_done[-1]], inc="tap")
            P.dma("y", lambda e: e.dma_start(out=self.ap["t_v"], in_=vtm[:]),
                  waits=[self.kv_done[-1]], inc="tap")
            P.dma("y", lambda e: e.dma_start(out=self.ap["t_q"], in_=qfm[:]),
                  waits=[self.q_done[-1]], inc="tap")
            self.tap_kv = ("tap", P.cnt["tap"])
        P.sem("abl")
        P.dma("y", lambda e: e.dma_start(out=abias[:], in_=self.ap["abias"]),
              waits=gate, inc="abl")
        P.dma("y", lambda e: e.dma_start(out=dqba[:], in_=self.ap["dqb_a"]),
              inc="abl")
        P.dma("y", lambda e: e.dma_start(out=dqbb[:], in_=self.ap["dqb_b"]),
              inc="abl")
        ab_tok = ("abl", P.cnt["abl"])

        msl = Slot(P, "mal")
        slopes = [float(np.float32(2.0) ** np.float32(-(h + 1) * 8.0 / H))
                  for h in range(H)]
        NBB = c["NTB"] // 4
        self.ctx_done = []
        prev_stt = None
        prev_ctx = None
        prev_den = None
        prev_dsb = None
        prev_u4 = [None, None]
        prev_p4 = [None, None]
        for qc in (0, 1):
            TW = c["QWA"] if qc == 0 else c["QWB"]
            NTq = c["NTA"] if qc == 0 else c["NTB"]
            NB = NTq // 4
            BW = 4 * TW
            qoff = 0 if qc == 0 else c["QWA"]
            dqb = dqba if qc == 0 else dqbb
            src_m = "ma" if qc == 0 else "mb"
            NSKIP = 0 if qc == 0 else c["NTA"]
            NM = NTq - NSKIP
            m_tok = msl.load("y", lambda e, src_m=src_m, TW=TW, NM=NM:
                             e.dma_start(
                out=mbuf[:, 0:NM * TW].rearrange("p (t f) -> p t f", f=TW),
                in_=self.ap[src_m].rearrange("(t p) f -> p t f", p=128)),
                extra=gate + [ab_tok] if qc == 0 else [])
            last_mask = None
            for h in range(H):
                g = h // REP

                def emit_qk(b, first_wait):
                    mmt = None
                    for i in range(4):
                        s = 4 * b + i
                        mmt = P.emit("t", lambda e, s=s, i=i, h=h, g=g,
                                     qoff=qoff, TW=TW: e.matmul(
                            pq4[:, i * TW:(i + 1) * TW],
                            kfm[:, g * T + s * 128: g * T + s * 128 + 128],
                            qfm[:, h * OWN + qoff: h * OWN + qoff + TW],
                            start=True, stop=True),
                            waits=(first_wait + gate) if i == 0 else [],
                            inc="tp" if i == 3 else None)
                    return mmt
                qk_tok = emit_qk(0, [prev_stt, prev_ctx][0:1] if prev_stt else [])
                for b in range(NB):
                    par = b % 2
                    stt = P.emit("v", lambda e, h=h, qc=qc, BW=BW, par=par,
                                 dqb=dqb: e.scalar_tensor_tensor(
                        u4[par][:, 0:BW], dqb[:, 0:BW], slopes[h],
                        pq4[:, 0:BW], ALU.mult, ALU.add),
                        waits=[qk_tok, ab_tok, prev_u4[par]], inc="vp")
                    prev_stt = stt
                    th = P.emit("s", lambda e, h=h, b=b, BW=BW, par=par:
                                e.activation(
                        out=u4[par][:, 0:BW], in_=u4[par][:, 0:BW],
                        func=AF.Tanh, scale=CS,
                        bias=abias[:, h * NBB + b: h * NBB + b + 1]),
                        waits=[stt], inc="sp")
                    ex = P.emit("s", lambda e, BW=BW, par=par: e.activation(
                        out=p4b[par][:, 0:BW], in_=u4[par][:, 0:BW],
                        func=AF.Exp, scale=SOFTCAP),
                        waits=[th, prev_p4[par]], inc="sp")
                    prev_u4[par] = ex
                    if 4 * b >= NSKIP:
                        mo = (4 * b - NSKIP) * TW
                        mk = P.emit("v", lambda e, mo=mo, BW=BW, par=par:
                                    e.tensor_mul(
                            p4b[par][:, 0:BW], p4b[par][:, 0:BW],
                            mbuf[:, mo: mo + BW]),
                            waits=[ex, m_tok], inc="vp")
                        last_mask = mk
                    else:
                        mk = ex
                    if b + 1 < NB:
                        qk_tok = emit_qk(b + 1, [stt])
                    pv = None
                    for i in range(4):
                        s = 4 * b + i
                        pv = P.emit("t", lambda e, s=s, i=i, g=g, TW=TW, b=b,
                                    NB=NB, par=par: e.matmul(
                            pf4[:, 0:TW],
                            vtm[:, s * GW + g * 128: s * GW + g * 128 + 128],
                            p4b[par][:, i * TW:(i + 1) * TW],
                            start=(b == 0 and i == 0),
                            stop=(b == NB - 1 and i == 3)),
                            waits=[mk] + ([prev_ctx] if b == 0 and i == 0
                                          else []),
                            inc=None)
                    dn = None
                    for i in range(4):
                        dn = P.emit("t", lambda e, i=i, TW=TW, b=b, NB=NB,
                                    par=par: e.matmul(
                            pf5[0:1, 0:TW], self.sb["ones_col_sb"][:],
                            p4b[par][:, i * TW:(i + 1) * TW],
                            start=(b == 0 and i == 0),
                            stop=(b == NB - 1 and i == 3)),
                            waits=[prev_den] if b == 0 and i == 0 else [],
                            inc="tp" if i == 3 else None)
                    prev_p4[par] = dn
                dc_ = P.emit("v", lambda e, TW=TW: e.tensor_copy(
                    out=dsb[:, 0:TW], in_=pf5[0:1, 0:TW]),
                    waits=[prev_p4[(NB - 1) % 2], prev_dsb], inc="vp")
                prev_den = dc_
                rcd = P.emit("v", lambda e, TW=TW: e.reciprocal(
                    dsb[:, 0:TW], dsb[:, 0:TW]), waits=[dc_], inc="vp")
                bc = P.emit("t", lambda e, TW=TW: e.matmul(
                    pq4[:, 0:TW], orf[:], dsb[:, 0:TW],
                    start=True, stop=True),
                    waits=[rcd, prev_stt, self.cl], inc="tp")
                ev = P.emit("s", lambda e, h=h, qoff=qoff, TW=TW: e.activation(
                    out=ctxb[:, h * OWN + qoff: h * OWN + qoff + TW],
                    in_=pf4[:, 0:TW], func=AF.Copy),
                    waits=[prev_p4[(NB - 1) % 2]], inc="sp")
                prev_ctx = ev
                dv = P.emit("v", lambda e, h=h, qoff=qoff, TW=TW:
                            e.tensor_tensor(
                    ctxb[:, h * OWN + qoff: h * OWN + qoff + TW],
                    ctxb[:, h * OWN + qoff: h * OWN + qoff + TW],
                    pq4[:, 0:TW], ALU.mult),
                    waits=[bc, ev], inc="vp")
                prev_dsb = bc
                prev_stt = dv
                self.ctx_done.append(dv)
                self.emit_quant(1)
            msl.done(last_mask)
        if self.taps:
            P.dma("y", lambda e: e.dma_start(out=self.ap["t_ctx"], in_=ctxb[:]),
                  waits=[self.ctx_done[-1]], inc="tap")
            self.tap_kv = ("tap", P.cnt["tap"])
        self.close_arena("L2")
        self.close_arena("Lqf")
        self.close_arena("Lkv")

    def wo_ln2(self):
        c, P = self.cfg, self.P
        D, DC, OWN, H = c["D"], c["DC"], c["OWN"], c["H"]
        NSLOT, NTC = c["NSLOT"], (OWN + 511) // 512
        pf = self.pf
        ctxb = self.sb["ctxb"]
        gate = [self.ctx_done[-1]]
        if self.taps and self.tap_kv:
            gate.append(self.tap_kv)

        ln2x = self.sbuf("L3b", "ln2x", [128, DC * OWN], BF16)
        x1 = self.sbuf("L3", "x1", [128, DC * OWN], F32)
        xo = self.sbuf("L3", "xw", [128, D], F32)
        wob = [self.sbuf("L3", f"wob{i}", [128, H * 128], BF16) for i in (0, 1)]
        x1b = self.sbuf("L3", "x1b", [128, 512], BF16)
        x1q = self.sbuf("L3", "x1q", [128, 512], BF16)
        smm = self.sbuf("L3", "smm", [1, 512], F32)
        smv = self.sbuf("L3", "smv", [1, 512], F32)
        smr = self.sbuf("L3", "smr", [1, 512], F32)
        tmp = self.sbuf("L3", "tmpw", [128, 512], F32)

        xsl = Slot(P, "xw0")
        wsl = [Slot(P, "wol0"), Slot(P, "wol1")]
        orf = self.sb["orf_sb"]

        prev_pf5 = None
        prev_ev = None
        for k in range(NSLOT):
            tlx = xsl.load("g", lambda e, k=k: e.dma_start(
                out=xo[:], in_=self.ap["x_own"][k * 128:(k + 1) * 128, :]),
                extra=gate if k == 0 else [])
            for j in range((DC + 3) // 4):
                dcs = list(range(4 * j, min(4 * j + 4, DC)))
                n4 = len(dcs)
                tt = None
                for ii, dc in enumerate(dcs):
                    tt = P.emit("t", lambda e, dc=dc, ii=ii: e.transpose(
                        pf[5][:, ii * 128:(ii + 1) * 128],
                        xo[:, dc * 128:(dc + 1) * 128],
                        self.sb["ident_f_sb"][:]),
                        waits=([tlx, prev_pf5, self.cl] if ii == 0 else []),
                        inc="tp" if ii == n4 - 1 else None)
                ev = P.emit("s", lambda e, k=k, j=j, n4=n4: e.copy(
                    out=x1[:].rearrange("p (dc n) -> p dc n", n=OWN)
                        [:, 4 * j:4 * j + n4, k * 128:(k + 1) * 128],
                    in_=pf[5][:].rearrange("p (q n) -> p q n", n=128)[:, 0:n4, :]),
                    waits=[tt, prev_ev], inc="sp")
                prev_pf5 = ev
                prev_ev = ev
            xsl.done(prev_ev)
        xfm_tok = prev_ev
        self.x1st = Slot(P, "x1st")
        ln2_done = []
        prev_bank = [None, None]
        prev_statmm = [None, None]
        prev_apply = None
        prev_x1bq = None
        prev_small = None
        for tc in range(NTC):
            off = tc * 512
            tw = min(512, OWN - off)
            bank = tc % 2
            stat_mm = [None, None]
            for dc in range(DC):
                par = dc % 2
                twl = wsl[par].load("g", lambda e, dc=dc, par=par: e.dma_start(
                    out=wob[par][:].rearrange("p (hc n) -> p hc n", n=128),
                    in_=self.ap["wo"][:, dc * 128:(dc + 1) * 128]
                        .rearrange("(hc p) n -> p hc n", p=128)),
                    extra=gate if (tc == 0 and dc < 2) else [])
                mmt = None
                for hc in range(H):
                    mmt = P.emit("t", lambda e, bank=bank, par=par, hc=hc, off=off,
                                 tw=tw: e.matmul(
                        pf[bank][:, 0:tw],
                        wob[par][:, hc * 128:(hc + 1) * 128],
                        ctxb[:, hc * OWN + off: hc * OWN + off + tw],
                        start=(hc == 0), stop=(hc == H - 1)),
                        waits=([twl, prev_bank[bank], xfm_tok] + gate)
                              if hc == 0 else [],
                        inc="tp" if hc == H - 1 else None)
                wsl[par].done(mmt)
                stt = P.emit("v", lambda e, bank=bank, dc=dc, off=off, tw=tw:
                             e.scalar_tensor_tensor(
                    x1[:, dc * OWN + off: dc * OWN + off + tw],
                    pf[bank][:, 0:tw], self.sb["bo_fm_sb"][:, dc:dc + 1],
                    x1[:, dc * OWN + off: dc * OWN + off + tw],
                    ALU.add, ALU.add),
                    waits=[mmt, self.cl], inc="vp")
                prev_bank[bank] = stt
                c1 = P.emit("s", lambda e, dc=dc, off=off, tw=tw: e.activation(
                    out=x1b[:, 0:tw],
                    in_=x1[:, dc * OWN + off: dc * OWN + off + tw],
                    func=AF.Copy), waits=[stt, prev_x1bq], inc="sp")
                c2 = P.emit("s", lambda e, dc=dc, off=off, tw=tw: e.activation(
                    out=x1q[:, 0:tw],
                    in_=x1[:, dc * OWN + off: dc * OWN + off + tw],
                    func=AF.Square), waits=[stt], inc="sp")
                m1 = P.emit("t", lambda e, dc=dc, tw=tw: e.matmul(
                    pf[2][0:1, 0:tw], self.sb["ones_col_sb"][:], x1b[:, 0:tw],
                    start=(dc == 0), stop=(dc == DC - 1)),
                    waits=[c1] + ([prev_statmm[0]] if dc == 0 else []), inc="tp")
                m2 = P.emit("t", lambda e, dc=dc, tw=tw: e.matmul(
                    pf[3][0:1, 0:tw], self.sb["ones_col_sb"][:], x1q[:, 0:tw],
                    start=(dc == 0), stop=(dc == DC - 1)),
                    waits=[c2] + ([prev_statmm[1]] if dc == 0 else []), inc="tp")
                stat_mm = [m1, m2]
                prev_x1bq = m2
            s1 = P.emit("v", lambda e, tw=tw: e.tensor_scalar(
                smm[:, 0:tw], pf[2][0:1, 0:tw], 1.0 / D, None, ALU.mult),
                waits=[stat_mm[0], prev_small], inc="vp")
            s2 = P.emit("v", lambda e, tw=tw: e.tensor_scalar(
                smv[:, 0:tw], pf[3][0:1, 0:tw], 1.0 / D, LN_EPS,
                ALU.mult, ALU.add), waits=[stat_mm[1]], inc="vp")
            s3 = P.emit("v", lambda e, tw=tw: e.tensor_mul(
                smr[:, 0:tw], smm[:, 0:tw], smm[:, 0:tw]), waits=[s1], inc="vp")
            s4 = P.emit("v", lambda e, tw=tw: e.tensor_sub(
                smv[:, 0:tw], smv[:, 0:tw], smr[:, 0:tw]), waits=[s2, s3], inc="vp")
            s5 = P.emit("s", lambda e, tw=tw: e.activation(
                out=smr[:, 0:tw], in_=smv[:, 0:tw], func=AF.Ln),
                waits=[s4], inc="sp")
            s6 = P.emit("s", lambda e, tw=tw: e.activation(
                out=smr[:, 0:tw], in_=smr[:, 0:tw], func=AF.Exp, scale=-0.5),
                waits=[s5], inc="sp")
            b1 = P.emit("t", lambda e, tw=tw: e.matmul(
                pf[2][:, 0:tw], orf[:], smm[:, 0:tw], start=True, stop=True),
                waits=[s6, s1], inc="tp")
            b2 = P.emit("t", lambda e, tw=tw: e.matmul(
                pf[3][:, 0:tw], orf[:], smr[:, 0:tw], start=True, stop=True),
                waits=[s6], inc="tp")
            prev_small = s6
            ap_t = prev_apply
            for dc in range(DC):
                a1 = P.emit("v", lambda e, dc=dc, off=off, tw=tw: e.tensor_tensor(
                    tmp[:, 0:tw], x1[:, dc * OWN + off: dc * OWN + off + tw],
                    pf[2][:, 0:tw], ALU.subtract),
                    waits=[b1, prev_bank[0], prev_bank[1], ap_t], inc="vp")
                ap_t = P.emit("v", lambda e, dc=dc, off=off, tw=tw: e.tensor_tensor(
                    ln2x[:, dc * OWN + off: dc * OWN + off + tw],
                    tmp[:, 0:tw], pf[3][:, 0:tw], ALU.mult),
                    waits=[b2, a1], inc="vp")
                ln2_done.append(ap_t)
            prev_apply = ap_t
            prev_statmm = [ap_t, ap_t]
            self.emit_quant(8)
        x1_stores = []
        for dc in range(DC):
            tk = self.x1st.load("y", lambda e, dc=dc: e.dma_start(
                out=self.ap["x1s"][dc * 128:(dc + 1) * 128, :],
                in_=x1[:, dc * OWN:(dc + 1) * OWN]), extra=[ln2_done[-1]])
            x1_stores.append(tk)
        self.ln2_done = ln2_done
        if self.taps:
            P.sem("tap")
            P.dma("y", lambda e: e.dma_start(out=self.ap["t_ctx"], in_=ctxb[:]),
                  waits=[self.ctx_done[-1]], inc="tap")
            for dc in range(DC):
                P.dma("y", lambda e, dc=dc: e.dma_start(
                    out=self.ap["t_x1"][dc * 128:(dc + 1) * 128, :],
                    in_=x1[:, dc * OWN:(dc + 1) * OWN]),
                    waits=[ln2_done[-1]], inc="tap")
            P.dma("y", lambda e: e.dma_start(out=self.ap["t_ln2"], in_=ln2x[:]),
                  waits=[ln2_done[-1]], inc="tap")
        self.close_arena("Lctx")
        self.tap_x1_gate = ("tap", P.cnt.get("tap", 0)) if self.taps else None
        self.close_arena("L3")

    def ffn(self):
        c, P = self.cfg, self.P
        D, DC, OWN, HFF, NH2 = c["D"], c["DC"], c["OWN"], c["HFF"], c["NH2"]
        NTC = (OWN + 511) // 512
        NST = c["NSTRIP"]
        pq4 = self.pq4
        pf4, pf5 = self.pf[4], self.pf[5]
        ln2x = self.sb["ln2x"]
        orf = self.sb["orf_sb"]
        DBW = min(512, D)
        NDB = D // DBW
        assert NDB * DBW == D
        WB2 = self.qWB2
        NRS = 2 * DC

        hn = self.sbuf("L4", "hn", [128, NH2 * OWN], BF16)
        ring1 = self.sbuf("L4", "ring1", [128, DC * 512], BF16,
                          is_transpose_dest=True)
        ring2 = self.sbuf("L4c", "ring2", [128, DC * 512], BF16,
                          is_transpose_dest=True)

        def rslice(sl, w=512):
            if sl < DC:
                return ring1[:, sl * 512: sl * 512 + w]
            return ring2[:, (sl - DC) * 512: (sl - DC) * 512 + w]
        rsl = [Slot(P, f"rg{i}") for i in range(NRS)]
        P.sem("outst0")
        P.sem("outst1")
        x1full = ("x1st", P.cnt["x1st"])
        gate0 = [x1full, self.ln2_done[-1]]
        if self.taps:
            gate0.append(self.tap_x1_gate)

        hn_last = None
        prev_bank = [None] * 4
        prev_evac = {}
        strip_last_evac = [None, None]
        for sidx in range(NST):
            j2, half = sidx // 2, sidx % 2
            r0 = half * HFF + j2 * 512
            par = sidx % 2
            base = (sidx % 2) * DC
            qts = [self.qtok[("w1h", (r0 + rb * 128) // 128)] for rb in range(4)]
            ltoks = []
            for dc in range(DC):
                sl = base + dc
                lt = rsl[sl].load("y", lambda e, r0=r0, dc=dc, sl=sl:
                                  e.dma_start(
                    out=rslice(sl),
                    in_=self.ap["w1h"][r0:r0 + 512, dc * 128:(dc + 1) * 128],
                    transpose=True),
                    extra=(qts + gate0) if dc == 0 else qts[0:0])
                ltoks.append(lt)
            for hc in range(4):
                hcg = j2 * 4 + hc
                for tc in range(NTC):
                    off = tc * 512
                    tw = min(512, OWN - off)
                    bank = (hc * NTC + tc) % 4
                    mmt = None
                    for dc in range(DC):
                        mmt = P.emit("t", lambda e, base=base, hc=hc, dc=dc,
                                     off=off, tw=tw, bank=bank: e.matmul(
                            pq4[:, bank * 512: bank * 512 + tw],
                            rslice(base + dc)[:, hc * 128: hc * 128 + 128],
                            ln2x[:, dc * OWN + off: dc * OWN + off + tw],
                            start=(dc == 0), stop=(dc == DC - 1)),
                            waits=(ltoks + [prev_bank[bank],
                                    self.ln2_done[-1]] if dc == 0 else []),
                            inc="tp" if dc == DC - 1 else None)
                    if half == 0:
                        evt = P.emit("s", lambda e, hcg=hcg, off=off, tw=tw,
                                     bank=bank: e.activation(
                            out=hn[:, hcg * OWN + off: hcg * OWN + off + tw],
                            in_=pq4[:, bank * 512: bank * 512 + tw],
                            func=AF.Silu),
                            waits=[mmt, prev_evac.get((hcg, tc))], inc="sp")
                    else:
                        evt = P.emit("v", lambda e, hcg=hcg, off=off, tw=tw,
                                     bank=bank: e.tensor_mul(
                            hn[:, hcg * OWN + off: hcg * OWN + off + tw],
                            hn[:, hcg * OWN + off: hcg * OWN + off + tw],
                            pq4[:, bank * 512: bank * 512 + tw]),
                            waits=[mmt, prev_evac.get((hcg, tc))], inc="vp")
                        hn_last = evt
                    prev_bank[bank] = evt
                    prev_evac[(hcg, tc)] = evt
            for dc in range(DC):
                rsl[base + dc].done(prev_bank[(3 * NTC + NTC - 1) % 4])
            self.emit_quant(2)
        self.emit_quant(self.qsched_at("all"))
        if self.taps and self.taps >= 3:
            P.dma("y", lambda e: e.dma_start(out=self.ap["t_hn"], in_=hn[:]),
                  waits=[hn_last], inc="tap")
        self.close_arena("L4c")

        sq0 = self.sbuf("L4b", "sq0", [128, OWN], BF16)
        sq = [sq0, sq0]
        srr = self.sbuf("L4b", "srr", [1, 512], F32)
        rbc = self.sbuf("L4b", "rbc", [128, OWN], F32)
        obuf = [self.sbuf("L4b", f"obuf{i}", [128, 512], F32) for i in (0, 1)]
        x1l = [self.sbuf("L4b", f"x1l{i}", [128, OWN], F32) for i in (0, 1)]
        x1sl = [Slot(P, "x1l0"), Slot(P, "x1l1")]

        prev_sq = [None, None]
        smm = [None, None]
        for hc in range(NH2):
            par = hc % 2
            a1 = P.emit("s", lambda e, hc=hc, par=par: e.activation(
                out=sq[par][:], in_=hn[:, hc * OWN:(hc + 1) * OWN],
                func=AF.Square),
                waits=[hn_last, prev_sq[0], prev_sq[1]], inc="sp")
            for tc in range(NTC):
                off = tc * 512
                tw = min(512, OWN - off)
                pfr = pf4 if tc == 0 else pf5
                mm = P.emit("t", lambda e, par=par, off=off, tw=tw, hc=hc,
                            pfr=pfr: e.matmul(
                    pfr[0:1, 0:tw], self.sb["ones_col_sb"][:],
                    sq[par][:, off:off + tw],
                    start=(hc == 0), stop=(hc == NH2 - 1)),
                    waits=[a1], inc="tp")
                smm[tc] = mm
            prev_sq[0] = prev_sq[1] = smm[NTC - 1]
        rb_done = []
        prev_r = None
        for tc in range(NTC):
            off = tc * 512
            tw = min(512, OWN - off)
            pfr = pf4 if tc == 0 else pf5
            s1 = P.emit("v", lambda e, tw=tw, pfr=pfr: e.tensor_scalar(
                srr[:, 0:tw], pfr[0:1, 0:tw], 1.0 / HFF, RMS_EPS,
                ALU.mult, ALU.add), waits=[smm[tc], prev_r], inc="vp")
            s2 = P.emit("s", lambda e, tw=tw: e.activation(
                out=srr[:, 0:tw], in_=srr[:, 0:tw], func=AF.Ln),
                waits=[s1], inc="sp")
            s3 = P.emit("s", lambda e, tw=tw: e.activation(
                out=srr[:, 0:tw], in_=srr[:, 0:tw], func=AF.Exp, scale=-0.5),
                waits=[s2], inc="sp")
            b1 = P.emit("t", lambda e, tw=tw, tc=tc: e.matmul(
                pq4[:, tc * 512: tc * 512 + tw], orf[:], srr[:, 0:tw],
                start=True, stop=True),
                waits=[s3, s1, prev_bank[tc]], inc="tp")
            cp = P.emit("v", lambda e, off=off, tw=tw, tc=tc: e.tensor_copy(
                out=rbc[:, off:off + tw], in_=pq4[:, tc * 512: tc * 512 + tw]),
                waits=[b1], inc="vp")
            prev_r = cp
            rb_done.append(cp)
        prev_acc = [prev_bank[i % 4] for i in range(4)]
        prev_ob = [None, None]
        tile_ctr = 0
        DB2 = min(256, D)
        ND2 = D // DB2
        nds = DB2 // 128
        for db in range(ND2):
            r0 = db * DB2
            txs = {}
            for ds in range(nds):
                dci = db * nds + ds
                gext = ([x1full, hn_last] + prev_bank) if db == 0 else [x1full]
                txs[ds] = x1sl[dci % 2].load(
                    "y", lambda e, dci=dci: e.dma_start(
                        out=x1l[dci % 2][:],
                        in_=self.ap["x1s"][dci * 128:(dci + 1) * 128, :]),
                    extra=gext)
            last_mm = [None] * 4
            for hc in range(NH2):
                sl = tile_ctr % DC
                tile_ctr += 1
                qks = [self.qtok[("w2h", rb, (hc * 128) // WB2)]
                       for rb in range(r0 // 128, (r0 + DB2) // 128)]
                lt = rsl[sl].load("y", lambda e, r0=r0, hc=hc, sl=sl, DB2=DB2:
                                  e.dma_start(
                    out=rslice(sl, DB2),
                    in_=self.ap["w2h"][r0:r0 + DB2, hc * 128:(hc + 1) * 128],
                    transpose=True), extra=qks)
                for ds in range(nds):
                    for tc in range(NTC):
                        off = tc * 512
                        tw = min(512, OWN - off)
                        bank = ds * NTC + tc
                        mm = P.emit("t", lambda e, sl=sl, ds=ds, hc=hc,
                                    off=off, tw=tw, bank=bank: e.matmul(
                            pq4[:, bank * 512: bank * 512 + tw],
                            rslice(sl)[:, ds * 128: ds * 128 + 128],
                            hn[:, hc * OWN + off: hc * OWN + off + tw],
                            start=(hc == 0), stop=(hc == NH2 - 1)),
                            waits=[lt] + ([prev_acc[bank], hn_last,
                                           rb_done[-1]] if hc == 0 else []),
                            inc="tp" if (hc == NH2 - 1 or
                                         (ds == nds - 1 and tc == NTC - 1))
                                else None)
                        last_mm[bank] = mm
                rsl[sl].done(last_mm[(nds - 1) * NTC + NTC - 1])
            for ds in range(nds):
                dci = db * nds + ds
                for tc in range(NTC):
                    off = tc * 512
                    tw = min(512, OWN - off)
                    bank = ds * NTC + tc
                    opar = (dci * NTC + tc) % 2
                    e1 = P.emit("v", lambda e, bank=bank, off=off, tw=tw,
                                opar=opar: e.tensor_tensor(
                        obuf[opar][:, 0:tw], pq4[:, bank * 512: bank * 512 + tw],
                        rbc[:, off:off + tw], ALU.mult),
                        waits=[last_mm[bank], rb_done[-1], prev_ob[opar]],
                        inc="vp")
                    e2 = P.emit("v", lambda e, dci=dci, off=off, tw=tw,
                                opar=opar: e.tensor_add(
                        obuf[opar][:, 0:tw], obuf[opar][:, 0:tw],
                        x1l[dci % 2][:, off:off + tw]),
                        waits=[txs[ds], e1], inc="vp")
                    ot = P.dma("y", lambda e, dci=dci, off=off, tw=tw,
                               opar=opar: e.dma_start(
                        out=self.ap["out_fm"][dci * 128:(dci + 1) * 128,
                                              off:off + tw],
                        in_=obuf[opar][:, 0:tw]),
                        waits=[e2], inc=f"outst{opar}")
                    prev_ob[opar] = ot
                    prev_acc[bank] = e1
                x1sl[dci % 2].done(prev_ob[(dci * NTC + NTC - 1) % 2])
        final_waits = [("outst0", P.cnt["outst0"]),
                       ("outst1", P.cnt["outst1"])]
        if self.taps:
            final_waits.append(("tap", P.cnt["tap"]))
        P.emit("y", lambda e: e.nop(), waits=final_waits)

def build(cfg, taps=False, stop_after=None):
    return Kern(cfg, taps=taps, stop_after=stop_after).build()



def kernel(L0_attn_hidden_states, L0_attn_Wq, L0_attn_Wk, L0_attn_Wv,
           L0_attn_Wo, L0_attn_bo, L0_ffn_w_gate_up, L0_ffn_w_proj):
    from concourse.bass_utils import run_bass_kernel_spmd
    cfg = REAL
    ins = dict(
        x=np.asarray(L0_attn_hidden_states, dtype=np.float32),
        Wq=np.asarray(L0_attn_Wq, dtype=np.float32),
        Wk=np.asarray(L0_attn_Wk, dtype=np.float32),
        Wv=np.asarray(L0_attn_Wv, dtype=np.float32),
        Wo=np.asarray(L0_attn_Wo, dtype=np.float32),
        bo=np.asarray(L0_attn_bo, dtype=np.float32),
        w_gate_up=np.asarray(L0_ffn_w_gate_up, dtype=np.float32),
        w_proj=np.asarray(L0_ffn_w_proj, dtype=np.float32),
    )
    in_maps = host_inputs(cfg, ins)
    nc = build(cfg, taps=False)
    res = run_bass_kernel_spmd(nc, in_maps, core_ids=list(range(8)))
    return assemble(cfg, res.results)


# version 04
# speedup vs baseline: 1.6690x; 1.0542x over previous
import numpy as np
import ml_dtypes
from contextlib import ExitStack

import concourse.bass as bass
import concourse.mybir as mybir

F32 = mybir.dt.float32
BF16 = mybir.dt.bfloat16
AF = mybir.ActivationFunctionType
ALU = mybir.AluOpType
AX = mybir.AxisListType
BF = ml_dtypes.bfloat16

MAGIC = float(1.5 * 2 ** 23)
CQ = 0.501953125
SOFTCAP = 30.0
ROPE_THETA = 10000.0
LN_EPS = 1e-5
RMS_EPS = float(np.finfo(np.float32).eps)
NEG = -1e30

REAL = dict(T=4096, D=2048, H=16, G=4, HFF=8192)
MINI = dict(T=1024, D=256, H=4, G=2, HFF=512)
MIDI2 = dict(T=1024, D=1536, H=4, G=2, HFF=512, NRING_CAP=2)
MIDI4 = dict(T=4096, D=256, H=2, G=1, HFF=512)


def derived(cfg):
    d = dict(cfg)
    d["DC"] = cfg["D"] // 128
    d["NQT"] = cfg["T"] // 128
    d["NSLOT"] = d["NQT"] // 4
    d["NA_SLOTS"] = d["NSLOT"] // 2
    d["NB_SLOTS"] = d["NSLOT"] - d["NA_SLOTS"]
    d["QWA"] = 128 * d["NA_SLOTS"]
    d["QWB"] = 128 * d["NB_SLOTS"]
    d["NTA"] = 4 * d["NA_SLOTS"]
    d["NTB"] = 4 * d["NSLOT"]
    d["OWN"] = 128 * d["NSLOT"]
    d["HB"] = cfg["H"] // 2
    d["W1R"] = 2 * cfg["HFF"]
    d["NG1"] = cfg["D"] // 64
    d["NG2"] = cfg["HFF"] // 64
    d["NSTRIP"] = d["W1R"] // 512
    d["NH2"] = cfg["HFF"] // 128
    d["REP"] = cfg["H"] // cfg["G"]
    return d


def own_qtiles(cfg, lane):
    d = derived(cfg)
    return [lane + 4 * k for k in range(d["NSLOT"])]


def qcols(cfg, lane, qc):
    d = derived(cfg)
    slots = (range(0, d["NA_SLOTS"]) if qc == 0
             else range(d["NA_SLOTS"], d["NSLOT"]))
    cols = []
    for k in slots:
        t = lane + 4 * k
        cols.extend(range(t * 128, t * 128 + 128))
    return np.array(cols, dtype=np.int64)



def host_inputs(cfg, ins):
    d = derived(cfg)
    T, D, H, G, HFF = cfg["T"], cfg["D"], cfg["H"], cfg["G"], cfg["HFF"]
    x = np.ascontiguousarray(np.asarray(ins["x"], dtype=np.float32))
    assert x.shape == (2, T, D)

    inv = (1.0 / ROPE_THETA ** (np.arange(0, 128, 2, dtype=np.float64) / 128))
    pos = np.arange(T, dtype=np.float64)
    ang = pos[None, :] * np.repeat(inv, 2)[:, None]
    cos_fm = np.cos(ang).astype(np.float32)
    sin_fm = np.sin(ang).astype(np.float32)

    ident = np.eye(128, dtype=BF)
    ident_f = np.eye(128, dtype=np.float32)
    ones_col = np.ones((128, 1), dtype=BF)
    ones_row_f = np.ones((1, 128), dtype=np.float32)
    PT = np.zeros((128, 128), dtype=np.float32)
    for i in range(64):
        PT[2 * i, 2 * i + 1] = 1.0
        PT[2 * i + 1, 2 * i] = -1.0
    PT = PT.astype(BF)

    CSv = 1.0 / (np.float64(30.0) * np.sqrt(np.float64(128.0)))
    slopes = [np.float64(np.float32(2.0) ** np.float32(-(h + 1) * 8.0 / H))
              for h in range(H)]
    NBB = d["NTB"] // 4
    abias = np.zeros((128, H * NBB), dtype=np.float32)
    for h in range(H):
        for b_ in range(NBB):
            abias[:, h * NBB + b_] = np.float32(CSv * slopes[h] * (b_ * 512.0))

    bo_fm = np.ascontiguousarray(
        np.asarray(ins["bo"], dtype=np.float32).reshape(d["DC"], 128).T)

    in_maps = []
    for c in range(8):
        b, lane = c // 4, c % 4
        qa, qb = qcols(cfg, lane, 0), qcols(cfg, lane, 1)
        qall = np.concatenate([qa, qb])
        p = np.arange(128, dtype=np.int64)
        MA = np.zeros((d["NTA"], 128, d["QWA"]), dtype=BF)
        for t in range(d["NTA"]):
            MA[t] = ((t * 128 + p)[:, None] <= qa[None, :]).astype(BF)
        NMB = d["NTB"] - d["NTA"]
        MB = np.zeros((NMB, 128, d["QWB"]), dtype=BF)
        for t in range(NMB):
            MB[t] = (((t + d["NTA"]) * 128 + p)[:, None] <= qb[None, :]).astype(BF)
        DQB_A = np.zeros((128, 4 * d["QWA"]), dtype=np.float32)
        DQB_B = np.zeros((128, 4 * d["QWB"]), dtype=np.float32)
        for i in range(4):
            DQB_A[:, i * d["QWA"]:(i + 1) * d["QWA"]] = \
                (p[:, None] - qa[None, :] + 128 * i)
            DQB_B[:, i * d["QWB"]:(i + 1) * d["QWB"]] = \
                (p[:, None] - qb[None, :] + 128 * i)
        im = dict(
            x_all=x[b],
            wq=np.asarray(ins["Wq"], dtype=np.float32),
            wk=np.asarray(ins["Wk"], dtype=np.float32),
            wv=np.asarray(ins["Wv"], dtype=np.float32),
            wo=np.asarray(ins["Wo"], dtype=np.float32),
            bo_fm=bo_fm,
            w1=np.asarray(ins["w_gate_up"], dtype=np.float32),
            w2=np.asarray(ins["w_proj"], dtype=np.float32),
            cos_fm=cos_fm, sin_fm=sin_fm,
            cos_own=np.ascontiguousarray(cos_fm[:, qall]),
            sin_own=np.ascontiguousarray(sin_fm[:, qall]),
            ident=ident, ident_f=ident_f, ones_col=ones_col,
            ones_row_f=ones_row_f, rot_pt=PT,
            x_own=np.ascontiguousarray(x[b][qall, :]),
            abias=abias,
            ma=MA.reshape(d["NTA"] * 128, d["QWA"]),
            mb=MB.reshape(NMB * 128, d["QWB"]),
            dqb_a=DQB_A, dqb_b=DQB_B,
        )
        in_maps.append(im)
    return in_maps


def assemble(cfg, results):
    d = derived(cfg)
    T, D = cfg["T"], cfg["D"]
    out = np.empty((2, T, D), dtype=np.float32)
    for c in range(8):
        b, lane = c // 4, c % 4
        qall = np.concatenate([qcols(cfg, lane, 0), qcols(cfg, lane, 1)])
        out[b, qall, :] = results[c]["out_fm"].T
    return out



class PB:

    def __init__(self, nc, ctx):
        self.nc = nc
        self.ctx = ctx
        self.q = {k: [] for k in "tsvgy"}
        self.semh = {}
        self.cnt = {}

    def sem(self, name):
        if name not in self.semh:
            self.semh[name] = self.ctx.enter_context(self.nc.semaphore(name))
            self.cnt[name] = 0
        return name

    def emit(self, e, fn, waits=(), inc=None, k=1):
        ws = tuple(t for t in waits if t is not None and t[1] > 0)
        tok = None
        if inc is not None:
            self.sem(inc)
            self.cnt[inc] += k
            tok = (inc, self.cnt[inc])
        self.q[e].append((fn, ws, inc, k))
        return tok

    def dma(self, e, fn, waits=(), inc=None):
        return self.emit(e, fn, waits, inc, 16)

    def replay(self, block):
        semh = self.semh

        def runner(e):
            ops = self.q[e]

            def body(eng):
                clock = {}
                for fn, ws, inc, k in ops:
                    for s, v in ws:
                        if v > clock.get(s, 0):
                            eng.wait_ge(semh[s], v)
                            clock[s] = v
                    inst = fn(eng)
                    if inc is not None:
                        inst.then_inc(semh[inc], k)
            return body

        block.tensor(runner("t"))
        block.scalar(runner("s"))
        block.vector(runner("v"))
        block.gpsimd(runner("g"))
        block.sync(runner("y"))


class Slot:

    def __init__(self, P, name):
        self.P = P
        self.name = P.sem(name)
        self.consumers = []

    def load(self, e, fn, extra=()):
        waits = tuple(self.consumers) + tuple(extra)
        self.consumers = []
        return self.P.dma(e, fn, waits, inc=self.name)

    def done(self, tok):
        if tok is not None:
            self.consumers.append(tok)


class Kern:
    def __init__(self, cfg, taps=False, stop_after=None):
        self.cfg = derived(cfg)
        self.taps = taps
        self.stop_after = stop_after
        self.nc = bass.Bass("TRN2", target_bir_lowering=False)
        self.top = ExitStack()
        self.P = PB(self.nc, self.top)
        self.sb = {}
        self.stacks = {}

    def arena(self, key):
        if key not in self.stacks:
            self.stacks[key] = ExitStack()
        return self.stacks[key]

    def close_arena(self, key):
        if key in self.stacks:
            self.stacks.pop(key).close()

    ARENA_SIDE = {"P": "left", "Q": "left", "Lkv": "left", "Lqf": "left",
                  "L3b": "left", "L3": "left", "L4": "left",
                  "L1": "right", "Lro": "right", "L1q": "right",
                  "Lctx": "right", "L2": "right", "L4b": "left", "L4c": "left"}

    def sbuf(self, key, name, shape, dtype, **kw):
        h = self.arena(key).enter_context(
            self.nc.sbuf_tensor(name, shape, dtype,
                                side=self.ARENA_SIDE[key], **kw))
        self.sb[name] = h
        return h

    def psum(self, name, shape, dtype):
        h = self.top.enter_context(self.nc.psum_tensor(name, shape, dtype))
        self.sb[name] = h
        return h

    def build(self):
        c = self.cfg
        nc = self.nc
        T, D, DC, OWN = c["T"], c["D"], c["DC"], c["OWN"]
        H, G, HFF = c["H"], c["G"], c["HFF"]

        ap = {}
        EI, EO = "ExternalInput", "ExternalOutput"
        for n_, sh, dt_, kd in [
            ("x_all", [T, D], F32, EI), ("x_own", [OWN, D], F32, EI),
            ("wq", [D, H * 128], F32, EI), ("wk", [D, G * 128], F32, EI),
            ("wv", [D, G * 128], F32, EI), ("wo", [H * 128, D], F32, EI),
            ("bo_fm", [128, DC], F32, EI),
            ("w1", [c["W1R"], D], F32, EI), ("w2", [D, HFF], F32, EI),
            ("cos_fm", [128, T], F32, EI), ("sin_fm", [128, T], F32, EI),
            ("cos_own", [128, OWN], F32, EI), ("sin_own", [128, OWN], F32, EI),
            ("ident", [128, 128], BF16, EI), ("ident_f", [128, 128], F32, EI),
            ("ones_col", [128, 1], BF16, EI), ("ones_row_f", [1, 128], F32, EI),
            ("rot_pt", [128, 128], BF16, EI),
            ("abias", [128, c["H"] * (c["NTB"] // 4)], F32, EI),
            ("dqb_a", [128, 4 * c["QWA"]], F32, EI),
            ("dqb_b", [128, 4 * c["QWB"]], F32, EI),
            ("ma", [c["NTA"] * 128, c["QWA"]], BF16, EI),
            ("mb", [(c["NTB"] - c["NTA"]) * 128, c["QWB"]], BF16, EI),
            ("out_fm", [D, OWN], F32, EO),
            ("w1h", [c["W1R"], D], BF16, "Internal"),
            ("w2h", [D, HFF], BF16, "Internal"),
            ("x1s", [D, OWN], F32, "Internal"),
        ]:
            ap[n_] = nc.dram_tensor(n_, sh, dt_, kind=kd).ap()
        if self.taps:
            for n_, sh, dt_ in [
                ("t_k", [128, G * T], BF16), ("t_v", [128, (T // 128) * G * 128], BF16),
                ("t_q", [128, H * OWN], BF16), ("t_ctx", [128, H * OWN], BF16),
                ("t_x1", [D, OWN], F32), ("t_ln2", [128, DC * OWN], BF16),
                ("t_w1h", [c["W1R"], D], BF16), ("t_w2h", [D, HFF], BF16),
                ("t_hn", [128, c["NH2"] * OWN], BF16),
            ]:
                ap[n_] = nc.dram_tensor(n_, sh, dt_, kind=EO).ap()
        self.ap = ap

        pq4 = self.psum("pq4", [128, 2048], F32)
        p4 = self.psum("pf4t", [128, 512], F32)
        p5 = self.psum("pf5t", [128, 512], F32)
        self.psum("pbA", [128, 1024], BF16)
        self.psum("pbB", [128, 1024], BF16)
        self.pf = [pq4[:, 0:512], pq4[:, 512:1024], pq4[:, 1024:1536],
                   pq4[:, 1536:2048], p4[:], p5[:]]
        self.pq4 = pq4

        self.sbuf("P", "ident_sb", [128, 128], BF16)
        self.sbuf("P", "ident_f_sb", [128, 128], F32)
        self.sbuf("P", "ones_col_sb", [128, 1], BF16)
        self.sbuf("P", "orf_sb", [1, 128], F32)
        self.sbuf("P", "rot_pt_sb", [128, 128], BF16)
        self.sbuf("P", "bo_fm_sb", [128, DC], F32)
        P = self.P
        P.sem("cl")
        for src, dst in [("ident", "ident_sb"), ("ident_f", "ident_f_sb"),
                         ("ones_col", "ones_col_sb"), ("ones_row_f", "orf_sb"),
                         ("rot_pt", "rot_pt_sb"), ("bo_fm", "bo_fm_sb")]:
            P.dma("y", (lambda e, s=src, d=dst: e.dma_start(
                out=self.sb[d][:], in_=self.ap[s])), inc="cl")
        self.cl = ("cl", P.cnt["cl"])

        self.quant_init()
        self.sbuf("Lkv", "kfm", [128, G * T], BF16)
        self.sbuf("Lkv", "vtm", [128, (T // 128) * G * 128], BF16)
        done = False
        for name, fn in [("p1a", self.phase1a), ("p1b", self.phase1b),
                         ("attn", self.attention), ("wo", self.wo_ln2),
                         ("ffn", self.ffn)]:
            fn()
            if self.stop_after == name:
                done = True
                break
        if done and self.stop_after not in ("ffn", "f1", "f2"):
            self.finish_early()

        block = self.top.enter_context(self.nc.Block())
        P.replay(block)
        for k in reversed(list(self.stacks)):
            self.close_arena(k)
        self.top.close()
        return self.nc

    def finish_early(self):
        P = self.P
        assert self.taps, "finish_early requires taps"
        P.sem("tap")
        if self.stop_after in ("p1a", "p1b") and "kfm" in self.sb:
            P.dma("y", lambda e: e.dma_start(out=self.ap["t_k"],
                                             in_=self.sb["kfm"][:]),
                  waits=[self.kv_done[-1]], inc="tap")
            P.dma("y", lambda e: e.dma_start(out=self.ap["t_v"],
                                             in_=self.sb["vtm"][:]),
                  waits=[self.kv_done[-1]], inc="tap")
        if self.stop_after == "p1b":
            P.dma("y", lambda e: e.dma_start(out=self.ap["t_q"],
                                             in_=self.sb["qfm"][:]),
                  waits=[self.q_done[-1]], inc="tap")
        if self.stop_after == "attn":
            P.dma("y", lambda e: e.dma_start(out=self.ap["t_ctx"],
                                             in_=self.sb["ctxb"][:]),
                  waits=[self.ctx_done[-1]], inc="tap")
        fw = [("tap", P.cnt["tap"])]
        for s in ("qst0", "qst1"):
            if s in P.cnt:
                fw.append((s, P.cnt[s]))
        P.emit("y", lambda e: e.nop(), waits=fw)

    def quant_init(self):
        c = self.cfg
        D, HFF, W1R = c["D"], c["HFF"], c["W1R"]
        WB2 = min(2048, HFF)
        W = max(D, WB2)
        self.qW, self.qWB2 = W, WB2
        ng = W // 64
        qw0 = self.sbuf("Q", "qw0", [128, W], BF16)
        qu0 = self.sbuf("Q", "qu0", [128, W], BF16)
        self.qw = [qw0, qw0]
        self.qu = [qu0, qu0]
        self.qh = [qw0, qw0]
        qs0 = self.sbuf("Q", "qsums0", [128, ng], F32)
        qb0 = self.sbuf("Q", "qscb0", [128, ng], BF16)
        qr0 = self.sbuf("Q", "qrcp0", [128, ng], F32)
        self.qsums = [qs0, qs0]
        self.qscb = [qb0, qb0]
        self.qrcp = [qr0, qr0]
        _sl = Slot(self.P, "qwl0")
        _st = Slot(self.P, "qst0")
        self.qwsl = [_sl, _sl]
        self.qssl = [_st, _st]
        sched = []
        for j in range(c["NSTRIP"] // 2):
            for half in (0, HFF):
                for rb in range(4):
                    r = half + j * 512 + rb * 128
                    sched.append(dict(src="w1", dst="w1h", key=("w1h", r // 128),
                                      r=r, cc=0, w=D))
        for r in range(0, D, 128):
            for cc in range(0, HFF, WB2):
                sched.append(dict(src="w2", dst="w2h",
                                  key=("w2h", r // 128, cc // WB2),
                                  r=r, cc=cc, w=WB2))
        self.qsched = sched
        self.qpos = 0
        self.qgps_left = max(8, min(64, len(sched) // 3))
        self.qquota = {"p1b": 8, "attn": 32, "wo": 16}
        self.qtok = {}
        self.qprev = dict(u=[[], []], small=[[], []], qh=[[], []])
        self.qidx = 0

    def qsched_at(self, stage):
        if stage == "all":
            return len(self.qsched) - self.qpos
        q = self.qquota.get(stage, 0)
        if q > 0:
            self.qquota[stage] = q - 1
            return 1
        return 0

    def emit_quant(self, n, eng=None):
        P = self.P
        for _ in range(n):
            if self.qpos >= len(self.qsched):
                return
            blk = self.qsched[self.qpos]
            self.qpos += 1
            i = self.qidx
            self.qidx += 1
            par = i % 2
            E = eng or "v"
            w, ng = blk["w"], blk["w"] // 64
            r, cc = blk["r"], blk["cc"]
            src, dst = self.ap[blk["src"]], self.ap[blk["dst"]]
            qw, qu, qh = self.qw[par], self.qu[par], self.qh[par]
            qsums, qscb, qrcp = self.qsums[par], self.qscb[par], self.qrcp[par]
            ld = self.qwsl[par].load("g", lambda e, qw=qw, src=src, r=r, cc=cc,
                                     w=w: e.dma_start(
                out=qw[:, 0:w], in_=src[r:r + 128, cc:cc + w]))
            e_s = "vp" if E == "v" else "gp"
            red = P.emit("v", lambda e, qw=qw, qsums=qsums, w=w, ng=ng: e.reduce_sum(
                qsums[:, 0:ng].unsqueeze(2),
                qw[:, 0:w].rearrange("p (g k) -> p g k", k=64), axis=AX.X,
                apply_absolute_value=True),
                waits=[ld] + self.qprev["small"][par], inc="vp")
            sc1 = P.emit(E, lambda e, qscb=qscb, qsums=qsums, ng=ng: e.tensor_scalar(
                qscb[:, 0:ng], qsums[:, 0:ng], 1.0 / 64, None, ALU.mult),
                waits=[red], inc=e_s)
            sc2 = P.emit(E, lambda e, qscb=qscb, ng=ng: e.tensor_scalar_max(
                qscb[:, 0:ng], qscb[:, 0:ng], float(np.float32(BF(1e-8)))),
                waits=[sc1], inc=e_s)
            rc = P.emit("v", lambda e, qrcp=qrcp, qscb=qscb, ng=ng: e.reciprocal(
                qrcp[:, 0:ng], qscb[:, 0:ng]), waits=[sc2], inc="vp")
            u = P.emit(E, lambda e, qu=qu, qw=qw, qrcp=qrcp, w=w, ng=ng:
                       e.tensor_tensor(
                qu[:, 0:w].rearrange("p (g k) -> p g k", k=64),
                qw[:, 0:w].rearrange("p (g k) -> p g k", k=64),
                qrcp[:, 0:ng].unsqueeze(2).broadcast_to([128, ng, 64]),
                ALU.mult), waits=[rc] + self.qprev["u"][par], inc=e_s)
            self.qwsl[par].done(u)
            cl_ = P.emit(E, lambda e, qu=qu, w=w: e.tensor_scalar(
                qu[:, 0:w], qu[:, 0:w], 1.0, -1.0, ALU.min, ALU.max),
                waits=[u], inc=e_s)
            mg = P.emit(E, lambda e, qu=qu, w=w: e.tensor_scalar(
                qu[:, 0:w], qu[:, 0:w], MAGIC, -MAGIC, ALU.add, ALU.add),
                waits=[cl_], inc=e_s)
            wh = P.emit(E, lambda e, qh=qh, qu=qu, qscb=qscb, w=w, ng=ng:
                        e.tensor_tensor(
                qh[:, 0:w].rearrange("p (g k) -> p g k", k=64),
                qu[:, 0:w].rearrange("p (g k) -> p g k", k=64),
                qscb[:, 0:ng].unsqueeze(2).broadcast_to([128, ng, 64]),
                ALU.mult), waits=[mg] + self.qprev["qh"][par], inc=e_s)
            self.qprev["u"][0] = self.qprev["u"][1] = [wh]
            self.qprev["small"][0] = self.qprev["small"][1] = [u, wh]
            stok = self.qssl[par].load("y", lambda e, qh=qh, dst=dst, r=r, cc=cc,
                                       w=w: e.dma_start(
                out=dst[r:r + 128, cc:cc + w], in_=qh[:, 0:w]), extra=[wh])
            self.qssl[par].done(stok)
            self.qwsl[par].done(stok)
            self.qprev["qh"][0] = self.qprev["qh"][1] = [stok]
            self.qtok[blk["key"]] = stok

    def ln_tile(self, x_ap, out_ap, jk, st, sc, t, NT, extra, prev):
        P, D = self.P, self.cfg["D"]
        a1 = P.emit("s", lambda e: e.activation(
            out=jk[0][:], in_=x_ap, func=AF.Copy,
            accum_out=st[:, t:t + 1]),
            waits=[prev["a1"]] + extra, inc="sp")
        a2 = P.emit("s", lambda e: e.activation(
            out=jk[1][:], in_=x_ap, func=AF.Square,
            accum_out=st[:, NT + t:NT + t + 1]),
            waits=[prev["a2"], a1], inc="sp")
        prev["a1"], prev["a2"] = a1, a2
        b = 8 * (t % 2)
        pv = prev["sc"][t % 2]
        s1 = P.emit("v", lambda e: e.tensor_scalar(
            sc[:, b + 2:b + 3], st[:, t:t + 1], 1.0 / D, None, ALU.mult),
            waits=[a1, pv], inc="vp")
        s2 = P.emit("v", lambda e: e.tensor_mul(
            sc[:, b + 3:b + 4], sc[:, b + 2:b + 3], sc[:, b + 2:b + 3]),
            waits=[s1], inc="vp")
        s3 = P.emit("v", lambda e: e.tensor_scalar(
            sc[:, b + 4:b + 5], st[:, NT + t:NT + t + 1], 1.0 / D, LN_EPS,
            ALU.mult, ALU.add), waits=[a2], inc="vp")
        s4 = P.emit("v", lambda e: e.tensor_sub(
            sc[:, b + 4:b + 5], sc[:, b + 4:b + 5], sc[:, b + 3:b + 4]),
            waits=[s2, s3], inc="vp")
        s5 = P.emit("s", lambda e: e.activation(
            out=sc[:, b + 5:b + 6], in_=sc[:, b + 4:b + 5], func=AF.Ln),
            waits=[s4], inc="sp")
        s6 = P.emit("s", lambda e: e.activation(
            out=sc[:, b + 6:b + 7], in_=sc[:, b + 5:b + 6], func=AF.Exp,
            scale=-0.5), waits=[s5], inc="sp")
        s7 = P.emit("v", lambda e: e.scalar_tensor_tensor(
            sc[:, b + 7:b + 8], sc[:, b + 2:b + 3], -1.0,
            sc[:, b + 6:b + 7], ALU.mult, ALU.mult),
            waits=[s6, s2], inc="vp")
        prev["sc"][t % 2] = s7
        ap_ = P.emit("s", lambda e: e.activation(
            out=out_ap, in_=x_ap, func=AF.Identity,
            scale=sc[:, b + 6:b + 7], bias=sc[:, b + 7:b + 8]),
            waits=[s6, s7, prev["apply"]], inc="sp")
        prev["apply"] = ap_
        return ap_

    def phase1a(self):
        c, P = self.cfg, self.P
        T, D, DC, G = c["T"], c["D"], c["DC"], c["G"]
        NT, NCH = T // 128, T // 512
        GW = G * 128

        xt = [self.sbuf("L1", f"xt{i}", [128, D], F32) for i in (0, 1)]
        ln1tm = [self.sbuf("L1", f"ln1tm{i}", [128, D], BF16) for i in (0, 1)]
        lnfm = [self.sbuf("L1", f"lnfm{i}", [128, DC * 512], BF16) for i in (0, 1)]
        csb = [self.sbuf("L1", f"csb{i}", [128, 512], F32) for i in (0, 1)]
        ssb = [self.sbuf("L1", f"ssb{i}", [128, 512], F32) for i in (0, 1)]
        jk = [self.sbuf("L1", f"jk{i}", [128, D], BF16) for i in (0, 1)]
        st = self.sbuf("L1", "st", [128, 2 * NT], F32)
        sc = self.sbuf("L1", "sc", [128, 16], F32)
        kraw = [self.sbuf("L1", f"kraw{i}", [128, 512], BF16) for i in (0, 1)]
        rt1 = self.sbuf("L1", "rt1", [128, 512], F32)
        rt2 = self.sbuf("L1", "rt2", [128, 512], F32)
        wkb = self.sbuf("L1", "wkb", [128, DC * GW], BF16)
        wvb = self.sbuf("L1", "wvb", [128, DC * GW], BF16)

        kfm, vtm = self.sb["kfm"], self.sb["vtm"]
        pf = self.pf
        pbA = self.sb["pbA"]

        P.sem("wkv")
        P.dma("g", lambda e: e.dma_start(
            out=wkb[:].rearrange("p (dc n) -> p dc n", n=GW),
            in_=self.ap["wk"].rearrange("(dc p) n -> p dc n", p=128)), inc="wkv")
        P.dma("g", lambda e: e.dma_start(
            out=wvb[:].rearrange("p (dc n) -> p dc n", n=GW),
            in_=self.ap["wv"].rearrange("(dc p) n -> p dc n", p=128)), inc="wkv")
        wkv_tok = ("wkv", 32)

        xsl = [Slot(P, "xls0"), Slot(P, "xls1")]
        cssl = [Slot(P, "csl0"), Slot(P, "csl1")]
        t_ms = P.emit("v", lambda e: e.memset(st[:], 0.0), inc="vp")

        prev = dict(a1=None, a2=None, apply=None, sc=[None, None],
                    lnfm=[[], []], kraw=[[], []], rt=None, pf=[None] * 6, pbA=[])
        kv_done = []

        for ch in range(NCH):
            par = ch % 2
            tcs = cssl[par].load("y", lambda e, ch=ch, par=par: e.dma_start(
                out=csb[par][:], in_=self.ap["cos_fm"][:, ch * 512:(ch + 1) * 512]))
            tss = cssl[par].load("y", lambda e, ch=ch, par=par: e.dma_start(
                out=ssb[par][:], in_=self.ap["sin_fm"][:, ch * 512:(ch + 1) * 512]))
            lnfm_rdy = []
            for tl in range(4):
                t = ch * 4 + tl
                tp2 = t % 2
                tlx = xsl[tp2].load("g", lambda e, t=t, tp2=tp2: e.dma_start(
                    out=xt[tp2][:], in_=self.ap["x_all"][t * 128:(t + 1) * 128, :]))
                ap_ = self.ln_tile(xt[tp2][:], ln1tm[tp2][:], jk, st, sc, t, NT,
                                   [tlx, t_ms], prev)
                xsl[tp2].done(ap_)
                nb = (DC + 7) // 8
                for j in range(nb):
                    dcs = list(range(8 * j, min(8 * j + 8, DC)))
                    n8 = len(dcs)
                    tt = None
                    for ii, dc in enumerate(dcs):
                        tt = P.emit("t", lambda e, tp2=tp2, dc=dc, ii=ii: e.transpose(
                            pbA[:, ii * 128:(ii + 1) * 128],
                            ln1tm[tp2][:, dc * 128:(dc + 1) * 128],
                            self.sb["ident_sb"][:]),
                            waits=([ap_, self.cl] + prev["pbA"]) if ii == 0 else [],
                            inc="tp" if ii == n8 - 1 else None)
                    ev = P.emit("v", lambda e, par=par, tl=tl, j=j, n8=n8:
                                e.tensor_copy(
                        out=lnfm[par][:].rearrange("p (dc n) -> p dc n", n=512)
                            [:, 8 * j:8 * j + n8, tl * 128:(tl + 1) * 128],
                        in_=pbA[:].rearrange("p (k n) -> p k n", n=128)[:, 0:n8, :]),
                        waits=[tt] + prev["lnfm"][par], inc="vp")
                    lnfm_rdy.append(ev)
                    prev["pbA"] = [ev]
                self.emit_quant(1)
            prev["lnfm"][par] = []
            for g in range(G):
                bank = g % 2
                mmt = None
                for dc in range(DC):
                    mmt = P.emit("t", lambda e, g=g, dc=dc, par=par, bank=bank:
                                 e.matmul(
                        pf[bank][:, 0:512],
                        wkb[:, dc * GW + g * 128: dc * GW + g * 128 + 128],
                        lnfm[par][:, dc * 512:(dc + 1) * 512],
                        start=(dc == 0), stop=(dc == DC - 1)),
                        waits=([wkv_tok, prev["pf"][bank]] + lnfm_rdy)
                              if dc == 0 else [],
                        inc="tp" if dc == DC - 1 else None)
                kb = g % 2
                e1 = P.emit("s", lambda e, bank=bank, kb=kb: e.activation(
                    out=kraw[kb][:], in_=pf[bank][:, 0:512], func=AF.Copy),
                    waits=[mmt] + prev["kraw"][kb], inc="sp")
                prev["pf"][bank] = e1
                rmt = P.emit("t", lambda e, kb=kb: e.matmul(
                    pf[2][:, 0:512], self.sb["rot_pt_sb"][:], kraw[kb][:],
                    start=True, stop=True),
                    waits=[e1, prev["pf"][2], self.cl], inc="tp")
                d1 = P.emit("v", lambda e, par=par, kb=kb: e.tensor_mul(
                    rt1[:], kraw[kb][:], csb[par][:]),
                    waits=[e1, tss, prev["rt"]], inc="vp")
                d2 = P.emit("v", lambda e, par=par: e.tensor_mul(
                    rt2[:], pf[2][:, 0:512], ssb[par][:]),
                    waits=[rmt, tss, prev["rt"]], inc="vp")
                prev["pf"][2] = d2
                d3 = P.emit("v", lambda e, g=g, ch=ch: e.tensor_add(
                    kfm[:, g * T + ch * 512: g * T + ch * 512 + 512],
                    rt1[:], rt2[:]), waits=[d1, d2], inc="vp")
                prev["rt"] = d3
                prev["kraw"][kb] = [rmt, d1]
                kv_done.append(d3)
            cssl[par].done(kv_done[-1])
            last_v = None
            for tl in range(4):
                t = ch * 4 + tl
                mmt = None
                for dc in range(DC):
                    mmt = P.emit("t", lambda e, tl=tl, dc=dc, par=par: e.matmul(
                        pf[4][:, 0:GW],
                        lnfm[par][:, dc * 512 + tl * 128: dc * 512 + tl * 128 + 128],
                        wvb[:, dc * GW:(dc + 1) * GW],
                        start=(dc == 0), stop=(dc == DC - 1)),
                        waits=[wkv_tok, prev["pf"][4]] if dc == 0 else [],
                        inc="tp" if dc == DC - 1 else None)
                ev = P.emit("s", lambda e, t=t: e.activation(
                    out=vtm[:, t * GW:(t + 1) * GW], in_=pf[4][:, 0:GW],
                    func=AF.Copy), waits=[mmt], inc="sp")
                prev["pf"][4] = ev
                kv_done.append(ev)
                last_v = ev
            prev["lnfm"][par] = [last_v]
        self.kv_done = kv_done
        self.close_arena("L1")

    def phase1b(self):
        c, P = self.cfg, self.P
        D, DC, OWN, H, T = c["D"], c["DC"], c["OWN"], c["H"], c["T"]
        NSLOT = c["NSLOT"]
        NTC = (OWN + 511) // 512
        self.sbuf("Lqf", "qfm", [128, H * OWN], BF16)
        cosq = self.sbuf("Lro", "cosq", [128, OWN], F32)
        sinq = self.sbuf("Lro", "sinq", [128, OWN], F32)
        q_in = self.sbuf("L1q", "q_in", [128, DC * OWN], BF16)
        xo = self.sbuf("L1q", "xoq", [128, D], F32)
        lntm = [self.sbuf("L1q", f"lntm{i}", [128, D], BF16) for i in (0, 1)]
        jkq = self.sbuf("L1q", "jkq0", [128, D], BF16)
        jk = [jkq, jkq]
        st = self.sbuf("L1q", "stq", [128, 2 * NSLOT], F32)
        sc = self.sbuf("L1q", "scq", [128, 16], F32)
        wqh = [self.sbuf("L1q", f"wqh{i}", [128, DC * 128], BF16) for i in (0, 1)]
        qraw = self.sbuf("L1q", "qraw", [128, OWN], BF16)
        rq1 = self.sbuf("L1q", "rq1", [128, 512], F32)
        rq2 = self.sbuf("L1q", "rq2", [128, 512], F32)
        qfm = self.sb["qfm"]
        pf = self.pf
        pbA = self.sb["pbA"]

        P.sem("roL")
        pgate = [self.kv_done[-1]]
        for src, dst in [("cos_own", cosq), ("sin_own", sinq)]:
            P.dma("y", lambda e, s=src, d=dst: e.dma_start(out=d[:], in_=self.ap[s]),
                  waits=pgate, inc="roL")
        ro_tok = ("roL", P.cnt["roL"])

        xsl = Slot(P, "xoq0")
        wsl = [Slot(P, "wql0"), Slot(P, "wql1")]
        t_ms = P.emit("v", lambda e: e.memset(st[:], 0.0), waits=pgate, inc="vp")
        prev = dict(a1=None, a2=None, apply=None, sc=[None, None], pbA=[],
                    pf=[None] * 6, qraw=[], rq=None)
        qin_rdy = []
        for k in range(NSLOT):
            par = k % 2
            tlx = xsl.load("g", lambda e, k=k: e.dma_start(
                out=xo[:], in_=self.ap["x_own"][k * 128:(k + 1) * 128, :]),
                extra=pgate if k == 0 else [])
            ap_ = self.ln_tile(xo[:], lntm[par][:], jk, st, sc, k, NSLOT,
                               [tlx, t_ms], prev)
            xsl.done(ap_)
            nb = (DC + 7) // 8
            for j in range(nb):
                dcs = list(range(8 * j, min(8 * j + 8, DC)))
                n8 = len(dcs)
                tt = None
                for ii, dc in enumerate(dcs):
                    tt = P.emit("t", lambda e, par=par, dc=dc, ii=ii: e.transpose(
                        pbA[:, ii * 128:(ii + 1) * 128],
                        lntm[par][:, dc * 128:(dc + 1) * 128],
                        self.sb["ident_sb"][:]),
                        waits=([ap_, self.cl] + prev["pbA"]) if ii == 0 else [],
                        inc="tp" if ii == n8 - 1 else None)
                ev = P.emit("v", lambda e, k=k, j=j, n8=n8: e.tensor_copy(
                    out=q_in[:].rearrange("p (dc n) -> p dc n", n=OWN)
                        [:, 8 * j:8 * j + n8, k * 128:(k + 1) * 128],
                    in_=pbA[:].rearrange("p (k n) -> p k n", n=128)[:, 0:n8, :]),
                    waits=[tt], inc="vp")
                prev["pbA"] = [ev]
                qin_rdy.append(ev)
        self.q_done = []
        for h in range(H):
            par = h % 2
            tw_ = wsl[par].load("g", lambda e, h=h, par=par: e.dma_start(
                out=wqh[par][:].rearrange("p (dc n) -> p dc n", n=128),
                in_=self.ap["wq"][:, h * 128:(h + 1) * 128]
                    .rearrange("(dc p) n -> p dc n", p=128)),
                extra=pgate if h < 2 else [])
            for tc in range(NTC):
                off = tc * 512
                tww = min(512, OWN - off)
                bank = tc % 2
                mmt = None
                for dc in range(DC):
                    mmt = P.emit("t", lambda e, dc=dc, par=par, bank=bank,
                                 off=off, tww=tww: e.matmul(
                        pf[bank][:, 0:tww],
                        wqh[par][:, dc * 128:(dc + 1) * 128],
                        q_in[:, dc * OWN + off: dc * OWN + off + tww],
                        start=(dc == 0), stop=(dc == DC - 1)),
                        waits=([tw_, prev["pf"][bank]] + qin_rdy)
                              if dc == 0 else [],
                        inc="tp" if dc == DC - 1 else None)
                e1 = P.emit("s", lambda e, bank=bank, off=off, tww=tww: e.activation(
                    out=qraw[:, off:off + tww], in_=pf[bank][:, 0:tww],
                    func=AF.Copy), waits=[mmt] + prev["qraw"], inc="sp")
                prev["pf"][bank] = e1
                rmt = P.emit("t", lambda e, off=off, tww=tww: e.matmul(
                    pf[2][:, 0:tww], self.sb["rot_pt_sb"][:],
                    qraw[:, off:off + tww], start=True, stop=True),
                    waits=[e1, prev["pf"][2]], inc="tp")
                d1 = P.emit("v", lambda e, off=off, tww=tww: e.tensor_mul(
                    rq1[:, 0:tww], qraw[:, off:off + tww],
                    cosq[:, off:off + tww]),
                    waits=[e1, ro_tok, prev["rq"]], inc="vp")
                d2 = P.emit("v", lambda e, off=off, tww=tww: e.tensor_mul(
                    rq2[:, 0:tww], pf[2][:, 0:tww],
                    sinq[:, off:off + tww]), waits=[rmt, prev["rq"]], inc="vp")
                prev["pf"][2] = d2
                d3 = P.emit("v", lambda e, h=h, off=off, tww=tww: e.tensor_add(
                    qfm[:, h * OWN + off: h * OWN + off + tww],
                    rq1[:, 0:tww], rq2[:, 0:tww]),
                    waits=[d1, d2], inc="vp")
                prev["rq"] = d3
                prev["qraw"] = [rmt, d1]
                self.q_done.append(d3)
            wsl[par].done(self.q_done[-1])
            self.emit_quant(1)
        self.close_arena("L1q")
        self.close_arena("Lro")

    def attention(self):
        c, P = self.cfg, self.P
        T, OWN, H, G = c["T"], c["OWN"], c["H"], c["G"]
        REP, GW = c["REP"], c["G"] * 128
        kfm, vtm, qfm = self.sb["kfm"], self.sb["vtm"], self.sb["qfm"]
        ctxb = self.sbuf("Lctx", "ctxb", [128, H * OWN], BF16)
        pq4 = self.pq4
        pf4, pf5 = self.pf[4], self.pf[5]
        CS = 1.0 / (SOFTCAP * float(np.sqrt(128.0)))
        gate = [self.q_done[-1], self.kv_done[-1]]

        NMA = c["NTA"]
        NMB = c["NTB"] - c["NTA"]
        MW = max(NMA * c["QWA"], NMB * c["QWB"])
        mbuf = self.sbuf("L2", "mbuf", [128, MW], BF16)
        u4 = [self.sbuf("L2", f"u4{i}", [128, 2048], F32) for i in (0, 1)]
        p4b = [self.sbuf("L2", f"p4b{i}", [128, 2048], BF16) for i in (0, 1)]
        dsb = self.sbuf("L2", "dsb0", [1, 512], F32)
        abias = self.sbuf("L2", "abias_sb", [128, c["H"] * (c["NTB"] // 4)], F32)
        dqba = self.sbuf("L2", "dqba", [128, 4 * c["QWA"]], F32)
        dqbb = self.sbuf("L2", "dqbb", [128, 4 * c["QWB"]], F32)
        orf = self.sb["orf_sb"]

        self.tap_kv = None
        if self.taps:
            P.sem("tap")
            P.dma("y", lambda e: e.dma_start(out=self.ap["t_k"], in_=kfm[:]),
                  waits=[self.kv_done[-1], self.q_done[-1]], inc="tap")
            P.dma("y", lambda e: e.dma_start(out=self.ap["t_v"], in_=vtm[:]),
                  waits=[self.kv_done[-1]], inc="tap")
            P.dma("y", lambda e: e.dma_start(out=self.ap["t_q"], in_=qfm[:]),
                  waits=[self.q_done[-1]], inc="tap")
            self.tap_kv = ("tap", P.cnt["tap"])
        P.sem("abl")
        P.dma("y", lambda e: e.dma_start(out=abias[:], in_=self.ap["abias"]),
              waits=gate, inc="abl")
        P.dma("y", lambda e: e.dma_start(out=dqba[:], in_=self.ap["dqb_a"]),
              inc="abl")
        P.dma("y", lambda e: e.dma_start(out=dqbb[:], in_=self.ap["dqb_b"]),
              inc="abl")
        ab_tok = ("abl", P.cnt["abl"])

        msl = Slot(P, "mal")
        slopes = [float(np.float32(2.0) ** np.float32(-(h + 1) * 8.0 / H))
                  for h in range(H)]
        NBB = c["NTB"] // 4
        self.ctx_done = []
        prev_stt = None
        prev_ctx = None
        prev_den = None
        prev_dsb = None
        prev_u4 = [None, None]
        prev_p4 = [None, None]
        for qc in (0, 1):
            TW = c["QWA"] if qc == 0 else c["QWB"]
            NTq = c["NTA"] if qc == 0 else c["NTB"]
            NB = NTq // 4
            BW = 4 * TW
            qoff = 0 if qc == 0 else c["QWA"]
            dqb = dqba if qc == 0 else dqbb
            src_m = "ma" if qc == 0 else "mb"
            NSKIP = 0 if qc == 0 else c["NTA"]
            NM = NTq - NSKIP
            m_tok = msl.load("y", lambda e, src_m=src_m, TW=TW, NM=NM:
                             e.dma_start(
                out=mbuf[:, 0:NM * TW].rearrange("p (t f) -> p t f", f=TW),
                in_=self.ap[src_m].rearrange("(t p) f -> p t f", p=128)),
                extra=gate + [ab_tok] if qc == 0 else [])
            last_mask = None
            for h in range(H):
                g = h // REP

                def emit_qk(b, first_wait):
                    mmt = None
                    for i in range(4):
                        s = 4 * b + i
                        mmt = P.emit("t", lambda e, s=s, i=i, h=h, g=g,
                                     qoff=qoff, TW=TW: e.matmul(
                            pq4[:, i * TW:(i + 1) * TW],
                            kfm[:, g * T + s * 128: g * T + s * 128 + 128],
                            qfm[:, h * OWN + qoff: h * OWN + qoff + TW],
                            start=True, stop=True),
                            waits=(first_wait + gate) if i == 0 else [],
                            inc="tp" if i == 3 else None)
                    return mmt
                qk_toks = [None] * (NB + 2)
                qk_toks[0] = emit_qk(0, [prev_stt] if prev_stt else [])

                def emit_stt(b):
                    par = b % 2
                    return P.emit("v", lambda e, h=h, BW=BW, par=par,
                                  dqb=dqb: e.scalar_tensor_tensor(
                        u4[par][:, 0:BW], dqb[:, 0:BW], slopes[h],
                        pq4[:, 0:BW], ALU.mult, ALU.add),
                        waits=[qk_toks[b], ab_tok, prev_u4[par]], inc="vp")
                stt_toks = [None] * (NB + 1)
                stt_toks[0] = emit_stt(0)
                if NB > 1:
                    qk_toks[1] = emit_qk(1, [stt_toks[0]])
                for b in range(NB):
                    par = b % 2
                    stt = stt_toks[b]
                    th = P.emit("s", lambda e, h=h, b=b, BW=BW, par=par:
                                e.activation(
                        out=u4[par][:, 0:BW], in_=u4[par][:, 0:BW],
                        func=AF.Tanh, scale=CS,
                        bias=abias[:, h * NBB + b: h * NBB + b + 1]),
                        waits=[stt], inc="sp")
                    ex = P.emit("s", lambda e, BW=BW, par=par: e.activation(
                        out=p4b[par][:, 0:BW], in_=u4[par][:, 0:BW],
                        func=AF.Exp, scale=SOFTCAP),
                        waits=[th, prev_p4[par]], inc="sp")
                    prev_u4[par] = ex
                    if b + 1 < NB:
                        stt_toks[b + 1] = emit_stt(b + 1)
                    if b + 2 < NB:
                        qk_toks[b + 2] = emit_qk(b + 2, [stt_toks[b + 1]])
                    if 4 * b >= NSKIP:
                        mo = (4 * b - NSKIP) * TW
                        mk = P.emit("v", lambda e, mo=mo, BW=BW, par=par:
                                    e.tensor_mul(
                            p4b[par][:, 0:BW], p4b[par][:, 0:BW],
                            mbuf[:, mo: mo + BW]),
                            waits=[ex, m_tok], inc="vp")
                        last_mask = mk
                    else:
                        mk = ex
                    pv = None
                    for i in range(4):
                        s = 4 * b + i
                        pv = P.emit("t", lambda e, s=s, i=i, g=g, TW=TW, b=b,
                                    NB=NB, par=par: e.matmul(
                            pf4[:, 0:TW],
                            vtm[:, s * GW + g * 128: s * GW + g * 128 + 128],
                            p4b[par][:, i * TW:(i + 1) * TW],
                            start=(b == 0 and i == 0),
                            stop=(b == NB - 1 and i == 3)),
                            waits=[mk] + ([prev_ctx] if b == 0 and i == 0
                                          else []),
                            inc=None)
                    dn = None
                    for i in range(4):
                        dn = P.emit("t", lambda e, i=i, TW=TW, b=b, NB=NB,
                                    par=par: e.matmul(
                            pf5[0:1, 0:TW], self.sb["ones_col_sb"][:],
                            p4b[par][:, i * TW:(i + 1) * TW],
                            start=(b == 0 and i == 0),
                            stop=(b == NB - 1 and i == 3)),
                            waits=[prev_den] if b == 0 and i == 0 else [],
                            inc="tp" if i == 3 else None)
                    prev_p4[par] = dn
                prev_stt = stt_toks[NB - 1]
                dc_ = P.emit("v", lambda e, TW=TW: e.tensor_copy(
                    out=dsb[:, 0:TW], in_=pf5[0:1, 0:TW]),
                    waits=[prev_p4[(NB - 1) % 2], prev_dsb], inc="vp")
                prev_den = dc_
                rcd = P.emit("v", lambda e, TW=TW: e.reciprocal(
                    dsb[:, 0:TW], dsb[:, 0:TW]), waits=[dc_], inc="vp")
                bc = P.emit("t", lambda e, TW=TW: e.matmul(
                    pq4[:, 0:TW], orf[:], dsb[:, 0:TW],
                    start=True, stop=True),
                    waits=[rcd, prev_stt, self.cl], inc="tp")
                ev = P.emit("s", lambda e, h=h, qoff=qoff, TW=TW: e.activation(
                    out=ctxb[:, h * OWN + qoff: h * OWN + qoff + TW],
                    in_=pf4[:, 0:TW], func=AF.Copy),
                    waits=[prev_p4[(NB - 1) % 2]], inc="sp")
                prev_ctx = ev
                dv = P.emit("v", lambda e, h=h, qoff=qoff, TW=TW:
                            e.tensor_tensor(
                    ctxb[:, h * OWN + qoff: h * OWN + qoff + TW],
                    ctxb[:, h * OWN + qoff: h * OWN + qoff + TW],
                    pq4[:, 0:TW], ALU.mult),
                    waits=[bc, ev], inc="vp")
                prev_dsb = bc
                prev_stt = dv
                self.ctx_done.append(dv)
                self.emit_quant(1)
            msl.done(last_mask)
        if self.taps:
            P.dma("y", lambda e: e.dma_start(out=self.ap["t_ctx"], in_=ctxb[:]),
                  waits=[self.ctx_done[-1]], inc="tap")
            self.tap_kv = ("tap", P.cnt["tap"])
        self.close_arena("L2")
        self.close_arena("Lqf")
        self.close_arena("Lkv")

    def wo_ln2(self):
        c, P = self.cfg, self.P
        D, DC, OWN, H = c["D"], c["DC"], c["OWN"], c["H"]
        NSLOT, NTC = c["NSLOT"], (OWN + 511) // 512
        pf = self.pf
        ctxb = self.sb["ctxb"]
        gate = [self.ctx_done[-1]]
        if self.taps and self.tap_kv:
            gate.append(self.tap_kv)

        ln2x = self.sbuf("L3b", "ln2x", [128, DC * OWN], BF16)
        x1 = self.sbuf("L3", "x1", [128, DC * OWN], F32)
        xo = self.sbuf("L3", "xw", [128, D], F32)
        wob = [self.sbuf("L3", f"wob{i}", [128, H * 128], BF16) for i in (0, 1)]
        x1b = self.sbuf("L3", "x1b", [128, 512], BF16)
        x1q = self.sbuf("L3", "x1q", [128, 512], BF16)
        smm = self.sbuf("L3", "smm", [1, 512], F32)
        smv = self.sbuf("L3", "smv", [1, 512], F32)
        smr = self.sbuf("L3", "smr", [1, 512], F32)
        tmp = self.sbuf("L3", "tmpw", [128, 512], F32)

        xsl = Slot(P, "xw0")
        wsl = [Slot(P, "wol0"), Slot(P, "wol1")]
        orf = self.sb["orf_sb"]

        prev_pf5 = None
        prev_ev = None
        for k in range(NSLOT):
            tlx = xsl.load("g", lambda e, k=k: e.dma_start(
                out=xo[:], in_=self.ap["x_own"][k * 128:(k + 1) * 128, :]),
                extra=gate if k == 0 else [])
            for j in range((DC + 3) // 4):
                dcs = list(range(4 * j, min(4 * j + 4, DC)))
                n4 = len(dcs)
                tt = None
                for ii, dc in enumerate(dcs):
                    tt = P.emit("t", lambda e, dc=dc, ii=ii: e.transpose(
                        pf[5][:, ii * 128:(ii + 1) * 128],
                        xo[:, dc * 128:(dc + 1) * 128],
                        self.sb["ident_f_sb"][:]),
                        waits=([tlx, prev_pf5, self.cl] if ii == 0 else []),
                        inc="tp" if ii == n4 - 1 else None)
                ev = P.emit("s", lambda e, k=k, j=j, n4=n4: e.copy(
                    out=x1[:].rearrange("p (dc n) -> p dc n", n=OWN)
                        [:, 4 * j:4 * j + n4, k * 128:(k + 1) * 128],
                    in_=pf[5][:].rearrange("p (q n) -> p q n", n=128)[:, 0:n4, :]),
                    waits=[tt, prev_ev], inc="sp")
                prev_pf5 = ev
                prev_ev = ev
            xsl.done(prev_ev)
        xfm_tok = prev_ev
        self.x1st = Slot(P, "x1st")
        ln2_done = []
        prev_bank = [None, None]
        prev_statmm = [None, None]
        prev_apply = None
        prev_x1bq = None
        prev_small = None
        for tc in range(NTC):
            off = tc * 512
            tw = min(512, OWN - off)
            bank = tc % 2
            stat_mm = [None, None]
            for dc in range(DC):
                par = dc % 2
                twl = wsl[par].load("g", lambda e, dc=dc, par=par: e.dma_start(
                    out=wob[par][:].rearrange("p (hc n) -> p hc n", n=128),
                    in_=self.ap["wo"][:, dc * 128:(dc + 1) * 128]
                        .rearrange("(hc p) n -> p hc n", p=128)),
                    extra=gate if (tc == 0 and dc < 2) else [])
                mmt = None
                for hc in range(H):
                    mmt = P.emit("t", lambda e, bank=bank, par=par, hc=hc, off=off,
                                 tw=tw: e.matmul(
                        pf[bank][:, 0:tw],
                        wob[par][:, hc * 128:(hc + 1) * 128],
                        ctxb[:, hc * OWN + off: hc * OWN + off + tw],
                        start=(hc == 0), stop=(hc == H - 1)),
                        waits=([twl, prev_bank[bank], xfm_tok] + gate)
                              if hc == 0 else [],
                        inc="tp" if hc == H - 1 else None)
                wsl[par].done(mmt)
                stt = P.emit("v", lambda e, bank=bank, dc=dc, off=off, tw=tw:
                             e.scalar_tensor_tensor(
                    x1[:, dc * OWN + off: dc * OWN + off + tw],
                    pf[bank][:, 0:tw], self.sb["bo_fm_sb"][:, dc:dc + 1],
                    x1[:, dc * OWN + off: dc * OWN + off + tw],
                    ALU.add, ALU.add),
                    waits=[mmt, self.cl], inc="vp")
                prev_bank[bank] = stt
                c1 = P.emit("s", lambda e, dc=dc, off=off, tw=tw: e.activation(
                    out=x1b[:, 0:tw],
                    in_=x1[:, dc * OWN + off: dc * OWN + off + tw],
                    func=AF.Copy), waits=[stt, prev_x1bq], inc="sp")
                c2 = P.emit("s", lambda e, dc=dc, off=off, tw=tw: e.activation(
                    out=x1q[:, 0:tw],
                    in_=x1[:, dc * OWN + off: dc * OWN + off + tw],
                    func=AF.Square), waits=[stt], inc="sp")
                m1 = P.emit("t", lambda e, dc=dc, tw=tw: e.matmul(
                    pf[2][0:1, 0:tw], self.sb["ones_col_sb"][:], x1b[:, 0:tw],
                    start=(dc == 0), stop=(dc == DC - 1)),
                    waits=[c1] + ([prev_statmm[0]] if dc == 0 else []), inc="tp")
                m2 = P.emit("t", lambda e, dc=dc, tw=tw: e.matmul(
                    pf[3][0:1, 0:tw], self.sb["ones_col_sb"][:], x1q[:, 0:tw],
                    start=(dc == 0), stop=(dc == DC - 1)),
                    waits=[c2] + ([prev_statmm[1]] if dc == 0 else []), inc="tp")
                stat_mm = [m1, m2]
                prev_x1bq = m2
            s1 = P.emit("v", lambda e, tw=tw: e.tensor_scalar(
                smm[:, 0:tw], pf[2][0:1, 0:tw], 1.0 / D, None, ALU.mult),
                waits=[stat_mm[0], prev_small], inc="vp")
            s2 = P.emit("v", lambda e, tw=tw: e.tensor_scalar(
                smv[:, 0:tw], pf[3][0:1, 0:tw], 1.0 / D, LN_EPS,
                ALU.mult, ALU.add), waits=[stat_mm[1]], inc="vp")
            s3 = P.emit("v", lambda e, tw=tw: e.tensor_mul(
                smr[:, 0:tw], smm[:, 0:tw], smm[:, 0:tw]), waits=[s1], inc="vp")
            s4 = P.emit("v", lambda e, tw=tw: e.tensor_sub(
                smv[:, 0:tw], smv[:, 0:tw], smr[:, 0:tw]), waits=[s2, s3], inc="vp")
            s5 = P.emit("s", lambda e, tw=tw: e.activation(
                out=smr[:, 0:tw], in_=smv[:, 0:tw], func=AF.Ln),
                waits=[s4], inc="sp")
            s6 = P.emit("s", lambda e, tw=tw: e.activation(
                out=smr[:, 0:tw], in_=smr[:, 0:tw], func=AF.Exp, scale=-0.5),
                waits=[s5], inc="sp")
            b1 = P.emit("t", lambda e, tw=tw: e.matmul(
                pf[2][:, 0:tw], orf[:], smm[:, 0:tw], start=True, stop=True),
                waits=[s6, s1], inc="tp")
            b2 = P.emit("t", lambda e, tw=tw: e.matmul(
                pf[3][:, 0:tw], orf[:], smr[:, 0:tw], start=True, stop=True),
                waits=[s6], inc="tp")
            prev_small = s6
            ap_t = prev_apply
            for dc in range(DC):
                a1 = P.emit("v", lambda e, dc=dc, off=off, tw=tw: e.tensor_tensor(
                    tmp[:, 0:tw], x1[:, dc * OWN + off: dc * OWN + off + tw],
                    pf[2][:, 0:tw], ALU.subtract),
                    waits=[b1, prev_bank[0], prev_bank[1], ap_t], inc="vp")
                ap_t = P.emit("v", lambda e, dc=dc, off=off, tw=tw: e.tensor_tensor(
                    ln2x[:, dc * OWN + off: dc * OWN + off + tw],
                    tmp[:, 0:tw], pf[3][:, 0:tw], ALU.mult),
                    waits=[b2, a1], inc="vp")
                ln2_done.append(ap_t)
            prev_apply = ap_t
            prev_statmm = [ap_t, ap_t]
            self.emit_quant(8)
        x1_stores = []
        for dc in range(DC):
            tk = self.x1st.load("y", lambda e, dc=dc: e.dma_start(
                out=self.ap["x1s"][dc * 128:(dc + 1) * 128, :],
                in_=x1[:, dc * OWN:(dc + 1) * OWN]), extra=[ln2_done[-1]])
            x1_stores.append(tk)
        self.ln2_done = ln2_done
        if self.taps:
            P.sem("tap")
            P.dma("y", lambda e: e.dma_start(out=self.ap["t_ctx"], in_=ctxb[:]),
                  waits=[self.ctx_done[-1]], inc="tap")
            for dc in range(DC):
                P.dma("y", lambda e, dc=dc: e.dma_start(
                    out=self.ap["t_x1"][dc * 128:(dc + 1) * 128, :],
                    in_=x1[:, dc * OWN:(dc + 1) * OWN]),
                    waits=[ln2_done[-1]], inc="tap")
            P.dma("y", lambda e: e.dma_start(out=self.ap["t_ln2"], in_=ln2x[:]),
                  waits=[ln2_done[-1]], inc="tap")
        self.close_arena("Lctx")
        self.tap_x1_gate = ("tap", P.cnt.get("tap", 0)) if self.taps else None
        self.close_arena("L3")

    def ffn(self):
        c, P = self.cfg, self.P
        D, DC, OWN, HFF, NH2 = c["D"], c["DC"], c["OWN"], c["HFF"], c["NH2"]
        NTC = (OWN + 511) // 512
        NST = c["NSTRIP"]
        pq4 = self.pq4
        pf4, pf5 = self.pf[4], self.pf[5]
        ln2x = self.sb["ln2x"]
        orf = self.sb["orf_sb"]
        DBW = min(512, D)
        NDB = D // DBW
        assert NDB * DBW == D
        WB2 = self.qWB2
        NRS = 2 * DC

        hn = self.sbuf("L4", "hn", [128, NH2 * OWN], BF16)
        ring1 = self.sbuf("L4", "ring1", [128, DC * 512], BF16,
                          is_transpose_dest=True)
        ring2 = self.sbuf("L4c", "ring2", [128, DC * 512], BF16,
                          is_transpose_dest=True)

        def rslice(sl, w=512):
            if sl < DC:
                return ring1[:, sl * 512: sl * 512 + w]
            return ring2[:, (sl - DC) * 512: (sl - DC) * 512 + w]
        rsl = [Slot(P, f"rg{i}") for i in range(NRS)]
        P.sem("outst0")
        P.sem("outst1")
        x1full = ("x1st", P.cnt["x1st"])
        gate0 = [x1full, self.ln2_done[-1]]
        if self.taps:
            gate0.append(self.tap_x1_gate)

        hn_last = None
        prev_bank = [None] * 4
        prev_evac = {}
        strip_last_evac = [None, None]
        for sidx in range(NST):
            j2, half = sidx // 2, sidx % 2
            r0 = half * HFF + j2 * 512
            par = sidx % 2
            base = (sidx % 2) * DC
            qts = [self.qtok[("w1h", (r0 + rb * 128) // 128)] for rb in range(4)]
            ltoks = []
            for dc in range(DC):
                sl = base + dc
                lt = rsl[sl].load("y",
                                  lambda e, r0=r0, dc=dc, sl=sl: e.dma_start(
                    out=rslice(sl),
                    in_=self.ap["w1h"][r0:r0 + 512, dc * 128:(dc + 1) * 128],
                    transpose=True),
                    extra=qts + (gate0 if dc < 2 else []))
                ltoks.append(lt)
            for hc in range(4):
                hcg = j2 * 4 + hc
                for tc in range(NTC):
                    off = tc * 512
                    tw = min(512, OWN - off)
                    bank = (hc * NTC + tc) % 4
                    mmt = None
                    for dc in range(DC):
                        mmt = P.emit("t", lambda e, base=base, hc=hc, dc=dc,
                                     off=off, tw=tw, bank=bank: e.matmul(
                            pq4[:, bank * 512: bank * 512 + tw],
                            rslice(base + dc)[:, hc * 128: hc * 128 + 128],
                            ln2x[:, dc * OWN + off: dc * OWN + off + tw],
                            start=(dc == 0), stop=(dc == DC - 1)),
                            waits=(ltoks + [prev_bank[bank],
                                    self.ln2_done[-1]] if dc == 0 else []),
                            inc="tp" if dc == DC - 1 else None)
                    if half == 0:
                        evt = P.emit("s", lambda e, hcg=hcg, off=off, tw=tw,
                                     bank=bank: e.activation(
                            out=hn[:, hcg * OWN + off: hcg * OWN + off + tw],
                            in_=pq4[:, bank * 512: bank * 512 + tw],
                            func=AF.Silu),
                            waits=[mmt, prev_evac.get((hcg, tc))], inc="sp")
                    else:
                        evt = P.emit("v", lambda e, hcg=hcg, off=off, tw=tw,
                                     bank=bank: e.tensor_mul(
                            hn[:, hcg * OWN + off: hcg * OWN + off + tw],
                            hn[:, hcg * OWN + off: hcg * OWN + off + tw],
                            pq4[:, bank * 512: bank * 512 + tw]),
                            waits=[mmt, prev_evac.get((hcg, tc))], inc="vp")
                        hn_last = evt
                    prev_bank[bank] = evt
                    prev_evac[(hcg, tc)] = evt
            for dc in range(DC):
                rsl[base + dc].done(prev_bank[(3 * NTC + NTC - 1) % 4])
            self.emit_quant(4)
        self.emit_quant(self.qsched_at("all"))
        if self.taps and self.taps >= 3:
            P.dma("y", lambda e: e.dma_start(out=self.ap["t_hn"], in_=hn[:]),
                  waits=[hn_last], inc="tap")
        self.close_arena("L4c")

        sq0 = self.sbuf("L4b", "sq0", [128, OWN], BF16)
        sq = [sq0, sq0]
        srr = self.sbuf("L4b", "srr", [1, 512], F32)
        rbc = self.sbuf("L4b", "rbc", [128, OWN], F32)
        obuf = [self.sbuf("L4b", f"obuf{i}", [128, 512], F32) for i in (0, 1)]
        x1l = [self.sbuf("L4b", f"x1l{i}", [128, OWN], F32) for i in (0, 1)]
        x1sl = [Slot(P, "x1l0"), Slot(P, "x1l1")]

        prev_sq = [None, None]
        smm = [None, None]
        for hc in range(NH2):
            par = hc % 2
            a1 = P.emit("s", lambda e, hc=hc, par=par: e.activation(
                out=sq[par][:], in_=hn[:, hc * OWN:(hc + 1) * OWN],
                func=AF.Square),
                waits=[hn_last, prev_sq[0], prev_sq[1]], inc="sp")
            for tc in range(NTC):
                off = tc * 512
                tw = min(512, OWN - off)
                pfr = pf4 if tc == 0 else pf5
                mm = P.emit("t", lambda e, par=par, off=off, tw=tw, hc=hc,
                            pfr=pfr: e.matmul(
                    pfr[0:1, 0:tw], self.sb["ones_col_sb"][:],
                    sq[par][:, off:off + tw],
                    start=(hc == 0), stop=(hc == NH2 - 1)),
                    waits=[a1], inc="tp")
                smm[tc] = mm
            prev_sq[0] = prev_sq[1] = smm[NTC - 1]
        rb_done = []
        prev_r = None
        for tc in range(NTC):
            off = tc * 512
            tw = min(512, OWN - off)
            pfr = pf4 if tc == 0 else pf5
            s1 = P.emit("v", lambda e, tw=tw, pfr=pfr: e.tensor_scalar(
                srr[:, 0:tw], pfr[0:1, 0:tw], 1.0 / HFF, RMS_EPS,
                ALU.mult, ALU.add), waits=[smm[tc], prev_r], inc="vp")
            s2 = P.emit("s", lambda e, tw=tw: e.activation(
                out=srr[:, 0:tw], in_=srr[:, 0:tw], func=AF.Ln),
                waits=[s1], inc="sp")
            s3 = P.emit("s", lambda e, tw=tw: e.activation(
                out=srr[:, 0:tw], in_=srr[:, 0:tw], func=AF.Exp, scale=-0.5),
                waits=[s2], inc="sp")
            b1 = P.emit("t", lambda e, tw=tw, tc=tc: e.matmul(
                pq4[:, tc * 512: tc * 512 + tw], orf[:], srr[:, 0:tw],
                start=True, stop=True),
                waits=[s3, s1, prev_bank[tc]], inc="tp")
            cp = P.emit("v", lambda e, off=off, tw=tw, tc=tc: e.tensor_copy(
                out=rbc[:, off:off + tw], in_=pq4[:, tc * 512: tc * 512 + tw]),
                waits=[b1], inc="vp")
            prev_r = cp
            rb_done.append(cp)
        prev_acc = [prev_bank[i % 4] for i in range(4)]
        prev_ob = [None, None]
        tile_ctr = 0
        DB2 = min(256, D)
        ND2 = D // DB2
        nds = DB2 // 128
        for db in range(ND2):
            r0 = db * DB2
            txs = {}
            for ds in range(nds):
                dci = db * nds + ds
                gext = ([x1full, hn_last] + prev_bank) if db == 0 else [x1full]
                txs[ds] = x1sl[dci % 2].load(
                    "y", lambda e, dci=dci: e.dma_start(
                        out=x1l[dci % 2][:],
                        in_=self.ap["x1s"][dci * 128:(dci + 1) * 128, :]),
                    extra=gext)
            last_mm = [None] * 4
            for hc in range(NH2):
                sl = tile_ctr % DC
                tile_ctr += 1
                qks = [self.qtok[("w2h", rb, (hc * 128) // WB2)]
                       for rb in range(r0 // 128, (r0 + DB2) // 128)]
                lt = rsl[sl].load("y",
                                  lambda e, r0=r0, hc=hc, sl=sl, DB2=DB2:
                                  e.dma_start(
                    out=rslice(sl, DB2),
                    in_=self.ap["w2h"][r0:r0 + DB2, hc * 128:(hc + 1) * 128],
                    transpose=True), extra=qks)
                for ds in range(nds):
                    for tc in range(NTC):
                        off = tc * 512
                        tw = min(512, OWN - off)
                        bank = ds * NTC + tc
                        mm = P.emit("t", lambda e, sl=sl, ds=ds, hc=hc,
                                    off=off, tw=tw, bank=bank: e.matmul(
                            pq4[:, bank * 512: bank * 512 + tw],
                            rslice(sl)[:, ds * 128: ds * 128 + 128],
                            hn[:, hc * OWN + off: hc * OWN + off + tw],
                            start=(hc == 0), stop=(hc == NH2 - 1)),
                            waits=[lt] + ([prev_acc[bank], hn_last,
                                           rb_done[-1]] if hc == 0 else []),
                            inc="tp" if (hc == NH2 - 1 or
                                         (ds == nds - 1 and tc == NTC - 1))
                                else None)
                        last_mm[bank] = mm
                rsl[sl].done(last_mm[(nds - 1) * NTC + NTC - 1])
            for ds in range(nds):
                dci = db * nds + ds
                for tc in range(NTC):
                    off = tc * 512
                    tw = min(512, OWN - off)
                    bank = ds * NTC + tc
                    opar = (dci * NTC + tc) % 2
                    e1 = P.emit("v", lambda e, bank=bank, off=off, tw=tw,
                                opar=opar: e.tensor_tensor(
                        obuf[opar][:, 0:tw], pq4[:, bank * 512: bank * 512 + tw],
                        rbc[:, off:off + tw], ALU.mult),
                        waits=[last_mm[bank], rb_done[-1], prev_ob[opar]],
                        inc="vp")
                    e2 = P.emit("v", lambda e, dci=dci, off=off, tw=tw,
                                opar=opar: e.tensor_add(
                        obuf[opar][:, 0:tw], obuf[opar][:, 0:tw],
                        x1l[dci % 2][:, off:off + tw]),
                        waits=[txs[ds], e1], inc="vp")
                    ot = P.dma("y", lambda e, dci=dci, off=off, tw=tw,
                               opar=opar: e.dma_start(
                        out=self.ap["out_fm"][dci * 128:(dci + 1) * 128,
                                              off:off + tw],
                        in_=obuf[opar][:, 0:tw]),
                        waits=[e2], inc=f"outst{opar}")
                    prev_ob[opar] = ot
                    prev_acc[bank] = e1
                x1sl[dci % 2].done(prev_ob[(dci * NTC + NTC - 1) % 2])
        final_waits = [("outst0", P.cnt["outst0"]),
                       ("outst1", P.cnt["outst1"])]
        if self.taps:
            final_waits.append(("tap", P.cnt["tap"]))
        P.emit("y", lambda e: e.nop(), waits=final_waits)

def build(cfg, taps=False, stop_after=None):
    return Kern(cfg, taps=taps, stop_after=stop_after).build()



def kernel(L0_attn_hidden_states, L0_attn_Wq, L0_attn_Wk, L0_attn_Wv,
           L0_attn_Wo, L0_attn_bo, L0_ffn_w_gate_up, L0_ffn_w_proj):
    from concourse.bass_utils import run_bass_kernel_spmd
    cfg = REAL
    ins = dict(
        x=np.asarray(L0_attn_hidden_states, dtype=np.float32),
        Wq=np.asarray(L0_attn_Wq, dtype=np.float32),
        Wk=np.asarray(L0_attn_Wk, dtype=np.float32),
        Wv=np.asarray(L0_attn_Wv, dtype=np.float32),
        Wo=np.asarray(L0_attn_Wo, dtype=np.float32),
        bo=np.asarray(L0_attn_bo, dtype=np.float32),
        w_gate_up=np.asarray(L0_ffn_w_gate_up, dtype=np.float32),
        w_proj=np.asarray(L0_ffn_w_proj, dtype=np.float32),
    )
    in_maps = host_inputs(cfg, ins)
    nc = build(cfg, taps=False)
    res = run_bass_kernel_spmd(nc, in_maps, core_ids=list(range(8)))
    return assemble(cfg, res.results)


# version 05
# speedup vs baseline: 1.9370x; 1.1606x over previous
import numpy as np
import ml_dtypes
from contextlib import ExitStack

import concourse.bass as bass
import concourse.mybir as mybir

F32 = mybir.dt.float32
BF16 = mybir.dt.bfloat16
AF = mybir.ActivationFunctionType
ALU = mybir.AluOpType
AX = mybir.AxisListType
BF = ml_dtypes.bfloat16

MAGIC = float(1.5 * 2 ** 23)
CQ = 0.501953125
SOFTCAP = 30.0
ROPE_THETA = 10000.0
LN_EPS = 1e-5
RMS_EPS = float(np.finfo(np.float32).eps)
NEG = -1e30

REAL = dict(T=4096, D=2048, H=16, G=4, HFF=8192)
MINI = dict(T=1024, D=256, H=4, G=2, HFF=512)
MIDI2 = dict(T=1024, D=1536, H=4, G=2, HFF=512, NRING_CAP=2)
MIDI4 = dict(T=4096, D=256, H=2, G=1, HFF=512)


def derived(cfg):
    d = dict(cfg)
    d["DC"] = cfg["D"] // 128
    d["NQT"] = cfg["T"] // 128
    d["NSLOT"] = d["NQT"] // 4
    d["NA_SLOTS"] = d["NSLOT"] // 2
    d["NB_SLOTS"] = d["NSLOT"] - d["NA_SLOTS"]
    d["QWA"] = 128 * d["NA_SLOTS"]
    d["QWB"] = 128 * d["NB_SLOTS"]
    d["NTA"] = 4 * d["NA_SLOTS"]
    d["NTB"] = 4 * d["NSLOT"]
    d["OWN"] = 128 * d["NSLOT"]
    d["HB"] = cfg["H"] // 2
    d["W1R"] = 2 * cfg["HFF"]
    d["NG1"] = cfg["D"] // 64
    d["NG2"] = cfg["HFF"] // 64
    d["NSTRIP"] = d["W1R"] // 512
    d["NH2"] = cfg["HFF"] // 128
    d["REP"] = cfg["H"] // cfg["G"]
    return d


def own_qtiles(cfg, lane):
    d = derived(cfg)
    return [lane + 4 * k for k in range(d["NSLOT"])]


def qcols(cfg, lane, qc):
    d = derived(cfg)
    slots = (range(0, d["NA_SLOTS"]) if qc == 0
             else range(d["NA_SLOTS"], d["NSLOT"]))
    cols = []
    for k in slots:
        t = lane + 4 * k
        cols.extend(range(t * 128, t * 128 + 128))
    return np.array(cols, dtype=np.int64)



def host_inputs(cfg, ins):
    d = derived(cfg)
    T, D, H, G, HFF = cfg["T"], cfg["D"], cfg["H"], cfg["G"], cfg["HFF"]
    x = np.ascontiguousarray(np.asarray(ins["x"], dtype=np.float32))
    assert x.shape == (2, T, D)

    inv = (1.0 / ROPE_THETA ** (np.arange(0, 128, 2, dtype=np.float64) / 128))
    pos = np.arange(T, dtype=np.float64)
    ang = pos[None, :] * np.repeat(inv, 2)[:, None]
    cos_fm = np.cos(ang).astype(np.float32)
    sin_fm = np.sin(ang).astype(np.float32)

    ident = np.eye(128, dtype=BF)
    ident_f = np.eye(128, dtype=np.float32)
    ones_col = np.ones((128, 1), dtype=BF)
    ones_row_f = np.ones((1, 128), dtype=np.float32)
    PT = np.zeros((128, 128), dtype=np.float32)
    for i in range(64):
        PT[2 * i, 2 * i + 1] = 1.0
        PT[2 * i + 1, 2 * i] = -1.0
    PT = PT.astype(BF)

    CSv = 1.0 / (np.float64(30.0) * np.sqrt(np.float64(128.0)))
    slopes = [np.float64(np.float32(2.0) ** np.float32(-(h + 1) * 8.0 / H))
              for h in range(H)]
    NBB = d["NTB"] // 4
    abias = np.zeros((128, H * NBB), dtype=np.float32)
    for h in range(H):
        for b_ in range(NBB):
            abias[:, h * NBB + b_] = np.float32(CSv * slopes[h] * (b_ * 512.0))

    bo_fm = np.ascontiguousarray(
        np.asarray(ins["bo"], dtype=np.float32).reshape(d["DC"], 128).T)

    in_maps = []
    for c in range(8):
        b, lane = c // 4, c % 4
        qa, qb = qcols(cfg, lane, 0), qcols(cfg, lane, 1)
        qall = np.concatenate([qa, qb])
        p = np.arange(128, dtype=np.int64)
        MA = np.zeros((d["NTA"], 128, d["QWA"]), dtype=BF)
        for t in range(d["NTA"]):
            MA[t] = ((t * 128 + p)[:, None] <= qa[None, :]).astype(BF)
        NMB = d["NTB"] - d["NTA"]
        MB = np.zeros((NMB, 128, d["QWB"]), dtype=BF)
        for t in range(NMB):
            MB[t] = (((t + d["NTA"]) * 128 + p)[:, None] <= qb[None, :]).astype(BF)
        DQB_A = np.zeros((128, 4 * d["QWA"]), dtype=np.float32)
        DQB_B = np.zeros((128, 4 * d["QWB"]), dtype=np.float32)
        for i in range(4):
            DQB_A[:, i * d["QWA"]:(i + 1) * d["QWA"]] = \
                (p[:, None] - qa[None, :] + 128 * i)
            DQB_B[:, i * d["QWB"]:(i + 1) * d["QWB"]] = \
                (p[:, None] - qb[None, :] + 128 * i)
        im = dict(
            x_all=x[b],
            wq=np.asarray(ins["Wq"], dtype=np.float32),
            wk=np.asarray(ins["Wk"], dtype=np.float32),
            wv=np.asarray(ins["Wv"], dtype=np.float32),
            wo=np.asarray(ins["Wo"], dtype=np.float32),
            bo_fm=bo_fm,
            w1=np.asarray(ins["w_gate_up"], dtype=np.float32),
            w2=np.asarray(ins["w_proj"], dtype=np.float32),
            cos_fm=cos_fm, sin_fm=sin_fm,
            cos_own=np.ascontiguousarray(cos_fm[:, qall]),
            sin_own=np.ascontiguousarray(sin_fm[:, qall]),
            ident=ident, ident_f=ident_f, ones_col=ones_col,
            ones_row_f=ones_row_f, rot_pt=PT,
            x_own=np.ascontiguousarray(x[b][qall, :]),
            abias=abias,
            ma=MA.reshape(d["NTA"] * 128, d["QWA"]),
            mb=MB.reshape(NMB * 128, d["QWB"]),
            dqb_a=DQB_A, dqb_b=DQB_B,
        )
        in_maps.append(im)
    return in_maps


def assemble(cfg, results):
    d = derived(cfg)
    T, D = cfg["T"], cfg["D"]
    out = np.empty((2, T, D), dtype=np.float32)
    for c in range(8):
        b, lane = c // 4, c % 4
        qall = np.concatenate([qcols(cfg, lane, 0), qcols(cfg, lane, 1)])
        out[b, qall, :] = results[c]["out_fm"].T
    return out



class PB:

    def __init__(self, nc, ctx):
        self.nc = nc
        self.ctx = ctx
        self.q = {k: [] for k in "tsvgy"}
        self.semh = {}
        self.cnt = {}

    def sem(self, name):
        if name not in self.semh:
            self.semh[name] = self.ctx.enter_context(self.nc.semaphore(name))
            self.cnt[name] = 0
        return name

    def emit(self, e, fn, waits=(), inc=None, k=1):
        ws = tuple(t for t in waits if t is not None and t[1] > 0)
        tok = None
        if inc is not None:
            self.sem(inc)
            self.cnt[inc] += k
            tok = (inc, self.cnt[inc])
        self.q[e].append((fn, ws, inc, k))
        return tok

    def dma(self, e, fn, waits=(), inc=None):
        return self.emit(e, fn, waits, inc, 16)

    def replay(self, block):
        semh = self.semh

        def runner(e):
            ops = self.q[e]

            def body(eng):
                clock = {}
                for fn, ws, inc, k in ops:
                    for s, v in ws:
                        if v > clock.get(s, 0):
                            eng.wait_ge(semh[s], v)
                            clock[s] = v
                    inst = fn(eng)
                    if inc is not None:
                        inst.then_inc(semh[inc], k)
            return body

        block.tensor(runner("t"))
        block.scalar(runner("s"))
        block.vector(runner("v"))
        block.gpsimd(runner("g"))
        block.sync(runner("y"))


class Slot:

    def __init__(self, P, name):
        self.P = P
        self.name = P.sem(name)
        self.consumers = []

    def load(self, e, fn, extra=()):
        waits = tuple(self.consumers) + tuple(extra)
        self.consumers = []
        return self.P.dma(e, fn, waits, inc=self.name)

    def done(self, tok):
        if tok is not None:
            self.consumers.append(tok)


class Kern:
    def __init__(self, cfg, taps=False, stop_after=None):
        self.cfg = derived(cfg)
        self.taps = taps
        self.stop_after = stop_after
        self.nc = bass.Bass("TRN2", target_bir_lowering=False)
        self.top = ExitStack()
        self.P = PB(self.nc, self.top)
        self.sb = {}
        self.stacks = {}

    def arena(self, key):
        if key not in self.stacks:
            self.stacks[key] = ExitStack()
        return self.stacks[key]

    def close_arena(self, key):
        if key in self.stacks:
            self.stacks.pop(key).close()

    ARENA_SIDE = {"P": "left", "Q": "left", "Lkv": "left", "Lqf": "left",
                  "L3b": "left", "L3": "left", "L4": "left",
                  "L1": "right", "Lro": "right", "L1q": "right",
                  "Lctx": "right", "L2": "right", "L4b": "left", "L4c": "left"}

    def sbuf(self, key, name, shape, dtype, **kw):
        h = self.arena(key).enter_context(
            self.nc.sbuf_tensor(name, shape, dtype,
                                side=self.ARENA_SIDE[key], **kw))
        self.sb[name] = h
        return h

    def psum(self, name, shape, dtype):
        h = self.top.enter_context(self.nc.psum_tensor(name, shape, dtype))
        self.sb[name] = h
        return h

    def build(self):
        c = self.cfg
        nc = self.nc
        T, D, DC, OWN = c["T"], c["D"], c["DC"], c["OWN"]
        H, G, HFF = c["H"], c["G"], c["HFF"]

        ap = {}
        EI, EO = "ExternalInput", "ExternalOutput"
        for n_, sh, dt_, kd in [
            ("x_all", [T, D], F32, EI), ("x_own", [OWN, D], F32, EI),
            ("wq", [D, H * 128], F32, EI), ("wk", [D, G * 128], F32, EI),
            ("wv", [D, G * 128], F32, EI), ("wo", [H * 128, D], F32, EI),
            ("bo_fm", [128, DC], F32, EI),
            ("w1", [c["W1R"], D], F32, EI), ("w2", [D, HFF], F32, EI),
            ("cos_fm", [128, T], F32, EI), ("sin_fm", [128, T], F32, EI),
            ("cos_own", [128, OWN], F32, EI), ("sin_own", [128, OWN], F32, EI),
            ("ident", [128, 128], BF16, EI), ("ident_f", [128, 128], F32, EI),
            ("ones_col", [128, 1], BF16, EI), ("ones_row_f", [1, 128], F32, EI),
            ("rot_pt", [128, 128], BF16, EI),
            ("abias", [128, c["H"] * (c["NTB"] // 4)], F32, EI),
            ("dqb_a", [128, 4 * c["QWA"]], F32, EI),
            ("dqb_b", [128, 4 * c["QWB"]], F32, EI),
            ("ma", [c["NTA"] * 128, c["QWA"]], BF16, EI),
            ("mb", [(c["NTB"] - c["NTA"]) * 128, c["QWB"]], BF16, EI),
            ("out_fm", [D, OWN], F32, EO),
            ("w1h", [c["W1R"], D], BF16, "Internal"),
            ("w2h", [D, HFF], BF16, "Internal"),
            ("x1s", [D, OWN], F32, "Internal"),
        ]:
            ap[n_] = nc.dram_tensor(n_, sh, dt_, kind=kd).ap()
        if self.taps:
            for n_, sh, dt_ in [
                ("t_k", [128, G * T], BF16), ("t_v", [128, (T // 128) * G * 128], BF16),
                ("t_q", [128, H * OWN], BF16), ("t_ctx", [128, H * OWN], BF16),
                ("t_x1", [D, OWN], F32), ("t_ln2", [128, DC * OWN], BF16),
                ("t_w1h", [c["W1R"], D], BF16), ("t_w2h", [D, HFF], BF16),
                ("t_hn", [128, c["NH2"] * OWN], BF16),
            ]:
                ap[n_] = nc.dram_tensor(n_, sh, dt_, kind=EO).ap()
        self.ap = ap

        pq4 = self.psum("pq4", [128, 2048], F32)
        p4 = self.psum("pf4t", [128, 512], F32)
        p5 = self.psum("pf5t", [128, 512], F32)
        self.psum("pbA", [128, 1024], BF16)
        self.psum("pbB", [128, 1024], BF16)
        self.pf = [pq4[:, 0:512], pq4[:, 512:1024], pq4[:, 1024:1536],
                   pq4[:, 1536:2048], p4[:], p5[:]]
        self.pq4 = pq4

        self.sbuf("P", "ident_sb", [128, 128], BF16)
        self.sbuf("P", "ident_f_sb", [128, 128], F32)
        self.sbuf("P", "ones_col_sb", [128, 1], BF16)
        self.sbuf("P", "orf_sb", [1, 128], F32)
        self.sbuf("P", "rot_pt_sb", [128, 128], BF16)
        self.sbuf("P", "bo_fm_sb", [128, DC], F32)
        P = self.P
        P.sem("cl")
        for src, dst in [("ident", "ident_sb"), ("ident_f", "ident_f_sb"),
                         ("ones_col", "ones_col_sb"), ("ones_row_f", "orf_sb"),
                         ("rot_pt", "rot_pt_sb"), ("bo_fm", "bo_fm_sb")]:
            P.dma("y", (lambda e, s=src, d=dst: e.dma_start(
                out=self.sb[d][:], in_=self.ap[s])), inc="cl")
        self.cl = ("cl", P.cnt["cl"])

        self.quant_init()
        self.sbuf("Lkv", "kfm", [128, G * T], BF16)
        self.sbuf("Lkv", "vtm", [128, (T // 128) * G * 128], BF16)
        done = False
        for name, fn in [("p1a", self.phase1a), ("p1b", self.phase1b),
                         ("attn", self.attention), ("wo", self.wo_ln2),
                         ("ffn", self.ffn)]:
            fn()
            if self.stop_after == name:
                done = True
                break
        if done and self.stop_after not in ("ffn", "f1", "f2"):
            self.finish_early()

        block = self.top.enter_context(self.nc.Block())
        P.replay(block)
        for k in reversed(list(self.stacks)):
            self.close_arena(k)
        self.top.close()
        return self.nc

    def finish_early(self):
        P = self.P
        assert self.taps, "finish_early requires taps"
        P.sem("tap")
        if self.stop_after in ("p1a", "p1b") and "kfm" in self.sb:
            P.dma("y", lambda e: e.dma_start(out=self.ap["t_k"],
                                             in_=self.sb["kfm"][:]),
                  waits=[self.kv_done[-1]], inc="tap")
            P.dma("y", lambda e: e.dma_start(out=self.ap["t_v"],
                                             in_=self.sb["vtm"][:]),
                  waits=[self.kv_done[-1]], inc="tap")
        if self.stop_after == "p1b":
            P.dma("y", lambda e: e.dma_start(out=self.ap["t_q"],
                                             in_=self.sb["qfm"][:]),
                  waits=[self.q_done[-1]], inc="tap")
        if self.stop_after == "attn":
            P.dma("y", lambda e: e.dma_start(out=self.ap["t_ctx"],
                                             in_=self.sb["ctxb"][:]),
                  waits=[self.ctx_done[-1]], inc="tap")
        fw = [("tap", P.cnt["tap"])]
        for s in ("qst0", "qst1"):
            if s in P.cnt:
                fw.append((s, P.cnt[s]))
        P.emit("y", lambda e: e.nop(), waits=fw)

    def quant_init(self):
        c = self.cfg
        D, HFF, W1R = c["D"], c["HFF"], c["W1R"]
        WB2 = min(2048, HFF)
        W = max(D, WB2)
        self.qW, self.qWB2 = W, WB2
        ng = W // 64
        qw0 = self.sbuf("Q", "qw0", [128, W], BF16)
        qu0 = self.sbuf("Q", "qu0", [128, W], BF16)
        self.qw = [qw0, qw0]
        self.qu = [qu0, qu0]
        self.qh = [qw0, qw0]
        qs0 = self.sbuf("Q", "qsums0", [128, ng], F32)
        qb0 = self.sbuf("Q", "qscb0", [128, ng], BF16)
        qr0 = self.sbuf("Q", "qrcp0", [128, ng], F32)
        self.qsums = [qs0, qs0]
        self.qscb = [qb0, qb0]
        self.qrcp = [qr0, qr0]
        _sl = Slot(self.P, "qwl0")
        _st = Slot(self.P, "qst0")
        _sg = Slot(self.P, "qstg0")
        self.qwsl = [_sl, _sl]
        self.qssl = [_st, _st]
        self.qssl_g = [_sg, _sg]
        sched = []
        for j in range(c["NSTRIP"] // 2):
            for half in (0, HFF):
                for rb in range(4):
                    r = half + j * 512 + rb * 128
                    sched.append(dict(src="w1", dst="w1h", key=("w1h", r // 128),
                                      r=r, cc=0, w=D))
        for r in range(0, D, 128):
            for cc in range(0, HFF, WB2):
                sched.append(dict(src="w2", dst="w2h",
                                  key=("w2h", r // 128, cc // WB2),
                                  r=r, cc=cc, w=WB2))
        self.qsched = sched
        self.qpos = 0
        self.qgps_left = max(8, min(64, len(sched) // 3))
        self.qquota = {"p1b": 8, "attn": 32, "wo": 16}
        self.qtok = {}
        self.qprev = dict(u=[[], []], small=[[], []], qh=[[], []])
        self.qidx = 0
        self.qgen = None
        self.qstore_q = "y"

    def qsched_at(self, stage):
        if stage == "all":
            return len(self.qsched) - self.qpos
        q = self.qquota.get(stage, 0)
        if q > 0:
            self.qquota[stage] = q - 1
            return 1
        return 0

    def _quant_gen(self, blk, i):
        P = self.P
        par = i % 2
        w, ng = blk["w"], blk["w"] // 64
        r, cc = blk["r"], blk["cc"]
        src, dst = self.ap[blk["src"]], self.ap[blk["dst"]]
        qw, qu, qh = self.qw[par], self.qu[par], self.qh[par]
        qsums, qscb, qrcp = self.qsums[par], self.qscb[par], self.qrcp[par]
        ld = self.qwsl[par].load("g", lambda e, qw=qw, src=src, r=r, cc=cc,
                                 w=w: e.dma_start(
            out=qw[:, 0:w], in_=src[r:r + 128, cc:cc + w]))
        red = P.emit("v", lambda e, qw=qw, qsums=qsums, w=w, ng=ng: e.reduce_sum(
            qsums[:, 0:ng].unsqueeze(2),
            qw[:, 0:w].rearrange("p (g k) -> p g k", k=64), axis=AX.X,
            apply_absolute_value=True),
            waits=[ld] + self.qprev["small"][par], inc="vp")
        yield
        sc1 = P.emit("v", lambda e, qscb=qscb, qsums=qsums, ng=ng: e.tensor_scalar(
            qscb[:, 0:ng], qsums[:, 0:ng], 1.0 / 64, None, ALU.mult),
            waits=[red], inc="vp")
        sc2 = P.emit("v", lambda e, qscb=qscb, ng=ng: e.tensor_scalar_max(
            qscb[:, 0:ng], qscb[:, 0:ng], float(np.float32(BF(1e-8)))),
            waits=[sc1], inc="vp")
        rc = P.emit("v", lambda e, qrcp=qrcp, qscb=qscb, ng=ng: e.reciprocal(
            qrcp[:, 0:ng], qscb[:, 0:ng]), waits=[sc2], inc="vp")
        u = P.emit("v", lambda e, qu=qu, qw=qw, qrcp=qrcp, w=w, ng=ng:
                   e.tensor_tensor(
            qu[:, 0:w].rearrange("p (g k) -> p g k", k=64),
            qw[:, 0:w].rearrange("p (g k) -> p g k", k=64),
            qrcp[:, 0:ng].unsqueeze(2).broadcast_to([128, ng, 64]),
            ALU.mult), waits=[rc] + self.qprev["u"][par], inc="vp")
        self.qwsl[par].done(u)
        yield
        cl_ = P.emit("v", lambda e, qu=qu, w=w: e.tensor_scalar(
            qu[:, 0:w], qu[:, 0:w], 1.0, -1.0, ALU.min, ALU.max),
            waits=[u], inc="vp")
        mg = P.emit("v", lambda e, qu=qu, w=w: e.tensor_scalar(
            qu[:, 0:w], qu[:, 0:w], MAGIC, -MAGIC, ALU.add, ALU.add),
            waits=[cl_], inc="vp")
        yield
        wh = P.emit("v", lambda e, qh=qh, qu=qu, qscb=qscb, w=w, ng=ng:
                    e.tensor_tensor(
            qh[:, 0:w].rearrange("p (g k) -> p g k", k=64),
            qu[:, 0:w].rearrange("p (g k) -> p g k", k=64),
            qscb[:, 0:ng].unsqueeze(2).broadcast_to([128, ng, 64]),
            ALU.mult), waits=[mg] + self.qprev["qh"][par], inc="vp")
        self.qprev["u"][0] = self.qprev["u"][1] = [wh]
        self.qprev["small"][0] = self.qprev["small"][1] = [u, wh]
        sq_ = self.qstore_q
        ssl = self.qssl if sq_ == "y" else self.qssl_g
        stok = ssl[par].load(sq_,
                             lambda e, qh=qh, dst=dst, r=r, cc=cc,
                             w=w: e.dma_start(
            out=dst[r:r + 128, cc:cc + w], in_=qh[:, 0:w]), extra=[wh])
        ssl[par].done(stok)
        self.qwsl[par].done(stok)
        self.qprev["qh"][0] = self.qprev["qh"][1] = [stok]
        self.qtok[blk["key"]] = stok

    def emit_quant_ops(self, k):
        for _ in range(k):
            if self.qgen is None:
                if self.qpos >= len(self.qsched):
                    return
                blk = self.qsched[self.qpos]
                self.qpos += 1
                self.qgen = self._quant_gen(blk, self.qidx)
                self.qidx += 1
            try:
                next(self.qgen)
            except StopIteration:
                self.qgen = None

    def emit_quant(self, n, eng=None):
        done = 0
        while done < n:
            if self.qgen is None:
                if self.qpos >= len(self.qsched):
                    return
                blk = self.qsched[self.qpos]
                self.qpos += 1
                self.qgen = self._quant_gen(blk, self.qidx)
                self.qidx += 1
            for _ in self.qgen:
                pass
            self.qgen = None
            done += 1

    def ln_tile(self, x_ap, out_ap, jk, st, sc, t, NT, extra, prev):
        P, D = self.P, self.cfg["D"]
        a1 = P.emit("s", lambda e: e.activation(
            out=jk[0][:], in_=x_ap, func=AF.Copy,
            accum_out=st[:, t:t + 1]),
            waits=[prev["a1"]] + extra, inc="sp")
        a2 = P.emit("s", lambda e: e.activation(
            out=jk[1][:], in_=x_ap, func=AF.Square,
            accum_out=st[:, NT + t:NT + t + 1]),
            waits=[prev["a2"], a1], inc="sp")
        prev["a1"], prev["a2"] = a1, a2
        b = 8 * (t % 2)
        pv = prev["sc"][t % 2]
        s1 = P.emit("v", lambda e: e.tensor_scalar(
            sc[:, b + 2:b + 3], st[:, t:t + 1], 1.0 / D, None, ALU.mult),
            waits=[a1, pv], inc="vp")
        s2 = P.emit("v", lambda e: e.tensor_mul(
            sc[:, b + 3:b + 4], sc[:, b + 2:b + 3], sc[:, b + 2:b + 3]),
            waits=[s1], inc="vp")
        s3 = P.emit("v", lambda e: e.tensor_scalar(
            sc[:, b + 4:b + 5], st[:, NT + t:NT + t + 1], 1.0 / D, LN_EPS,
            ALU.mult, ALU.add), waits=[a2], inc="vp")
        s4 = P.emit("v", lambda e: e.tensor_sub(
            sc[:, b + 4:b + 5], sc[:, b + 4:b + 5], sc[:, b + 3:b + 4]),
            waits=[s2, s3], inc="vp")
        s5 = P.emit("s", lambda e: e.activation(
            out=sc[:, b + 5:b + 6], in_=sc[:, b + 4:b + 5], func=AF.Ln),
            waits=[s4], inc="sp")
        s6 = P.emit("s", lambda e: e.activation(
            out=sc[:, b + 6:b + 7], in_=sc[:, b + 5:b + 6], func=AF.Exp,
            scale=-0.5), waits=[s5], inc="sp")
        s7 = P.emit("v", lambda e: e.scalar_tensor_tensor(
            sc[:, b + 7:b + 8], sc[:, b + 2:b + 3], -1.0,
            sc[:, b + 6:b + 7], ALU.mult, ALU.mult),
            waits=[s6, s2], inc="vp")
        prev["sc"][t % 2] = s7
        ap_ = P.emit("s", lambda e: e.activation(
            out=out_ap, in_=x_ap, func=AF.Identity,
            scale=sc[:, b + 6:b + 7], bias=sc[:, b + 7:b + 8]),
            waits=[s6, s7, prev["apply"]], inc="sp")
        prev["apply"] = ap_
        return ap_

    def phase1a(self):
        c, P = self.cfg, self.P
        T, D, DC, G = c["T"], c["D"], c["DC"], c["G"]
        NT, NCH = T // 128, T // 512
        GW = G * 128

        xt = [self.sbuf("L1", f"xt{i}", [128, D], F32) for i in (0, 1)]
        ln1tm = [self.sbuf("L1", f"ln1tm{i}", [128, D], BF16) for i in (0, 1)]
        lnfm = [self.sbuf("L1", f"lnfm{i}", [128, DC * 512], BF16) for i in (0, 1)]
        csb = [self.sbuf("L1", f"csb{i}", [128, 512], F32) for i in (0, 1)]
        ssb = [self.sbuf("L1", f"ssb{i}", [128, 512], F32) for i in (0, 1)]
        jk = [self.sbuf("L1", f"jk{i}", [128, D], BF16) for i in (0, 1)]
        st = self.sbuf("L1", "st", [128, 2 * NT], F32)
        sc = self.sbuf("L1", "sc", [128, 16], F32)
        kraw = [self.sbuf("L1", f"kraw{i}", [128, 512], BF16) for i in (0, 1)]
        rt1 = self.sbuf("L1", "rt1", [128, 512], F32)
        rt2 = self.sbuf("L1", "rt2", [128, 512], F32)
        wkb = self.sbuf("L1", "wkb", [128, DC * GW], BF16)
        wvb = self.sbuf("L1", "wvb", [128, DC * GW], BF16)

        kfm, vtm = self.sb["kfm"], self.sb["vtm"]
        pf = self.pf
        pbA = self.sb["pbA"]

        P.sem("wkv")
        P.dma("g", lambda e: e.dma_start(
            out=wkb[:].rearrange("p (dc n) -> p dc n", n=GW),
            in_=self.ap["wk"].rearrange("(dc p) n -> p dc n", p=128)), inc="wkv")
        P.dma("g", lambda e: e.dma_start(
            out=wvb[:].rearrange("p (dc n) -> p dc n", n=GW),
            in_=self.ap["wv"].rearrange("(dc p) n -> p dc n", p=128)), inc="wkv")
        wkv_tok = ("wkv", 32)

        xsl = [Slot(P, "xls0"), Slot(P, "xls1")]
        cssl = [Slot(P, "csl0"), Slot(P, "csl1")]
        t_ms = P.emit("v", lambda e: e.memset(st[:], 0.0), inc="vp")

        prev = dict(a1=None, a2=None, apply=None, sc=[None, None],
                    lnfm=[[], []], kraw=[[], []], rt=None, pf=[None] * 6, pbA=[])
        kv_done = []

        for ch in range(NCH):
            par = ch % 2
            tcs = cssl[par].load("y", lambda e, ch=ch, par=par: e.dma_start(
                out=csb[par][:], in_=self.ap["cos_fm"][:, ch * 512:(ch + 1) * 512]))
            tss = cssl[par].load("y", lambda e, ch=ch, par=par: e.dma_start(
                out=ssb[par][:], in_=self.ap["sin_fm"][:, ch * 512:(ch + 1) * 512]))
            lnfm_rdy = []
            for tl in range(4):
                t = ch * 4 + tl
                tp2 = t % 2
                tlx = xsl[tp2].load("g", lambda e, t=t, tp2=tp2: e.dma_start(
                    out=xt[tp2][:], in_=self.ap["x_all"][t * 128:(t + 1) * 128, :]))
                ap_ = self.ln_tile(xt[tp2][:], ln1tm[tp2][:], jk, st, sc, t, NT,
                                   [tlx, t_ms], prev)
                xsl[tp2].done(ap_)
                nb = (DC + 7) // 8
                for j in range(nb):
                    dcs = list(range(8 * j, min(8 * j + 8, DC)))
                    n8 = len(dcs)
                    tt = None
                    for ii, dc in enumerate(dcs):
                        tt = P.emit("t", lambda e, tp2=tp2, dc=dc, ii=ii: e.transpose(
                            pbA[:, ii * 128:(ii + 1) * 128],
                            ln1tm[tp2][:, dc * 128:(dc + 1) * 128],
                            self.sb["ident_sb"][:]),
                            waits=([ap_, self.cl] + prev["pbA"]) if ii == 0 else [],
                            inc="tp" if ii == n8 - 1 else None)
                    ev = P.emit("s", lambda e, par=par, tl=tl, j=j, n8=n8:
                                e.copy(
                        out=lnfm[par][:].rearrange("p (dc n) -> p dc n", n=512)
                            [:, 8 * j:8 * j + n8, tl * 128:(tl + 1) * 128],
                        in_=pbA[:].rearrange("p (k n) -> p k n", n=128)[:, 0:n8, :]),
                        waits=[tt] + prev["lnfm"][par], inc="sp")
                    lnfm_rdy.append(ev)
                    prev["pbA"] = [ev]
                    self.emit_quant_ops(1)
            prev["lnfm"][par] = []
            for g in range(G):
                bank = g % 2
                mmt = None
                for dc in range(DC):
                    mmt = P.emit("t", lambda e, g=g, dc=dc, par=par, bank=bank:
                                 e.matmul(
                        pf[bank][:, 0:512],
                        wkb[:, dc * GW + g * 128: dc * GW + g * 128 + 128],
                        lnfm[par][:, dc * 512:(dc + 1) * 512],
                        start=(dc == 0), stop=(dc == DC - 1)),
                        waits=([wkv_tok, prev["pf"][bank]] + lnfm_rdy)
                              if dc == 0 else [],
                        inc="tp" if dc == DC - 1 else None)
                kb = g % 2
                e1 = P.emit("s", lambda e, bank=bank, kb=kb: e.activation(
                    out=kraw[kb][:], in_=pf[bank][:, 0:512], func=AF.Copy),
                    waits=[mmt] + prev["kraw"][kb], inc="sp")
                prev["pf"][bank] = e1
                rmt = P.emit("t", lambda e, kb=kb: e.matmul(
                    pf[2][:, 0:512], self.sb["rot_pt_sb"][:], kraw[kb][:],
                    start=True, stop=True),
                    waits=[e1, prev["pf"][2], self.cl], inc="tp")
                d1 = P.emit("v", lambda e, par=par, kb=kb: e.tensor_mul(
                    rt1[:], kraw[kb][:], csb[par][:]),
                    waits=[e1, tss, prev["rt"]], inc="vp")
                d2 = P.emit("v", lambda e, par=par: e.tensor_mul(
                    rt2[:], pf[2][:, 0:512], ssb[par][:]),
                    waits=[rmt, tss, prev["rt"]], inc="vp")
                prev["pf"][2] = d2
                d3 = P.emit("v", lambda e, g=g, ch=ch: e.tensor_add(
                    kfm[:, g * T + ch * 512: g * T + ch * 512 + 512],
                    rt1[:], rt2[:]), waits=[d1, d2], inc="vp")
                prev["rt"] = d3
                prev["kraw"][kb] = [rmt, d1]
                kv_done.append(d3)
                self.emit_quant_ops(1)
            cssl[par].done(kv_done[-1])
            last_v = None
            for tl in range(4):
                t = ch * 4 + tl
                mmt = None
                for dc in range(DC):
                    mmt = P.emit("t", lambda e, tl=tl, dc=dc, par=par: e.matmul(
                        pf[4][:, 0:GW],
                        lnfm[par][:, dc * 512 + tl * 128: dc * 512 + tl * 128 + 128],
                        wvb[:, dc * GW:(dc + 1) * GW],
                        start=(dc == 0), stop=(dc == DC - 1)),
                        waits=[wkv_tok, prev["pf"][4]] if dc == 0 else [],
                        inc="tp" if dc == DC - 1 else None)
                ev = P.emit("s", lambda e, t=t: e.activation(
                    out=vtm[:, t * GW:(t + 1) * GW], in_=pf[4][:, 0:GW],
                    func=AF.Copy), waits=[mmt], inc="sp")
                prev["pf"][4] = ev
                kv_done.append(ev)
                last_v = ev
                self.emit_quant_ops(1)
            prev["lnfm"][par] = [last_v]
        self.kv_done = kv_done
        self.close_arena("L1")

    def phase1b(self):
        c, P = self.cfg, self.P
        D, DC, OWN, H, T = c["D"], c["DC"], c["OWN"], c["H"], c["T"]
        NSLOT = c["NSLOT"]
        NTC = (OWN + 511) // 512
        self.sbuf("Lqf", "qfm", [128, H * OWN], BF16)
        cosq = self.sbuf("Lro", "cosq", [128, OWN], F32)
        sinq = self.sbuf("Lro", "sinq", [128, OWN], F32)
        q_in = self.sbuf("L1q", "q_in", [128, DC * OWN], BF16)
        xo = self.sbuf("L1q", "xoq", [128, D], F32)
        lntm = [self.sbuf("L1q", f"lntm{i}", [128, D], BF16) for i in (0, 1)]
        jkq = self.sbuf("L1q", "jkq0", [128, D], BF16)
        jk = [jkq, jkq]
        st = self.sbuf("L1q", "stq", [128, 2 * NSLOT], F32)
        sc = self.sbuf("L1q", "scq", [128, 16], F32)
        wqh = [self.sbuf("L1q", f"wqh{i}", [128, DC * 128], BF16) for i in (0, 1)]
        qraw = self.sbuf("L1q", "qraw", [128, OWN], BF16)
        rq1 = self.sbuf("L1q", "rq1", [128, 512], F32)
        rq2 = self.sbuf("L1q", "rq2", [128, 512], F32)
        qfm = self.sb["qfm"]
        pf = self.pf
        pbA = self.sb["pbA"]

        P.sem("roL")
        pgate = [self.kv_done[-1]]
        for src, dst in [("cos_own", cosq), ("sin_own", sinq)]:
            P.dma("y", lambda e, s=src, d=dst: e.dma_start(out=d[:], in_=self.ap[s]),
                  waits=pgate, inc="roL")
        ro_tok = ("roL", P.cnt["roL"])

        xsl = Slot(P, "xoq0")
        wsl = [Slot(P, "wql0"), Slot(P, "wql1")]
        t_ms = P.emit("v", lambda e: e.memset(st[:], 0.0), waits=pgate, inc="vp")
        prev = dict(a1=None, a2=None, apply=None, sc=[None, None], pbA=[],
                    pf=[None] * 6, qraw=[], rq=None)
        qin_rdy = []
        for k in range(NSLOT):
            par = k % 2
            tlx = xsl.load("g", lambda e, k=k: e.dma_start(
                out=xo[:], in_=self.ap["x_own"][k * 128:(k + 1) * 128, :]),
                extra=pgate if k == 0 else [])
            ap_ = self.ln_tile(xo[:], lntm[par][:], jk, st, sc, k, NSLOT,
                               [tlx, t_ms], prev)
            xsl.done(ap_)
            nb = (DC + 7) // 8
            for j in range(nb):
                dcs = list(range(8 * j, min(8 * j + 8, DC)))
                n8 = len(dcs)
                tt = None
                for ii, dc in enumerate(dcs):
                    tt = P.emit("t", lambda e, par=par, dc=dc, ii=ii: e.transpose(
                        pbA[:, ii * 128:(ii + 1) * 128],
                        lntm[par][:, dc * 128:(dc + 1) * 128],
                        self.sb["ident_sb"][:]),
                        waits=([ap_, self.cl] + prev["pbA"]) if ii == 0 else [],
                        inc="tp" if ii == n8 - 1 else None)
                ev = P.emit("s", lambda e, k=k, j=j, n8=n8: e.copy(
                    out=q_in[:].rearrange("p (dc n) -> p dc n", n=OWN)
                        [:, 8 * j:8 * j + n8, k * 128:(k + 1) * 128],
                    in_=pbA[:].rearrange("p (k n) -> p k n", n=128)[:, 0:n8, :]),
                    waits=[tt], inc="sp")
                prev["pbA"] = [ev]
                qin_rdy.append(ev)
                self.emit_quant_ops(1)
        self.q_done = []
        for h in range(H):
            par = h % 2
            tw_ = wsl[par].load("g", lambda e, h=h, par=par: e.dma_start(
                out=wqh[par][:].rearrange("p (dc n) -> p dc n", n=128),
                in_=self.ap["wq"][:, h * 128:(h + 1) * 128]
                    .rearrange("(dc p) n -> p dc n", p=128)),
                extra=pgate if h < 2 else [])
            for tc in range(NTC):
                off = tc * 512
                tww = min(512, OWN - off)
                bank = tc % 2
                mmt = None
                for dc in range(DC):
                    mmt = P.emit("t", lambda e, dc=dc, par=par, bank=bank,
                                 off=off, tww=tww: e.matmul(
                        pf[bank][:, 0:tww],
                        wqh[par][:, dc * 128:(dc + 1) * 128],
                        q_in[:, dc * OWN + off: dc * OWN + off + tww],
                        start=(dc == 0), stop=(dc == DC - 1)),
                        waits=([tw_, prev["pf"][bank]] + qin_rdy)
                              if dc == 0 else [],
                        inc="tp" if dc == DC - 1 else None)
                e1 = P.emit("s", lambda e, bank=bank, off=off, tww=tww: e.activation(
                    out=qraw[:, off:off + tww], in_=pf[bank][:, 0:tww],
                    func=AF.Copy), waits=[mmt] + prev["qraw"], inc="sp")
                prev["pf"][bank] = e1
                rmt = P.emit("t", lambda e, off=off, tww=tww: e.matmul(
                    pf[2][:, 0:tww], self.sb["rot_pt_sb"][:],
                    qraw[:, off:off + tww], start=True, stop=True),
                    waits=[e1, prev["pf"][2]], inc="tp")
                d1 = P.emit("v", lambda e, off=off, tww=tww: e.tensor_mul(
                    rq1[:, 0:tww], qraw[:, off:off + tww],
                    cosq[:, off:off + tww]),
                    waits=[e1, ro_tok, prev["rq"]], inc="vp")
                d2 = P.emit("v", lambda e, off=off, tww=tww: e.tensor_mul(
                    rq2[:, 0:tww], pf[2][:, 0:tww],
                    sinq[:, off:off + tww]), waits=[rmt, prev["rq"]], inc="vp")
                prev["pf"][2] = d2
                d3 = P.emit("v", lambda e, h=h, off=off, tww=tww: e.tensor_add(
                    qfm[:, h * OWN + off: h * OWN + off + tww],
                    rq1[:, 0:tww], rq2[:, 0:tww]),
                    waits=[d1, d2], inc="vp")
                prev["rq"] = d3
                prev["qraw"] = [rmt, d1]
                self.q_done.append(d3)
                self.emit_quant_ops(1)
            wsl[par].done(self.q_done[-1])
        self.close_arena("L1q")
        self.close_arena("Lro")

    def attention(self):
        c, P = self.cfg, self.P
        T, OWN, H, G = c["T"], c["OWN"], c["H"], c["G"]
        REP, GW = c["REP"], c["G"] * 128
        kfm, vtm, qfm = self.sb["kfm"], self.sb["vtm"], self.sb["qfm"]
        ctxb = self.sbuf("Lctx", "ctxb", [128, H * OWN], BF16)
        pq4 = self.pq4
        pf4, pf5 = self.pf[4], self.pf[5]
        CS = 1.0 / (SOFTCAP * float(np.sqrt(128.0)))
        gate = [self.q_done[-1], self.kv_done[-1]]

        NMA = c["NTA"]
        NMB = c["NTB"] - c["NTA"]
        MW = max(NMA * c["QWA"], NMB * c["QWB"])
        mbuf = self.sbuf("L2", "mbuf", [128, MW], BF16)
        u4 = [self.sbuf("L2", f"u4{i}", [128, 2048], F32) for i in (0, 1)]
        p4b = [self.sbuf("L2", f"p4b{i}", [128, 2048], BF16) for i in (0, 1)]
        dsb = self.sbuf("L2", "dsb0", [1, 512], F32)
        abias = self.sbuf("L2", "abias_sb", [128, c["H"] * (c["NTB"] // 4)], F32)
        dqba = self.sbuf("L2", "dqba", [128, 4 * c["QWA"]], F32)
        dqbb = self.sbuf("L2", "dqbb", [128, 4 * c["QWB"]], F32)
        orf = self.sb["orf_sb"]

        self.tap_kv = None
        if self.taps:
            P.sem("tap")
            P.dma("y", lambda e: e.dma_start(out=self.ap["t_k"], in_=kfm[:]),
                  waits=[self.kv_done[-1], self.q_done[-1]], inc="tap")
            P.dma("y", lambda e: e.dma_start(out=self.ap["t_v"], in_=vtm[:]),
                  waits=[self.kv_done[-1]], inc="tap")
            P.dma("y", lambda e: e.dma_start(out=self.ap["t_q"], in_=qfm[:]),
                  waits=[self.q_done[-1]], inc="tap")
            self.tap_kv = ("tap", P.cnt["tap"])
        P.sem("abl")
        P.dma("y", lambda e: e.dma_start(out=abias[:], in_=self.ap["abias"]),
              waits=gate, inc="abl")
        P.dma("y", lambda e: e.dma_start(out=dqba[:], in_=self.ap["dqb_a"]),
              inc="abl")
        P.dma("y", lambda e: e.dma_start(out=dqbb[:], in_=self.ap["dqb_b"]),
              inc="abl")
        ab_tok = ("abl", P.cnt["abl"])

        msl = Slot(P, "mal")
        slopes = [float(np.float32(2.0) ** np.float32(-(h + 1) * 8.0 / H))
                  for h in range(H)]
        NBB = c["NTB"] // 4
        self.ctx_done = []
        prev_stt = None
        prev_ctx = None
        prev_den = None
        prev_dsb = None
        prev_u4 = [None, None]
        prev_p4 = [None, None]
        for qc in (0, 1):
            TW = c["QWA"] if qc == 0 else c["QWB"]
            NTq = c["NTA"] if qc == 0 else c["NTB"]
            NB = NTq // 4
            BW = 4 * TW
            qoff = 0 if qc == 0 else c["QWA"]
            dqb = dqba if qc == 0 else dqbb
            src_m = "ma" if qc == 0 else "mb"
            NSKIP = 0 if qc == 0 else c["NTA"]
            NM = NTq - NSKIP
            m_tok = msl.load("y", lambda e, src_m=src_m, TW=TW, NM=NM:
                             e.dma_start(
                out=mbuf[:, 0:NM * TW].rearrange("p (t f) -> p t f", f=TW),
                in_=self.ap[src_m].rearrange("(t p) f -> p t f", p=128)),
                extra=gate + [ab_tok] if qc == 0 else [])
            last_mask = None
            for h in range(H):
                g = h // REP

                def emit_qk(b, first_wait):
                    mmt = None
                    for i in range(4):
                        s = 4 * b + i
                        mmt = P.emit("t", lambda e, s=s, i=i, h=h, g=g,
                                     qoff=qoff, TW=TW: e.matmul(
                            pq4[:, i * TW:(i + 1) * TW],
                            kfm[:, g * T + s * 128: g * T + s * 128 + 128],
                            qfm[:, h * OWN + qoff: h * OWN + qoff + TW],
                            start=True, stop=True),
                            waits=(first_wait + gate) if i == 0 else [],
                            inc="tp" if i == 3 else None)
                    return mmt
                qk_toks = [None] * (NB + 2)
                qk_toks[0] = emit_qk(0, [prev_stt] if prev_stt else [])

                def emit_stt(b):
                    par = b % 2
                    return P.emit("v", lambda e, h=h, BW=BW, par=par,
                                  dqb=dqb: e.scalar_tensor_tensor(
                        u4[par][:, 0:BW], dqb[:, 0:BW], slopes[h],
                        pq4[:, 0:BW], ALU.mult, ALU.add),
                        waits=[qk_toks[b], ab_tok, prev_u4[par]], inc="vp")
                stt_toks = [None] * (NB + 1)
                stt_toks[0] = emit_stt(0)
                if NB > 1:
                    qk_toks[1] = emit_qk(1, [stt_toks[0]])
                for b in range(NB):
                    par = b % 2
                    stt = stt_toks[b]
                    th = P.emit("s", lambda e, h=h, b=b, BW=BW, par=par:
                                e.activation(
                        out=u4[par][:, 0:BW], in_=u4[par][:, 0:BW],
                        func=AF.Tanh, scale=CS,
                        bias=abias[:, h * NBB + b: h * NBB + b + 1]),
                        waits=[stt], inc="sp")
                    ex = P.emit("s", lambda e, BW=BW, par=par: e.activation(
                        out=p4b[par][:, 0:BW], in_=u4[par][:, 0:BW],
                        func=AF.Exp, scale=SOFTCAP),
                        waits=[th, prev_p4[par]], inc="sp")
                    prev_u4[par] = ex
                    if b + 1 < NB:
                        stt_toks[b + 1] = emit_stt(b + 1)
                    if b + 2 < NB:
                        qk_toks[b + 2] = emit_qk(b + 2, [stt_toks[b + 1]])
                    if 4 * b >= NSKIP:
                        mo = (4 * b - NSKIP) * TW
                        mk = P.emit("v", lambda e, mo=mo, BW=BW, par=par:
                                    e.tensor_mul(
                            p4b[par][:, 0:BW], p4b[par][:, 0:BW],
                            mbuf[:, mo: mo + BW]),
                            waits=[ex, m_tok], inc="vp")
                        last_mask = mk
                    else:
                        mk = ex
                    pv = None
                    for i in range(4):
                        s = 4 * b + i
                        pv = P.emit("t", lambda e, s=s, i=i, g=g, TW=TW, b=b,
                                    NB=NB, par=par: e.matmul(
                            pf4[:, 0:TW],
                            vtm[:, s * GW + g * 128: s * GW + g * 128 + 128],
                            p4b[par][:, i * TW:(i + 1) * TW],
                            start=(b == 0 and i == 0),
                            stop=(b == NB - 1 and i == 3)),
                            waits=[mk] + ([prev_ctx] if b == 0 and i == 0
                                          else []),
                            inc=None)
                    dn = None
                    for i in range(4):
                        dn = P.emit("t", lambda e, i=i, TW=TW, b=b, NB=NB,
                                    par=par: e.matmul(
                            pf5[0:1, 0:TW], self.sb["ones_col_sb"][:],
                            p4b[par][:, i * TW:(i + 1) * TW],
                            start=(b == 0 and i == 0),
                            stop=(b == NB - 1 and i == 3)),
                            waits=[prev_den] if b == 0 and i == 0 else [],
                            inc="tp" if i == 3 else None)
                    prev_p4[par] = dn
                    self.emit_quant_ops(1)
                prev_stt = stt_toks[NB - 1]
                dc_ = P.emit("v", lambda e, TW=TW: e.tensor_copy(
                    out=dsb[:, 0:TW], in_=pf5[0:1, 0:TW]),
                    waits=[prev_p4[(NB - 1) % 2], prev_dsb], inc="vp")
                prev_den = dc_
                rcd = P.emit("v", lambda e, TW=TW: e.reciprocal(
                    dsb[:, 0:TW], dsb[:, 0:TW]), waits=[dc_], inc="vp")
                bc = P.emit("t", lambda e, TW=TW: e.matmul(
                    pq4[:, 0:TW], orf[:], dsb[:, 0:TW],
                    start=True, stop=True),
                    waits=[rcd, prev_stt, self.cl], inc="tp")
                ev = P.emit("s", lambda e, h=h, qoff=qoff, TW=TW: e.activation(
                    out=ctxb[:, h * OWN + qoff: h * OWN + qoff + TW],
                    in_=pf4[:, 0:TW], func=AF.Copy),
                    waits=[prev_p4[(NB - 1) % 2]], inc="sp")
                prev_ctx = ev
                dv = P.emit("v", lambda e, h=h, qoff=qoff, TW=TW:
                            e.tensor_tensor(
                    ctxb[:, h * OWN + qoff: h * OWN + qoff + TW],
                    ctxb[:, h * OWN + qoff: h * OWN + qoff + TW],
                    pq4[:, 0:TW], ALU.mult),
                    waits=[bc, ev], inc="vp")
                prev_dsb = bc
                prev_stt = dv
                self.ctx_done.append(dv)
            msl.done(last_mask)
        if self.taps:
            P.dma("y", lambda e: e.dma_start(out=self.ap["t_ctx"], in_=ctxb[:]),
                  waits=[self.ctx_done[-1]], inc="tap")
            self.tap_kv = ("tap", P.cnt["tap"])
        self.close_arena("L2")
        self.close_arena("Lqf")
        self.close_arena("Lkv")

    def wo_ln2(self):
        c, P = self.cfg, self.P
        D, DC, OWN, H = c["D"], c["DC"], c["OWN"], c["H"]
        NSLOT, NTC = c["NSLOT"], (OWN + 511) // 512
        pf = self.pf
        ctxb = self.sb["ctxb"]
        gate = [self.ctx_done[-1]]
        if self.taps and self.tap_kv:
            gate.append(self.tap_kv)

        ln2x = self.sbuf("L3b", "ln2x", [128, DC * OWN], BF16)
        x1 = self.sbuf("L3", "x1", [128, DC * OWN], F32)
        xo = self.sbuf("L3", "xw", [128, D], F32)
        wob = [self.sbuf("L3", f"wob{i}", [128, H * 128], BF16) for i in (0, 1)]
        x1b = self.sbuf("L3", "x1b", [128, 512], BF16)
        x1q = self.sbuf("L3", "x1q", [128, 512], BF16)
        smm = self.sbuf("L3", "smm", [1, 512], F32)
        smv = self.sbuf("L3", "smv", [1, 512], F32)
        smr = self.sbuf("L3", "smr", [1, 512], F32)
        tmp = self.sbuf("L3", "tmpw", [128, 512], F32)

        xsl = Slot(P, "xw0")
        wsl = [Slot(P, "wol0"), Slot(P, "wol1")]
        orf = self.sb["orf_sb"]

        prev_pf5 = None
        prev_ev = None
        for k in range(NSLOT):
            tlx = xsl.load("g", lambda e, k=k: e.dma_start(
                out=xo[:], in_=self.ap["x_own"][k * 128:(k + 1) * 128, :]),
                extra=gate if k == 0 else [])
            for j in range((DC + 3) // 4):
                dcs = list(range(4 * j, min(4 * j + 4, DC)))
                n4 = len(dcs)
                tt = None
                for ii, dc in enumerate(dcs):
                    tt = P.emit("t", lambda e, dc=dc, ii=ii: e.transpose(
                        pf[5][:, ii * 128:(ii + 1) * 128],
                        xo[:, dc * 128:(dc + 1) * 128],
                        self.sb["ident_f_sb"][:]),
                        waits=([tlx, prev_pf5, self.cl] if ii == 0 else []),
                        inc="tp" if ii == n4 - 1 else None)
                ev = P.emit("s", lambda e, k=k, j=j, n4=n4: e.copy(
                    out=x1[:].rearrange("p (dc n) -> p dc n", n=OWN)
                        [:, 4 * j:4 * j + n4, k * 128:(k + 1) * 128],
                    in_=pf[5][:].rearrange("p (q n) -> p q n", n=128)[:, 0:n4, :]),
                    waits=[tt, prev_ev], inc="sp")
                prev_pf5 = ev
                prev_ev = ev
            xsl.done(prev_ev)
        xfm_tok = prev_ev
        self.x1st = Slot(P, "x1st")
        ln2_done = []
        prev_bank = [None, None]
        prev_statmm = [None, None]
        prev_apply = None
        prev_x1bq = None
        prev_small = None
        for tc in range(NTC):
            off = tc * 512
            tw = min(512, OWN - off)
            bank = tc % 2
            stat_mm = [None, None]
            for dc in range(DC):
                par = dc % 2
                twl = wsl[par].load("g", lambda e, dc=dc, par=par: e.dma_start(
                    out=wob[par][:].rearrange("p (hc n) -> p hc n", n=128),
                    in_=self.ap["wo"][:, dc * 128:(dc + 1) * 128]
                        .rearrange("(hc p) n -> p hc n", p=128)),
                    extra=gate if (tc == 0 and dc < 2) else [])
                mmt = None
                for hc in range(H):
                    mmt = P.emit("t", lambda e, bank=bank, par=par, hc=hc, off=off,
                                 tw=tw: e.matmul(
                        pf[bank][:, 0:tw],
                        wob[par][:, hc * 128:(hc + 1) * 128],
                        ctxb[:, hc * OWN + off: hc * OWN + off + tw],
                        start=(hc == 0), stop=(hc == H - 1)),
                        waits=([twl, prev_bank[bank], xfm_tok] + gate)
                              if hc == 0 else [],
                        inc="tp" if hc == H - 1 else None)
                wsl[par].done(mmt)
                stt = P.emit("v", lambda e, bank=bank, dc=dc, off=off, tw=tw:
                             e.scalar_tensor_tensor(
                    x1[:, dc * OWN + off: dc * OWN + off + tw],
                    pf[bank][:, 0:tw], self.sb["bo_fm_sb"][:, dc:dc + 1],
                    x1[:, dc * OWN + off: dc * OWN + off + tw],
                    ALU.add, ALU.add),
                    waits=[mmt, self.cl], inc="vp")
                prev_bank[bank] = stt
                c1 = P.emit("s", lambda e, dc=dc, off=off, tw=tw: e.activation(
                    out=x1b[:, 0:tw],
                    in_=x1[:, dc * OWN + off: dc * OWN + off + tw],
                    func=AF.Copy), waits=[stt, prev_x1bq], inc="sp")
                c2 = P.emit("s", lambda e, dc=dc, off=off, tw=tw: e.activation(
                    out=x1q[:, 0:tw],
                    in_=x1[:, dc * OWN + off: dc * OWN + off + tw],
                    func=AF.Square), waits=[stt], inc="sp")
                m1 = P.emit("t", lambda e, dc=dc, tw=tw: e.matmul(
                    pf[2][0:1, 0:tw], self.sb["ones_col_sb"][:], x1b[:, 0:tw],
                    start=(dc == 0), stop=(dc == DC - 1)),
                    waits=[c1] + ([prev_statmm[0]] if dc == 0 else []), inc="tp")
                m2 = P.emit("t", lambda e, dc=dc, tw=tw: e.matmul(
                    pf[3][0:1, 0:tw], self.sb["ones_col_sb"][:], x1q[:, 0:tw],
                    start=(dc == 0), stop=(dc == DC - 1)),
                    waits=[c2] + ([prev_statmm[1]] if dc == 0 else []), inc="tp")
                stat_mm = [m1, m2]
                prev_x1bq = m2
                self.emit_quant_ops(2)
            s1 = P.emit("v", lambda e, tw=tw: e.tensor_scalar(
                smm[:, 0:tw], pf[2][0:1, 0:tw], 1.0 / D, None, ALU.mult),
                waits=[stat_mm[0], prev_small], inc="vp")
            s2 = P.emit("v", lambda e, tw=tw: e.tensor_scalar(
                smv[:, 0:tw], pf[3][0:1, 0:tw], 1.0 / D, LN_EPS,
                ALU.mult, ALU.add), waits=[stat_mm[1]], inc="vp")
            s3 = P.emit("v", lambda e, tw=tw: e.tensor_mul(
                smr[:, 0:tw], smm[:, 0:tw], smm[:, 0:tw]), waits=[s1], inc="vp")
            s4 = P.emit("v", lambda e, tw=tw: e.tensor_sub(
                smv[:, 0:tw], smv[:, 0:tw], smr[:, 0:tw]), waits=[s2, s3], inc="vp")
            s5 = P.emit("s", lambda e, tw=tw: e.activation(
                out=smr[:, 0:tw], in_=smv[:, 0:tw], func=AF.Ln),
                waits=[s4], inc="sp")
            s6 = P.emit("s", lambda e, tw=tw: e.activation(
                out=smr[:, 0:tw], in_=smr[:, 0:tw], func=AF.Exp, scale=-0.5),
                waits=[s5], inc="sp")
            b1 = P.emit("t", lambda e, tw=tw: e.matmul(
                pf[2][:, 0:tw], orf[:], smm[:, 0:tw], start=True, stop=True),
                waits=[s6, s1], inc="tp")
            b2 = P.emit("t", lambda e, tw=tw: e.matmul(
                pf[3][:, 0:tw], orf[:], smr[:, 0:tw], start=True, stop=True),
                waits=[s6], inc="tp")
            prev_small = s6
            ap_t = prev_apply
            for dc in range(DC):
                a1 = P.emit("v", lambda e, dc=dc, off=off, tw=tw: e.tensor_tensor(
                    tmp[:, 0:tw], x1[:, dc * OWN + off: dc * OWN + off + tw],
                    pf[2][:, 0:tw], ALU.subtract),
                    waits=[b1, prev_bank[0], prev_bank[1], ap_t], inc="vp")
                ap_t = P.emit("v", lambda e, dc=dc, off=off, tw=tw: e.tensor_tensor(
                    ln2x[:, dc * OWN + off: dc * OWN + off + tw],
                    tmp[:, 0:tw], pf[3][:, 0:tw], ALU.mult),
                    waits=[b2, a1], inc="vp")
                ln2_done.append(ap_t)
            prev_apply = ap_t
            prev_statmm = [ap_t, ap_t]
        x1_stores = []
        for dc in range(DC):
            tk = self.x1st.load("y", lambda e, dc=dc: e.dma_start(
                out=self.ap["x1s"][dc * 128:(dc + 1) * 128, :],
                in_=x1[:, dc * OWN:(dc + 1) * OWN]), extra=[ln2_done[-1]])
            x1_stores.append(tk)
        self.ln2_done = ln2_done
        if self.taps:
            P.sem("tap")
            P.dma("y", lambda e: e.dma_start(out=self.ap["t_ctx"], in_=ctxb[:]),
                  waits=[self.ctx_done[-1]], inc="tap")
            for dc in range(DC):
                P.dma("y", lambda e, dc=dc: e.dma_start(
                    out=self.ap["t_x1"][dc * 128:(dc + 1) * 128, :],
                    in_=x1[:, dc * OWN:(dc + 1) * OWN]),
                    waits=[ln2_done[-1]], inc="tap")
            P.dma("y", lambda e: e.dma_start(out=self.ap["t_ln2"], in_=ln2x[:]),
                  waits=[ln2_done[-1]], inc="tap")
        self.close_arena("Lctx")
        self.tap_x1_gate = ("tap", P.cnt.get("tap", 0)) if self.taps else None
        self.close_arena("L3")

    def ffn(self):
        c, P = self.cfg, self.P
        D, DC, OWN, HFF, NH2 = c["D"], c["DC"], c["OWN"], c["HFF"], c["NH2"]
        NTC = (OWN + 511) // 512
        NST = c["NSTRIP"]
        pq4 = self.pq4
        pf4, pf5 = self.pf[4], self.pf[5]
        ln2x = self.sb["ln2x"]
        orf = self.sb["orf_sb"]
        DBW = min(512, D)
        NDB = D // DBW
        assert NDB * DBW == D
        WB2 = self.qWB2
        NRS = 2 * DC

        hn = self.sbuf("L4", "hn", [128, NH2 * OWN], BF16)
        ring1 = self.sbuf("L4", "ring1", [128, DC * 512], BF16,
                          is_transpose_dest=True)
        ring2 = self.sbuf("L4c", "ring2", [128, DC * 512], BF16,
                          is_transpose_dest=True)

        def rslice(sl, w=512):
            if sl < DC:
                return ring1[:, sl * 512: sl * 512 + w]
            return ring2[:, (sl - DC) * 512: (sl - DC) * 512 + w]

        def pbank(i, tw=512):
            if i < 4:
                return pq4[:, i * 512: i * 512 + tw]
            return (pf4 if i == 4 else pf5)[:, 0:tw]
        rsl = [Slot(P, f"rg{i}") for i in range(NRS)]
        P.sem("outst0")
        P.sem("outst1")
        x1full = ("x1st", P.cnt["x1st"])
        gate0 = [x1full, self.ln2_done[-1]]
        if self.taps:
            gate0.append(self.tap_x1_gate)
        self.qstore_q = "g"

        hn_last = None
        prev_bank = [None] * 6
        rot = 0
        prev_evac = {}
        for sidx in range(NST):
            j2, half = sidx // 2, sidx % 2
            r0 = half * HFF + j2 * 512
            base = (sidx % 2) * DC
            qts = []
            for rb in range(4):
                key = ("w1h", (r0 + rb * 128) // 128)
                while key not in self.qtok:
                    self.emit_quant_ops(4)
                qts.append(self.qtok[key])
            ltoks = []
            for dc in range(DC):
                sl = base + dc
                lt = rsl[sl].load("y",
                                  lambda e, r0=r0, dc=dc, sl=sl: e.dma_start(
                    out=rslice(sl),
                    in_=self.ap["w1h"][r0:r0 + 512, dc * 128:(dc + 1) * 128],
                    transpose=True),
                    extra=qts + (gate0 if dc < 2 else []))
                ltoks.append(lt)
            strip_evacs = []
            for hc in range(4):
                hcg = j2 * 4 + hc
                for tc in range(NTC):
                    off = tc * 512
                    tw = min(512, OWN - off)
                    bank = rot % 6
                    rot += 1
                    mmt = None
                    for dc in range(DC):
                        mmt = P.emit("t", lambda e, base=base, hc=hc, dc=dc,
                                     off=off, tw=tw, bank=bank: e.matmul(
                            pbank(bank, tw),
                            rslice(base + dc)[:, hc * 128: hc * 128 + 128],
                            ln2x[:, dc * OWN + off: dc * OWN + off + tw],
                            start=(dc == 0), stop=(dc == DC - 1)),
                            waits=[ltoks[dc]] + ([prev_bank[bank],
                                    self.ln2_done[-1]] if dc == 0 else []),
                            inc="tp" if dc == DC - 1 else None)
                    if half == 0:
                        evt = P.emit("s", lambda e, hcg=hcg, off=off, tw=tw,
                                     bank=bank: e.activation(
                            out=hn[:, hcg * OWN + off: hcg * OWN + off + tw],
                            in_=pbank(bank, tw),
                            func=AF.Silu),
                            waits=[mmt, prev_evac.get((hcg, tc))], inc="sp")
                    else:
                        evt = P.emit("v", lambda e, hcg=hcg, off=off, tw=tw,
                                     bank=bank: e.tensor_mul(
                            hn[:, hcg * OWN + off: hcg * OWN + off + tw],
                            hn[:, hcg * OWN + off: hcg * OWN + off + tw],
                            pbank(bank, tw)),
                            waits=[mmt, prev_evac.get((hcg, tc))], inc="vp")
                        hn_last = evt
                    prev_bank[bank] = evt
                    prev_evac[(hcg, tc)] = evt
                    strip_evacs.append(evt)
                    self.emit_quant_ops(2)
            for dc in range(DC):
                rsl[base + dc].done(strip_evacs[-1])
        self.emit_quant(len(self.qsched))
        if self.taps and self.taps >= 3:
            P.dma("y", lambda e: e.dma_start(out=self.ap["t_hn"], in_=hn[:]),
                  waits=[hn_last], inc="tap")
        self.close_arena("L4c")

        sq0 = self.sbuf("L4b", "sq0", [128, OWN], BF16)
        sq = [sq0, sq0]
        srr = self.sbuf("L4b", "srr", [1, 512], F32)
        rbc = self.sbuf("L4b", "rbc", [128, OWN], F32)
        obuf = [self.sbuf("L4b", f"obuf{i}", [128, 512], F32) for i in (0, 1)]
        x1l = [self.sbuf("L4b", f"x1l{i}", [128, OWN], F32) for i in (0, 1)]
        x1sl = [Slot(P, "x1l0"), Slot(P, "x1l1")]

        prev_sq = [None, None]
        smm = [None, None]
        for hc in range(NH2):
            par = hc % 2
            a1 = P.emit("s", lambda e, hc=hc, par=par: e.activation(
                out=sq[par][:], in_=hn[:, hc * OWN:(hc + 1) * OWN],
                func=AF.Square),
                waits=[hn_last, prev_sq[0], prev_sq[1]], inc="sp")
            for tc in range(NTC):
                off = tc * 512
                tw = min(512, OWN - off)
                pfr = pf4 if tc == 0 else pf5
                mm = P.emit("t", lambda e, par=par, off=off, tw=tw, hc=hc,
                            pfr=pfr: e.matmul(
                    pfr[0:1, 0:tw], self.sb["ones_col_sb"][:],
                    sq[par][:, off:off + tw],
                    start=(hc == 0), stop=(hc == NH2 - 1)),
                    waits=[a1] + ([prev_bank[4], prev_bank[5]]
                                  if hc == 0 else []), inc="tp")
                smm[tc] = mm
            prev_sq[0] = prev_sq[1] = smm[NTC - 1]
        rb_done = []
        prev_r = None
        for tc in range(NTC):
            off = tc * 512
            tw = min(512, OWN - off)
            pfr = pf4 if tc == 0 else pf5
            s1 = P.emit("v", lambda e, tw=tw, pfr=pfr: e.tensor_scalar(
                srr[:, 0:tw], pfr[0:1, 0:tw], 1.0 / HFF, RMS_EPS,
                ALU.mult, ALU.add), waits=[smm[tc], prev_r], inc="vp")
            s2 = P.emit("s", lambda e, tw=tw: e.activation(
                out=srr[:, 0:tw], in_=srr[:, 0:tw], func=AF.Ln),
                waits=[s1], inc="sp")
            s3 = P.emit("s", lambda e, tw=tw: e.activation(
                out=srr[:, 0:tw], in_=srr[:, 0:tw], func=AF.Exp, scale=-0.5),
                waits=[s2], inc="sp")
            b1 = P.emit("t", lambda e, tw=tw, tc=tc: e.matmul(
                pq4[:, tc * 512: tc * 512 + tw], orf[:], srr[:, 0:tw],
                start=True, stop=True),
                waits=[s3, s1, prev_bank[tc]], inc="tp")
            cp = P.emit("v", lambda e, off=off, tw=tw, tc=tc: e.tensor_copy(
                out=rbc[:, off:off + tw], in_=pq4[:, tc * 512: tc * 512 + tw]),
                waits=[b1], inc="vp")
            prev_r = cp
            rb_done.append(cp)
        prev_acc = [prev_bank[i] for i in range(4)]
        prev_ob = [None, None]
        tile_ctr = 0
        DB2 = min(256, D)
        ND2 = D // DB2
        nds = DB2 // 128
        for db in range(ND2):
            r0 = db * DB2
            txs = {}
            for ds in range(nds):
                dci = db * nds + ds
                gext = ([x1full, hn_last] + prev_bank) if db == 0 else [x1full]
                txs[ds] = x1sl[dci % 2].load(
                    "g", lambda e, dci=dci: e.dma_start(
                        out=x1l[dci % 2][:],
                        in_=self.ap["x1s"][dci * 128:(dci + 1) * 128, :]),
                    extra=gext)
            last_mm = [None] * 4
            for hc in range(NH2):
                sl = tile_ctr % DC
                tile_ctr += 1
                qks = [self.qtok[("w2h", rb, (hc * 128) // WB2)]
                       for rb in range(r0 // 128, (r0 + DB2) // 128)]
                lt = rsl[sl].load("y",
                                  lambda e, r0=r0, hc=hc, sl=sl, DB2=DB2:
                                  e.dma_start(
                    out=rslice(sl, DB2),
                    in_=self.ap["w2h"][r0:r0 + DB2, hc * 128:(hc + 1) * 128],
                    transpose=True), extra=qks)
                for ds in range(nds):
                    for tc in range(NTC):
                        off = tc * 512
                        tw = min(512, OWN - off)
                        bank = ds * NTC + tc
                        mm = P.emit("t", lambda e, sl=sl, ds=ds, hc=hc,
                                    off=off, tw=tw, bank=bank: e.matmul(
                            pq4[:, bank * 512: bank * 512 + tw],
                            rslice(sl)[:, ds * 128: ds * 128 + 128],
                            hn[:, hc * OWN + off: hc * OWN + off + tw],
                            start=(hc == 0), stop=(hc == NH2 - 1)),
                            waits=[lt] + ([prev_acc[bank], hn_last,
                                           rb_done[-1]] if hc == 0 else []),
                            inc="tp" if (hc == NH2 - 1 or
                                         (ds == nds - 1 and tc == NTC - 1))
                                else None)
                        last_mm[bank] = mm
                rsl[sl].done(last_mm[(nds - 1) * NTC + NTC - 1])
            for ds in range(nds):
                dci = db * nds + ds
                for tc in range(NTC):
                    off = tc * 512
                    tw = min(512, OWN - off)
                    bank = ds * NTC + tc
                    opar = (dci * NTC + tc) % 2
                    e1 = P.emit("v", lambda e, bank=bank, off=off, tw=tw,
                                opar=opar: e.tensor_tensor(
                        obuf[opar][:, 0:tw], pq4[:, bank * 512: bank * 512 + tw],
                        rbc[:, off:off + tw], ALU.mult),
                        waits=[last_mm[bank], rb_done[-1], prev_ob[opar]],
                        inc="vp")
                    e2 = P.emit("v", lambda e, dci=dci, off=off, tw=tw,
                                opar=opar: e.tensor_add(
                        obuf[opar][:, 0:tw], obuf[opar][:, 0:tw],
                        x1l[dci % 2][:, off:off + tw]),
                        waits=[txs[ds], e1], inc="vp")
                    ot = P.dma("g", lambda e, dci=dci, off=off, tw=tw,
                               opar=opar: e.dma_start(
                        out=self.ap["out_fm"][dci * 128:(dci + 1) * 128,
                                              off:off + tw],
                        in_=obuf[opar][:, 0:tw]),
                        waits=[e2], inc=f"outst{opar}")
                    prev_ob[opar] = ot
                    prev_acc[bank] = e1
                x1sl[dci % 2].done(prev_ob[(dci * NTC + NTC - 1) % 2])
        final_waits = [("outst0", P.cnt["outst0"]),
                       ("outst1", P.cnt["outst1"])]
        if self.taps:
            final_waits.append(("tap", P.cnt["tap"]))
        P.emit("y", lambda e: e.nop(), waits=final_waits)

def build(cfg, taps=False, stop_after=None):
    return Kern(cfg, taps=taps, stop_after=stop_after).build()


def kernel(L0_attn_hidden_states, L0_attn_Wq, L0_attn_Wk, L0_attn_Wv,
           L0_attn_Wo, L0_attn_bo, L0_ffn_w_gate_up, L0_ffn_w_proj):
    from concourse.bass_utils import run_bass_kernel_spmd
    cfg = REAL
    ins = dict(
        x=np.asarray(L0_attn_hidden_states, dtype=np.float32),
        Wq=np.asarray(L0_attn_Wq, dtype=np.float32),
        Wk=np.asarray(L0_attn_Wk, dtype=np.float32),
        Wv=np.asarray(L0_attn_Wv, dtype=np.float32),
        Wo=np.asarray(L0_attn_Wo, dtype=np.float32),
        bo=np.asarray(L0_attn_bo, dtype=np.float32),
        w_gate_up=np.asarray(L0_ffn_w_gate_up, dtype=np.float32),
        w_proj=np.asarray(L0_ffn_w_proj, dtype=np.float32),
    )
    in_maps = host_inputs(cfg, ins)
    nc = build(cfg, taps=False)
    res = run_bass_kernel_spmd(nc, in_maps, core_ids=list(range(8)))
    return assemble(cfg, res.results)


# version 06 (final)
# speedup vs baseline: 2.0523x; 1.0595x over previous
import numpy as np
import ml_dtypes
from contextlib import ExitStack

import concourse.bass as bass
import concourse.mybir as mybir

F32 = mybir.dt.float32
BF16 = mybir.dt.bfloat16
AF = mybir.ActivationFunctionType
ALU = mybir.AluOpType
AX = mybir.AxisListType
BF = ml_dtypes.bfloat16

MAGIC = float(1.5 * 2 ** 23)
CQ = 0.501953125
SOFTCAP = 30.0
ROPE_THETA = 10000.0
LN_EPS = 1e-5
RMS_EPS = float(np.finfo(np.float32).eps)
NEG = -1e30

REAL = dict(T=4096, D=2048, H=16, G=4, HFF=8192)
MINI = dict(T=1024, D=256, H=4, G=2, HFF=512)
MIDI2 = dict(T=1024, D=1536, H=4, G=2, HFF=512, NRING_CAP=2)
MIDI4 = dict(T=4096, D=256, H=2, G=1, HFF=512)


def derived(cfg):
    d = dict(cfg)
    d["DC"] = cfg["D"] // 128
    d["NQT"] = cfg["T"] // 128
    d["NSLOT"] = d["NQT"] // 4
    d["NA_SLOTS"] = d["NSLOT"] // 2
    d["NB_SLOTS"] = d["NSLOT"] - d["NA_SLOTS"]
    d["QWA"] = 128 * d["NA_SLOTS"]
    d["QWB"] = 128 * d["NB_SLOTS"]
    d["NTA"] = 4 * d["NA_SLOTS"]
    d["NTB"] = 4 * d["NSLOT"]
    d["OWN"] = 128 * d["NSLOT"]
    d["HB"] = cfg["H"] // 2
    d["W1R"] = 2 * cfg["HFF"]
    d["NG1"] = cfg["D"] // 64
    d["NG2"] = cfg["HFF"] // 64
    d["NSTRIP"] = d["W1R"] // 512
    d["NH2"] = cfg["HFF"] // 128
    d["REP"] = cfg["H"] // cfg["G"]
    return d


def own_qtiles(cfg, lane):
    d = derived(cfg)
    return [lane + 4 * k for k in range(d["NSLOT"])]


def qcols(cfg, lane, qc):
    d = derived(cfg)
    slots = (range(0, d["NA_SLOTS"]) if qc == 0
             else range(d["NA_SLOTS"], d["NSLOT"]))
    cols = []
    for k in slots:
        t = lane + 4 * k
        cols.extend(range(t * 128, t * 128 + 128))
    return np.array(cols, dtype=np.int64)



def host_inputs(cfg, ins):
    d = derived(cfg)
    T, D, H, G, HFF = cfg["T"], cfg["D"], cfg["H"], cfg["G"], cfg["HFF"]
    x = np.ascontiguousarray(np.asarray(ins["x"], dtype=np.float32))
    assert x.shape == (2, T, D)

    inv = (1.0 / ROPE_THETA ** (np.arange(0, 128, 2, dtype=np.float64) / 128))
    pos = np.arange(T, dtype=np.float64)
    ang = pos[None, :] * np.repeat(inv, 2)[:, None]
    cos_fm = np.cos(ang).astype(np.float32)
    sin_fm = np.sin(ang).astype(np.float32)

    ident = np.eye(128, dtype=BF)
    ident_f = np.eye(128, dtype=np.float32)
    ones_col = np.ones((128, 1), dtype=BF)
    ones_row_f = np.ones((1, 128), dtype=np.float32)
    PT = np.zeros((128, 128), dtype=np.float32)
    for i in range(64):
        PT[2 * i, 2 * i + 1] = 1.0
        PT[2 * i + 1, 2 * i] = -1.0
    PT = PT.astype(BF)

    CSv = 1.0 / (np.float64(30.0) * np.sqrt(np.float64(128.0)))
    slopes = [np.float64(np.float32(2.0) ** np.float32(-(h + 1) * 8.0 / H))
              for h in range(H)]
    NBB = d["NTB"] // 4
    abias = np.zeros((128, H * NBB), dtype=np.float32)
    for h in range(H):
        for b_ in range(NBB):
            abias[:, h * NBB + b_] = np.float32(CSv * slopes[h] * (b_ * 512.0))

    bo_fm = np.ascontiguousarray(
        np.asarray(ins["bo"], dtype=np.float32).reshape(d["DC"], 128).T)

    in_maps = []
    for c in range(8):
        b, lane = c // 4, c % 4
        qa, qb = qcols(cfg, lane, 0), qcols(cfg, lane, 1)
        qall = np.concatenate([qa, qb])
        p = np.arange(128, dtype=np.int64)
        MA = np.zeros((d["NTA"], 128, d["QWA"]), dtype=BF)
        for t in range(d["NTA"]):
            MA[t] = ((t * 128 + p)[:, None] <= qa[None, :]).astype(BF)
        NMB = d["NTB"] - d["NTA"]
        MB = np.zeros((NMB, 128, d["QWB"]), dtype=BF)
        for t in range(NMB):
            MB[t] = (((t + d["NTA"]) * 128 + p)[:, None] <= qb[None, :]).astype(BF)
        DQB_A = np.zeros((128, 4 * d["QWA"]), dtype=np.float32)
        DQB_B = np.zeros((128, 4 * d["QWB"]), dtype=np.float32)
        for i in range(4):
            DQB_A[:, i * d["QWA"]:(i + 1) * d["QWA"]] = \
                (p[:, None] - qa[None, :] + 128 * i)
            DQB_B[:, i * d["QWB"]:(i + 1) * d["QWB"]] = \
                (p[:, None] - qb[None, :] + 128 * i)
        im = dict(
            x_all=x[b],
            wq=np.asarray(ins["Wq"], dtype=np.float32),
            wk=np.asarray(ins["Wk"], dtype=np.float32),
            wv=np.asarray(ins["Wv"], dtype=np.float32),
            wo=np.asarray(ins["Wo"], dtype=np.float32),
            bo_fm=bo_fm,
            w1=np.asarray(ins["w_gate_up"], dtype=np.float32),
            w2=np.asarray(ins["w_proj"], dtype=np.float32),
            cos_fm=cos_fm, sin_fm=sin_fm,
            cos_own=np.ascontiguousarray(cos_fm[:, qall]),
            sin_own=np.ascontiguousarray(sin_fm[:, qall]),
            ident=ident, ident_f=ident_f, ones_col=ones_col,
            ones_row_f=ones_row_f, rot_pt=PT,
            x_own=np.ascontiguousarray(x[b][qall, :]),
            abias=abias,
            ma=MA.reshape(d["NTA"] * 128, d["QWA"]),
            mb=MB.reshape(NMB * 128, d["QWB"]),
            dqb_a=DQB_A, dqb_b=DQB_B,
        )
        in_maps.append(im)
    return in_maps


def assemble(cfg, results):
    d = derived(cfg)
    T, D = cfg["T"], cfg["D"]
    out = np.empty((2, T, D), dtype=np.float32)
    for c in range(8):
        b, lane = c // 4, c % 4
        qall = np.concatenate([qcols(cfg, lane, 0), qcols(cfg, lane, 1)])
        out[b, qall, :] = results[c]["out_fm"].T
    return out



class PB:

    def __init__(self, nc, ctx):
        self.nc = nc
        self.ctx = ctx
        self.q = {k: [] for k in "tsvgy"}
        self.semh = {}
        self.cnt = {}

    def sem(self, name):
        if name not in self.semh:
            self.semh[name] = self.ctx.enter_context(self.nc.semaphore(name))
            self.cnt[name] = 0
        return name

    def emit(self, e, fn, waits=(), inc=None, k=1):
        ws = tuple(t for t in waits if t is not None and t[1] > 0)
        tok = None
        if inc is not None:
            self.sem(inc)
            self.cnt[inc] += k
            tok = (inc, self.cnt[inc])
        self.q[e].append((fn, ws, inc, k))
        return tok

    def dma(self, e, fn, waits=(), inc=None):
        return self.emit(e, fn, waits, inc, 16)

    def replay(self, block):
        semh = self.semh

        def runner(e):
            ops = self.q[e]

            def body(eng):
                clock = {}
                for fn, ws, inc, k in ops:
                    for s, v in ws:
                        if v > clock.get(s, 0):
                            eng.wait_ge(semh[s], v)
                            clock[s] = v
                    inst = fn(eng)
                    if inc is not None:
                        inst.then_inc(semh[inc], k)
            return body

        block.tensor(runner("t"))
        block.scalar(runner("s"))
        block.vector(runner("v"))
        block.gpsimd(runner("g"))
        block.sync(runner("y"))


class Slot:

    def __init__(self, P, name):
        self.P = P
        self.name = P.sem(name)
        self.consumers = []

    def load(self, e, fn, extra=()):
        waits = tuple(self.consumers) + tuple(extra)
        self.consumers = []
        return self.P.dma(e, fn, waits, inc=self.name)

    def done(self, tok):
        if tok is not None:
            self.consumers.append(tok)


class Kern:
    def __init__(self, cfg, taps=False, stop_after=None):
        self.cfg = derived(cfg)
        self.taps = taps
        self.stop_after = stop_after
        self.nc = bass.Bass("TRN2", target_bir_lowering=False)
        self.top = ExitStack()
        self.P = PB(self.nc, self.top)
        self.sb = {}
        self.stacks = {}

    def arena(self, key):
        if key not in self.stacks:
            self.stacks[key] = ExitStack()
        return self.stacks[key]

    def close_arena(self, key):
        if key in self.stacks:
            self.stacks.pop(key).close()

    ARENA_SIDE = {"P": "left", "Q": "left", "Lkv": "left", "Lqf": "left",
                  "L3b": "left", "L3": "left", "L4": "left",
                  "L1": "right", "Lro": "right", "L1q": "right",
                  "Lctx": "right", "L2": "right", "L4b": "left", "L4c": "left"}

    def sbuf(self, key, name, shape, dtype, **kw):
        h = self.arena(key).enter_context(
            self.nc.sbuf_tensor(name, shape, dtype,
                                side=self.ARENA_SIDE[key], **kw))
        self.sb[name] = h
        return h

    def psum(self, name, shape, dtype):
        h = self.top.enter_context(self.nc.psum_tensor(name, shape, dtype))
        self.sb[name] = h
        return h

    def build(self):
        c = self.cfg
        nc = self.nc
        T, D, DC, OWN = c["T"], c["D"], c["DC"], c["OWN"]
        H, G, HFF = c["H"], c["G"], c["HFF"]

        ap = {}
        EI, EO = "ExternalInput", "ExternalOutput"
        for n_, sh, dt_, kd in [
            ("x_all", [T, D], F32, EI), ("x_own", [OWN, D], F32, EI),
            ("wq", [D, H * 128], F32, EI), ("wk", [D, G * 128], F32, EI),
            ("wv", [D, G * 128], F32, EI), ("wo", [H * 128, D], F32, EI),
            ("bo_fm", [128, DC], F32, EI),
            ("w1", [c["W1R"], D], F32, EI), ("w2", [D, HFF], F32, EI),
            ("cos_fm", [128, T], F32, EI), ("sin_fm", [128, T], F32, EI),
            ("cos_own", [128, OWN], F32, EI), ("sin_own", [128, OWN], F32, EI),
            ("ident", [128, 128], BF16, EI), ("ident_f", [128, 128], F32, EI),
            ("ones_col", [128, 1], BF16, EI), ("ones_row_f", [1, 128], F32, EI),
            ("rot_pt", [128, 128], BF16, EI),
            ("abias", [128, c["H"] * (c["NTB"] // 4)], F32, EI),
            ("dqb_a", [128, 4 * c["QWA"]], F32, EI),
            ("dqb_b", [128, 4 * c["QWB"]], F32, EI),
            ("ma", [c["NTA"] * 128, c["QWA"]], BF16, EI),
            ("mb", [(c["NTB"] - c["NTA"]) * 128, c["QWB"]], BF16, EI),
            ("out_fm", [D, OWN], F32, EO),
            ("w1h", [c["W1R"], D], BF16, "Internal"),
            ("w2h", [D, HFF], BF16, "Internal"),
            ("x1s", [D, OWN], F32, "Internal"),
        ]:
            ap[n_] = nc.dram_tensor(n_, sh, dt_, kind=kd).ap()
        if self.taps:
            for n_, sh, dt_ in [
                ("t_k", [128, G * T], BF16), ("t_v", [128, (T // 128) * G * 128], BF16),
                ("t_q", [128, H * OWN], BF16), ("t_ctx", [128, H * OWN], BF16),
                ("t_x1", [D, OWN], F32), ("t_ln2", [128, DC * OWN], BF16),
                ("t_w1h", [c["W1R"], D], BF16), ("t_w2h", [D, HFF], BF16),
                ("t_hn", [128, c["NH2"] * OWN], BF16),
            ]:
                ap[n_] = nc.dram_tensor(n_, sh, dt_, kind=EO).ap()
        self.ap = ap

        pq4 = self.psum("pq4", [128, 2048], F32)
        p4 = self.psum("pf4t", [128, 512], F32)
        p5 = self.psum("pf5t", [128, 512], F32)
        self.psum("pbA", [128, 1024], BF16)
        self.psum("pbB", [128, 1024], BF16)
        self.pf = [pq4[:, 0:512], pq4[:, 512:1024], pq4[:, 1024:1536],
                   pq4[:, 1536:2048], p4[:], p5[:]]
        self.pq4 = pq4

        self.sbuf("P", "ident_sb", [128, 128], BF16)
        self.sbuf("P", "ident_f_sb", [128, 128], F32)
        self.sbuf("P", "ones_col_sb", [128, 1], BF16)
        self.sbuf("P", "orf_sb", [1, 128], F32)
        self.sbuf("P", "rot_pt_sb", [128, 128], BF16)
        self.sbuf("P", "bo_fm_sb", [128, DC], F32)
        P = self.P
        P.sem("cl")
        for src, dst in [("ident", "ident_sb"), ("ident_f", "ident_f_sb"),
                         ("ones_col", "ones_col_sb"), ("ones_row_f", "orf_sb"),
                         ("rot_pt", "rot_pt_sb"), ("bo_fm", "bo_fm_sb")]:
            P.dma("y", (lambda e, s=src, d=dst: e.dma_start(
                out=self.sb[d][:], in_=self.ap[s])), inc="cl")
        self.cl = ("cl", P.cnt["cl"])

        self.quant_init()
        self.sbuf("Lkv", "kfm", [128, G * T], BF16)
        self.sbuf("Lkv", "vtm", [128, (T // 128) * G * 128], BF16)
        done = False
        for name, fn in [("p1a", self.phase1a), ("p1b", self.phase1b),
                         ("attn", self.attention), ("wo", self.wo_ln2),
                         ("ffn", self.ffn)]:
            fn()
            if self.stop_after == name:
                done = True
                break
        if done and self.stop_after not in ("ffn", "f1", "f2"):
            self.finish_early()

        block = self.top.enter_context(self.nc.Block())
        P.replay(block)
        for k in reversed(list(self.stacks)):
            self.close_arena(k)
        self.top.close()
        return self.nc

    def finish_early(self):
        P = self.P
        assert self.taps, "finish_early requires taps"
        P.sem("tap")
        if self.stop_after in ("p1a", "p1b") and "kfm" in self.sb:
            P.dma("y", lambda e: e.dma_start(out=self.ap["t_k"],
                                             in_=self.sb["kfm"][:]),
                  waits=[self.kv_done[-1]], inc="tap")
            P.dma("y", lambda e: e.dma_start(out=self.ap["t_v"],
                                             in_=self.sb["vtm"][:]),
                  waits=[self.kv_done[-1]], inc="tap")
        if self.stop_after == "p1b":
            P.dma("y", lambda e: e.dma_start(out=self.ap["t_q"],
                                             in_=self.sb["qfm"][:]),
                  waits=[self.q_done[-1]], inc="tap")
        if self.stop_after == "attn":
            P.dma("y", lambda e: e.dma_start(out=self.ap["t_ctx"],
                                             in_=self.sb["ctxb"][:]),
                  waits=[self.ctx_done[-1]], inc="tap")
        fw = [("tap", P.cnt["tap"])]
        for s in ("qst0", "qst1"):
            if s in P.cnt:
                fw.append((s, P.cnt[s]))
        P.emit("y", lambda e: e.nop(), waits=fw)

    def quant_init(self):
        c = self.cfg
        D, HFF, W1R = c["D"], c["HFF"], c["W1R"]
        WB2 = min(2048, HFF)
        W = max(D, WB2)
        self.qW, self.qWB2 = W, WB2
        ng = W // 64
        qw0 = self.sbuf("Q", "qw0", [128, W], BF16)
        qw1 = self.sbuf("Q", "qw1", [128, W], BF16)
        qu0 = self.sbuf("Q", "qu0", [128, W], BF16)
        self.qw = [qw0, qw1]
        self.qu = [qu0, qu0]
        self.qh = [qw0, qw1]
        qs0 = self.sbuf("Q", "qsums0", [128, ng], F32)
        qb0 = self.sbuf("Q", "qscb0", [128, ng], BF16)
        qr0 = self.sbuf("Q", "qrcp0", [128, ng], F32)
        self.qsums = [qs0, qs0]
        self.qscb = [qb0, qb0]
        self.qrcp = [qr0, qr0]
        _st = Slot(self.P, "qst0")
        _sg = Slot(self.P, "qstg0")
        self.qwsl = [Slot(self.P, "qwl0"), Slot(self.P, "qwl1")]
        self.qssl = [_st, _st]
        self.qssl_g = [_sg, _sg]
        self.qld_tok = None
        self.qld_for = -1
        sched = []
        for j in range(c["NSTRIP"] // 2):
            for half in (0, HFF):
                for rb in range(4):
                    r = half + j * 512 + rb * 128
                    sched.append(dict(src="w1", dst="w1h", key=("w1h", r // 128),
                                      r=r, cc=0, w=D))
        for r in range(0, D, 128):
            for cc in range(0, HFF, WB2):
                sched.append(dict(src="w2", dst="w2h",
                                  key=("w2h", r // 128, cc // WB2),
                                  r=r, cc=cc, w=WB2))
        self.qsched = sched
        self.qpos = 0
        self.qgps_left = max(8, min(64, len(sched) // 3))
        self.qquota = {"p1b": 8, "attn": 32, "wo": 16}
        self.qtok = {}
        self.qprev = dict(u=[[], []], small=[[], []], qh=[[], []])
        self.qidx = 0
        self.qgen = None
        self.qstore_q = "y"

    def qsched_at(self, stage):
        if stage == "all":
            return len(self.qsched) - self.qpos
        q = self.qquota.get(stage, 0)
        if q > 0:
            self.qquota[stage] = q - 1
            return 1
        return 0

    def _quant_gen(self, blk, i):
        P = self.P
        par = i % 2
        w, ng = blk["w"], blk["w"] // 64
        r, cc = blk["r"], blk["cc"]
        src, dst = self.ap[blk["src"]], self.ap[blk["dst"]]
        qw, qu, qh = self.qw[par], self.qu[par], self.qh[par]
        qsums, qscb, qrcp = self.qsums[par], self.qscb[par], self.qrcp[par]
        if self.qld_for == i:
            ld = self.qld_tok
        else:
            ld = self.qwsl[par].load("g", lambda e, qw=qw, src=src, r=r, cc=cc,
                                     w=w: e.dma_start(
                out=qw[:, 0:w], in_=src[r:r + 128, cc:cc + w]))
        red = P.emit("v", lambda e, qw=qw, qsums=qsums, w=w, ng=ng: e.reduce_sum(
            qsums[:, 0:ng].unsqueeze(2),
            qw[:, 0:w].rearrange("p (g k) -> p g k", k=64), axis=AX.X,
            apply_absolute_value=True),
            waits=[ld] + self.qprev["small"][par], inc="vp")
        yield
        sc1 = P.emit("v", lambda e, qscb=qscb, qsums=qsums, ng=ng: e.tensor_scalar(
            qscb[:, 0:ng], qsums[:, 0:ng], 1.0 / 64, None, ALU.mult),
            waits=[red], inc="vp")
        sc2 = P.emit("v", lambda e, qscb=qscb, ng=ng: e.tensor_scalar_max(
            qscb[:, 0:ng], qscb[:, 0:ng], float(np.float32(BF(1e-8)))),
            waits=[sc1], inc="vp")
        rc = P.emit("v", lambda e, qrcp=qrcp, qscb=qscb, ng=ng: e.reciprocal(
            qrcp[:, 0:ng], qscb[:, 0:ng]), waits=[sc2], inc="vp")
        u = P.emit("v", lambda e, qu=qu, qw=qw, qrcp=qrcp, w=w, ng=ng:
                   e.tensor_tensor(
            qu[:, 0:w].rearrange("p (g k) -> p g k", k=64),
            qw[:, 0:w].rearrange("p (g k) -> p g k", k=64),
            qrcp[:, 0:ng].unsqueeze(2).broadcast_to([128, ng, 64]),
            ALU.mult), waits=[rc] + self.qprev["u"][par], inc="vp")
        self.qwsl[par].done(u)
        yield
        cl_ = P.emit("v", lambda e, qu=qu, w=w: e.tensor_scalar(
            qu[:, 0:w], qu[:, 0:w], 1.0, -1.0, ALU.min, ALU.max),
            waits=[u], inc="vp")
        mg = P.emit("v", lambda e, qu=qu, w=w: e.tensor_scalar(
            qu[:, 0:w], qu[:, 0:w], MAGIC, -MAGIC, ALU.add, ALU.add),
            waits=[cl_], inc="vp")
        yield
        wh = P.emit("v", lambda e, qh=qh, qu=qu, qscb=qscb, w=w, ng=ng:
                    e.tensor_tensor(
            qh[:, 0:w].rearrange("p (g k) -> p g k", k=64),
            qu[:, 0:w].rearrange("p (g k) -> p g k", k=64),
            qscb[:, 0:ng].unsqueeze(2).broadcast_to([128, ng, 64]),
            ALU.mult), waits=[mg] + self.qprev["qh"][par], inc="vp")
        self.qprev["u"][0] = self.qprev["u"][1] = [wh]
        self.qprev["small"][0] = self.qprev["small"][1] = [u, wh]
        sq_ = self.qstore_q
        ssl = self.qssl if sq_ == "y" else self.qssl_g
        stok = ssl[par].load(sq_,
                             lambda e, qh=qh, dst=dst, r=r, cc=cc,
                             w=w: e.dma_start(
            out=dst[r:r + 128, cc:cc + w], in_=qh[:, 0:w]), extra=[wh])
        ssl[par].done(stok)
        self.qwsl[par].done(stok)
        self.qprev["qh"][0] = self.qprev["qh"][1] = [stok]
        self.qtok[blk["key"]] = stok
        if self.qpos < len(self.qsched) and self.qld_for <= i:
            nb_, ni = self.qsched[self.qpos], self.qidx
            np_ = ni % 2
            nsrc = self.ap[nb_["src"]]
            nw, nr, ncc = nb_["w"], nb_["r"], nb_["cc"]
            nqw = self.qw[np_]
            self.qld_tok = self.qwsl[np_].load(
                "g", lambda e, nqw=nqw, nsrc=nsrc, nr=nr, ncc=ncc,
                nw=nw: e.dma_start(
                    out=nqw[:, 0:nw], in_=nsrc[nr:nr + 128, ncc:ncc + nw]))
            self.qld_for = ni

    def emit_quant_ops(self, k):
        for _ in range(k):
            if self.qgen is None:
                if self.qpos >= len(self.qsched):
                    return
                blk = self.qsched[self.qpos]
                self.qpos += 1
                self.qgen = self._quant_gen(blk, self.qidx)
                self.qidx += 1
            try:
                next(self.qgen)
            except StopIteration:
                self.qgen = None

    def emit_quant(self, n, eng=None):
        done = 0
        while done < n:
            if self.qgen is None:
                if self.qpos >= len(self.qsched):
                    return
                blk = self.qsched[self.qpos]
                self.qpos += 1
                self.qgen = self._quant_gen(blk, self.qidx)
                self.qidx += 1
            for _ in self.qgen:
                pass
            self.qgen = None
            done += 1

    def ln_tile(self, x_ap, out_ap, jk, st, sc, t, NT, extra, prev):
        P, D = self.P, self.cfg["D"]
        a1 = P.emit("s", lambda e: e.activation(
            out=jk[0][:], in_=x_ap, func=AF.Copy,
            accum_out=st[:, t:t + 1]),
            waits=[prev["a1"]] + extra, inc="sp")
        a2 = P.emit("s", lambda e: e.activation(
            out=jk[1][:], in_=x_ap, func=AF.Square,
            accum_out=st[:, NT + t:NT + t + 1]),
            waits=[prev["a2"], a1], inc="sp")
        prev["a1"], prev["a2"] = a1, a2
        b = 8 * (t % 2)
        pv = prev["sc"][t % 2]
        s1 = P.emit("v", lambda e: e.tensor_scalar(
            sc[:, b + 2:b + 3], st[:, t:t + 1], 1.0 / D, None, ALU.mult),
            waits=[a1, pv], inc="vp")
        s2 = P.emit("v", lambda e: e.tensor_mul(
            sc[:, b + 3:b + 4], sc[:, b + 2:b + 3], sc[:, b + 2:b + 3]),
            waits=[s1], inc="vp")
        s3 = P.emit("v", lambda e: e.tensor_scalar(
            sc[:, b + 4:b + 5], st[:, NT + t:NT + t + 1], 1.0 / D, LN_EPS,
            ALU.mult, ALU.add), waits=[a2], inc="vp")
        s4 = P.emit("v", lambda e: e.tensor_sub(
            sc[:, b + 4:b + 5], sc[:, b + 4:b + 5], sc[:, b + 3:b + 4]),
            waits=[s2, s3], inc="vp")
        s5 = P.emit("s", lambda e: e.activation(
            out=sc[:, b + 5:b + 6], in_=sc[:, b + 4:b + 5], func=AF.Ln),
            waits=[s4], inc="sp")
        s6 = P.emit("s", lambda e: e.activation(
            out=sc[:, b + 6:b + 7], in_=sc[:, b + 5:b + 6], func=AF.Exp,
            scale=-0.5), waits=[s5], inc="sp")
        s7 = P.emit("v", lambda e: e.scalar_tensor_tensor(
            sc[:, b + 7:b + 8], sc[:, b + 2:b + 3], -1.0,
            sc[:, b + 6:b + 7], ALU.mult, ALU.mult),
            waits=[s6, s2], inc="vp")
        prev["sc"][t % 2] = s7
        ap_ = P.emit("s", lambda e: e.activation(
            out=out_ap, in_=x_ap, func=AF.Identity,
            scale=sc[:, b + 6:b + 7], bias=sc[:, b + 7:b + 8]),
            waits=[s6, s7, prev["apply"]], inc="sp")
        prev["apply"] = ap_
        return ap_

    def phase1a(self):
        c, P = self.cfg, self.P
        T, D, DC, G = c["T"], c["D"], c["DC"], c["G"]
        NT, NCH = T // 128, T // 512
        GW = G * 128

        xt = [self.sbuf("L1", f"xt{i}", [128, D], F32) for i in (0, 1)]
        ln1tm = [self.sbuf("L1", f"ln1tm{i}", [128, D], BF16) for i in (0, 1)]
        lnfm = [self.sbuf("L1", f"lnfm{i}", [128, DC * 512], BF16) for i in (0, 1)]
        csb = [self.sbuf("L1", f"csb{i}", [128, 512], F32) for i in (0, 1)]
        ssb = [self.sbuf("L1", f"ssb{i}", [128, 512], F32) for i in (0, 1)]
        jk = [self.sbuf("L1", f"jk{i}", [128, D], BF16) for i in (0, 1)]
        st = self.sbuf("L1", "st", [128, 2 * NT], F32)
        sc = self.sbuf("L1", "sc", [128, 16], F32)
        kraw = [self.sbuf("L1", f"kraw{i}", [128, 512], BF16) for i in (0, 1)]
        rt1 = self.sbuf("L1", "rt1", [128, 512], F32)
        rt2 = self.sbuf("L1", "rt2", [128, 512], F32)
        wkb = self.sbuf("L1", "wkb", [128, DC * GW], BF16)
        wvb = self.sbuf("L1", "wvb", [128, DC * GW], BF16)

        kfm, vtm = self.sb["kfm"], self.sb["vtm"]
        pf = self.pf
        pbA = self.sb["pbA"]

        P.sem("wkv")
        P.dma("g", lambda e: e.dma_start(
            out=wkb[:].rearrange("p (dc n) -> p dc n", n=GW),
            in_=self.ap["wk"].rearrange("(dc p) n -> p dc n", p=128)), inc="wkv")
        P.dma("g", lambda e: e.dma_start(
            out=wvb[:].rearrange("p (dc n) -> p dc n", n=GW),
            in_=self.ap["wv"].rearrange("(dc p) n -> p dc n", p=128)), inc="wkv")
        wkv_tok = ("wkv", 32)

        xsl = [Slot(P, "xls0"), Slot(P, "xls1")]
        cssl = [Slot(P, "csl0"), Slot(P, "csl1")]
        t_ms = P.emit("v", lambda e: e.memset(st[:], 0.0), inc="vp")

        prev = dict(a1=None, a2=None, apply=None, sc=[None, None],
                    lnfm=[[], []], kraw=[[], []], rt=None, pf=[None] * 6, pbA=[])
        kv_done = []

        for ch in range(NCH):
            par = ch % 2
            tcs = cssl[par].load("y", lambda e, ch=ch, par=par: e.dma_start(
                out=csb[par][:], in_=self.ap["cos_fm"][:, ch * 512:(ch + 1) * 512]))
            tss = cssl[par].load("y", lambda e, ch=ch, par=par: e.dma_start(
                out=ssb[par][:], in_=self.ap["sin_fm"][:, ch * 512:(ch + 1) * 512]))
            lnfm_rdy = []
            for tl in range(4):
                t = ch * 4 + tl
                tp2 = t % 2
                tlx = xsl[tp2].load("g", lambda e, t=t, tp2=tp2: e.dma_start(
                    out=xt[tp2][:], in_=self.ap["x_all"][t * 128:(t + 1) * 128, :]))
                ap_ = self.ln_tile(xt[tp2][:], ln1tm[tp2][:], jk, st, sc, t, NT,
                                   [tlx, t_ms], prev)
                xsl[tp2].done(ap_)
                nb = (DC + 7) // 8
                for j in range(nb):
                    dcs = list(range(8 * j, min(8 * j + 8, DC)))
                    n8 = len(dcs)
                    tt = None
                    for ii, dc in enumerate(dcs):
                        tt = P.emit("t", lambda e, tp2=tp2, dc=dc, ii=ii: e.transpose(
                            pbA[:, ii * 128:(ii + 1) * 128],
                            ln1tm[tp2][:, dc * 128:(dc + 1) * 128],
                            self.sb["ident_sb"][:]),
                            waits=([ap_, self.cl] + prev["pbA"]) if ii == 0 else [],
                            inc="tp" if ii == n8 - 1 else None)
                    ev = P.emit("s", lambda e, par=par, tl=tl, j=j, n8=n8:
                                e.copy(
                        out=lnfm[par][:].rearrange("p (dc n) -> p dc n", n=512)
                            [:, 8 * j:8 * j + n8, tl * 128:(tl + 1) * 128],
                        in_=pbA[:].rearrange("p (k n) -> p k n", n=128)[:, 0:n8, :]),
                        waits=[tt] + prev["lnfm"][par], inc="sp")
                    lnfm_rdy.append(ev)
                    prev["pbA"] = [ev]
                    self.emit_quant_ops(1)
            prev["lnfm"][par] = []
            for g in range(G):
                bank = g % 2
                mmt = None
                for dc in range(DC):
                    mmt = P.emit("t", lambda e, g=g, dc=dc, par=par, bank=bank:
                                 e.matmul(
                        pf[bank][:, 0:512],
                        wkb[:, dc * GW + g * 128: dc * GW + g * 128 + 128],
                        lnfm[par][:, dc * 512:(dc + 1) * 512],
                        start=(dc == 0), stop=(dc == DC - 1)),
                        waits=([wkv_tok, prev["pf"][bank]] + lnfm_rdy)
                              if dc == 0 else [],
                        inc="tp" if dc == DC - 1 else None)
                kb = g % 2
                e1 = P.emit("s", lambda e, bank=bank, kb=kb: e.activation(
                    out=kraw[kb][:], in_=pf[bank][:, 0:512], func=AF.Copy),
                    waits=[mmt] + prev["kraw"][kb], inc="sp")
                prev["pf"][bank] = e1
                rmt = P.emit("t", lambda e, kb=kb: e.matmul(
                    pf[2][:, 0:512], self.sb["rot_pt_sb"][:], kraw[kb][:],
                    start=True, stop=True),
                    waits=[e1, prev["pf"][2], self.cl], inc="tp")
                d1 = P.emit("v", lambda e, par=par, kb=kb: e.tensor_mul(
                    rt1[:], kraw[kb][:], csb[par][:]),
                    waits=[e1, tss, prev["rt"]], inc="vp")
                d2 = P.emit("v", lambda e, par=par: e.tensor_mul(
                    rt2[:], pf[2][:, 0:512], ssb[par][:]),
                    waits=[rmt, tss, prev["rt"]], inc="vp")
                prev["pf"][2] = d2
                d3 = P.emit("v", lambda e, g=g, ch=ch: e.tensor_add(
                    kfm[:, g * T + ch * 512: g * T + ch * 512 + 512],
                    rt1[:], rt2[:]), waits=[d1, d2], inc="vp")
                prev["rt"] = d3
                prev["kraw"][kb] = [rmt, d1]
                kv_done.append(d3)
                self.emit_quant_ops(1)
            cssl[par].done(kv_done[-1])
            last_v = None
            for tl in range(4):
                t = ch * 4 + tl
                mmt = None
                for dc in range(DC):
                    mmt = P.emit("t", lambda e, tl=tl, dc=dc, par=par: e.matmul(
                        pf[4][:, 0:GW],
                        lnfm[par][:, dc * 512 + tl * 128: dc * 512 + tl * 128 + 128],
                        wvb[:, dc * GW:(dc + 1) * GW],
                        start=(dc == 0), stop=(dc == DC - 1)),
                        waits=[wkv_tok, prev["pf"][4]] if dc == 0 else [],
                        inc="tp" if dc == DC - 1 else None)
                ev = P.emit("s", lambda e, t=t: e.activation(
                    out=vtm[:, t * GW:(t + 1) * GW], in_=pf[4][:, 0:GW],
                    func=AF.Copy), waits=[mmt], inc="sp")
                prev["pf"][4] = ev
                kv_done.append(ev)
                last_v = ev
                self.emit_quant_ops(1)
            prev["lnfm"][par] = [last_v]
        self.kv_done = kv_done
        self.close_arena("L1")

    def phase1b(self):
        c, P = self.cfg, self.P
        D, DC, OWN, H, T = c["D"], c["DC"], c["OWN"], c["H"], c["T"]
        NSLOT = c["NSLOT"]
        NTC = (OWN + 511) // 512
        self.sbuf("Lqf", "qfm", [128, H * OWN], BF16)
        cosq = self.sbuf("Lro", "cosq", [128, OWN], F32)
        sinq = self.sbuf("Lro", "sinq", [128, OWN], F32)
        q_in = self.sbuf("L1q", "q_in", [128, DC * OWN], BF16)
        xo = self.sbuf("L1q", "xoq", [128, D], F32)
        lntm = [self.sbuf("L1q", f"lntm{i}", [128, D], BF16) for i in (0, 1)]
        jkq = self.sbuf("L1q", "jkq0", [128, D], BF16)
        jk = [jkq, jkq]
        st = self.sbuf("L1q", "stq", [128, 2 * NSLOT], F32)
        sc = self.sbuf("L1q", "scq", [128, 16], F32)
        wqh = [self.sbuf("L1q", f"wqh{i}", [128, DC * 128], BF16) for i in (0, 1)]
        qraw = self.sbuf("L1q", "qraw", [128, OWN], BF16)
        rq1 = self.sbuf("L1q", "rq1", [128, 512], F32)
        rq2 = self.sbuf("L1q", "rq2", [128, 512], F32)
        qfm = self.sb["qfm"]
        pf = self.pf
        pbA = self.sb["pbA"]

        P.sem("roL")
        pgate = [self.kv_done[-1]]
        for src, dst in [("cos_own", cosq), ("sin_own", sinq)]:
            P.dma("y", lambda e, s=src, d=dst: e.dma_start(out=d[:], in_=self.ap[s]),
                  waits=pgate, inc="roL")
        ro_tok = ("roL", P.cnt["roL"])

        xsl = Slot(P, "xoq0")
        wsl = [Slot(P, "wql0"), Slot(P, "wql1")]
        t_ms = P.emit("v", lambda e: e.memset(st[:], 0.0), waits=pgate, inc="vp")
        prev = dict(a1=None, a2=None, apply=None, sc=[None, None], pbA=[],
                    pf=[None] * 6, qraw=[], rq=None)
        qin_rdy = []
        for k in range(NSLOT):
            par = k % 2
            tlx = xsl.load("g", lambda e, k=k: e.dma_start(
                out=xo[:], in_=self.ap["x_own"][k * 128:(k + 1) * 128, :]),
                extra=pgate if k == 0 else [])
            ap_ = self.ln_tile(xo[:], lntm[par][:], jk, st, sc, k, NSLOT,
                               [tlx, t_ms], prev)
            xsl.done(ap_)
            nb = (DC + 7) // 8
            for j in range(nb):
                dcs = list(range(8 * j, min(8 * j + 8, DC)))
                n8 = len(dcs)
                tt = None
                for ii, dc in enumerate(dcs):
                    tt = P.emit("t", lambda e, par=par, dc=dc, ii=ii: e.transpose(
                        pbA[:, ii * 128:(ii + 1) * 128],
                        lntm[par][:, dc * 128:(dc + 1) * 128],
                        self.sb["ident_sb"][:]),
                        waits=([ap_, self.cl] + prev["pbA"]) if ii == 0 else [],
                        inc="tp" if ii == n8 - 1 else None)
                ev = P.emit("s", lambda e, k=k, j=j, n8=n8: e.copy(
                    out=q_in[:].rearrange("p (dc n) -> p dc n", n=OWN)
                        [:, 8 * j:8 * j + n8, k * 128:(k + 1) * 128],
                    in_=pbA[:].rearrange("p (k n) -> p k n", n=128)[:, 0:n8, :]),
                    waits=[tt], inc="sp")
                prev["pbA"] = [ev]
                qin_rdy.append(ev)
                self.emit_quant_ops(1)
        self.q_done = []
        for h in range(H):
            par = h % 2
            tw_ = wsl[par].load("g", lambda e, h=h, par=par: e.dma_start(
                out=wqh[par][:].rearrange("p (dc n) -> p dc n", n=128),
                in_=self.ap["wq"][:, h * 128:(h + 1) * 128]
                    .rearrange("(dc p) n -> p dc n", p=128)),
                extra=pgate if h < 2 else [])
            for tc in range(NTC):
                off = tc * 512
                tww = min(512, OWN - off)
                bank = tc % 2
                mmt = None
                for dc in range(DC):
                    mmt = P.emit("t", lambda e, dc=dc, par=par, bank=bank,
                                 off=off, tww=tww: e.matmul(
                        pf[bank][:, 0:tww],
                        wqh[par][:, dc * 128:(dc + 1) * 128],
                        q_in[:, dc * OWN + off: dc * OWN + off + tww],
                        start=(dc == 0), stop=(dc == DC - 1)),
                        waits=([tw_, prev["pf"][bank]] + qin_rdy)
                              if dc == 0 else [],
                        inc="tp" if dc == DC - 1 else None)
                e1 = P.emit("s", lambda e, bank=bank, off=off, tww=tww: e.activation(
                    out=qraw[:, off:off + tww], in_=pf[bank][:, 0:tww],
                    func=AF.Copy), waits=[mmt] + prev["qraw"], inc="sp")
                prev["pf"][bank] = e1
                rmt = P.emit("t", lambda e, off=off, tww=tww: e.matmul(
                    pf[2][:, 0:tww], self.sb["rot_pt_sb"][:],
                    qraw[:, off:off + tww], start=True, stop=True),
                    waits=[e1, prev["pf"][2]], inc="tp")
                d1 = P.emit("v", lambda e, off=off, tww=tww: e.tensor_mul(
                    rq1[:, 0:tww], qraw[:, off:off + tww],
                    cosq[:, off:off + tww]),
                    waits=[e1, ro_tok, prev["rq"]], inc="vp")
                d2 = P.emit("v", lambda e, off=off, tww=tww: e.tensor_mul(
                    rq2[:, 0:tww], pf[2][:, 0:tww],
                    sinq[:, off:off + tww]), waits=[rmt, prev["rq"]], inc="vp")
                prev["pf"][2] = d2
                d3 = P.emit("v", lambda e, h=h, off=off, tww=tww: e.tensor_add(
                    qfm[:, h * OWN + off: h * OWN + off + tww],
                    rq1[:, 0:tww], rq2[:, 0:tww]),
                    waits=[d1, d2], inc="vp")
                prev["rq"] = d3
                prev["qraw"] = [rmt, d1]
                self.q_done.append(d3)
                self.emit_quant_ops(1)
            wsl[par].done(self.q_done[-1])
        self.close_arena("L1q")
        self.close_arena("Lro")

    def attention(self):
        c, P = self.cfg, self.P
        T, OWN, H, G = c["T"], c["OWN"], c["H"], c["G"]
        REP, GW = c["REP"], c["G"] * 128
        kfm, vtm, qfm = self.sb["kfm"], self.sb["vtm"], self.sb["qfm"]
        ctxb = self.sbuf("Lctx", "ctxb", [128, H * OWN], BF16)
        pq4 = self.pq4
        pf4, pf5 = self.pf[4], self.pf[5]
        CS = 1.0 / (SOFTCAP * float(np.sqrt(128.0)))
        gate = [self.q_done[-1], self.kv_done[-1]]

        NMA = c["NTA"]
        NMB = c["NTB"] - c["NTA"]
        MW = max(NMA * c["QWA"], NMB * c["QWB"])
        mbuf = self.sbuf("L2", "mbuf", [128, MW], BF16)
        u4 = [self.sbuf("L2", f"u4{i}", [128, 2048], F32) for i in (0, 1)]
        p4b = [self.sbuf("L2", f"p4b{i}", [128, 2048], BF16) for i in (0, 1)]
        dsb = self.sbuf("L2", "dsb0", [1, 512], F32)
        abias = self.sbuf("L2", "abias_sb", [128, c["H"] * (c["NTB"] // 4)], F32)
        dqba = self.sbuf("L2", "dqba", [128, 4 * c["QWA"]], F32)
        dqbb = self.sbuf("L2", "dqbb", [128, 4 * c["QWB"]], F32)
        orf = self.sb["orf_sb"]

        self.tap_kv = None
        if self.taps:
            P.sem("tap")
            P.dma("y", lambda e: e.dma_start(out=self.ap["t_k"], in_=kfm[:]),
                  waits=[self.kv_done[-1], self.q_done[-1]], inc="tap")
            P.dma("y", lambda e: e.dma_start(out=self.ap["t_v"], in_=vtm[:]),
                  waits=[self.kv_done[-1]], inc="tap")
            P.dma("y", lambda e: e.dma_start(out=self.ap["t_q"], in_=qfm[:]),
                  waits=[self.q_done[-1]], inc="tap")
            self.tap_kv = ("tap", P.cnt["tap"])
        P.sem("abl")
        P.dma("y", lambda e: e.dma_start(out=abias[:], in_=self.ap["abias"]),
              waits=gate, inc="abl")
        P.dma("y", lambda e: e.dma_start(out=dqba[:], in_=self.ap["dqb_a"]),
              inc="abl")
        P.dma("y", lambda e: e.dma_start(out=dqbb[:], in_=self.ap["dqb_b"]),
              inc="abl")
        ab_tok = ("abl", P.cnt["abl"])

        msl = Slot(P, "mal")
        slopes = [float(np.float32(2.0) ** np.float32(-(h + 1) * 8.0 / H))
                  for h in range(H)]
        NBB = c["NTB"] // 4
        self.ctx_done = []
        prev_stt = None
        prev_ctx = None
        prev_den = None
        prev_dsb = None
        prev_u4 = [None, None]
        prev_p4 = [None, None]
        for qc in (0, 1):
            TW = c["QWA"] if qc == 0 else c["QWB"]
            NTq = c["NTA"] if qc == 0 else c["NTB"]
            NB = NTq // 4
            BW = 4 * TW
            qoff = 0 if qc == 0 else c["QWA"]
            dqb = dqba if qc == 0 else dqbb
            src_m = "ma" if qc == 0 else "mb"
            NSKIP = 0 if qc == 0 else c["NTA"]
            NM = NTq - NSKIP
            m_tok = msl.load("y", lambda e, src_m=src_m, TW=TW, NM=NM:
                             e.dma_start(
                out=mbuf[:, 0:NM * TW].rearrange("p (t f) -> p t f", f=TW),
                in_=self.ap[src_m].rearrange("(t p) f -> p t f", p=128)),
                extra=gate + [ab_tok] if qc == 0 else [])
            last_mask = None
            for h in range(H):
                g = h // REP

                def emit_qk(b, first_wait):
                    mmt = None
                    for i in range(4):
                        s = 4 * b + i
                        mmt = P.emit("t", lambda e, s=s, i=i, h=h, g=g,
                                     qoff=qoff, TW=TW: e.matmul(
                            pq4[:, i * TW:(i + 1) * TW],
                            kfm[:, g * T + s * 128: g * T + s * 128 + 128],
                            qfm[:, h * OWN + qoff: h * OWN + qoff + TW],
                            start=True, stop=True),
                            waits=(first_wait + gate) if i == 0 else [],
                            inc="tp" if i == 3 else None)
                    return mmt
                qk_toks = [None] * (NB + 2)
                qk_toks[0] = emit_qk(0, [prev_stt] if prev_stt else [])

                def emit_stt(b):
                    par = b % 2
                    return P.emit("v", lambda e, h=h, BW=BW, par=par,
                                  dqb=dqb: e.scalar_tensor_tensor(
                        u4[par][:, 0:BW], dqb[:, 0:BW], slopes[h],
                        pq4[:, 0:BW], ALU.mult, ALU.add),
                        waits=[qk_toks[b], ab_tok, prev_u4[par]], inc="vp")
                stt_toks = [None] * (NB + 1)
                stt_toks[0] = emit_stt(0)
                if NB > 1:
                    qk_toks[1] = emit_qk(1, [stt_toks[0]])
                for b in range(NB):
                    par = b % 2
                    stt = stt_toks[b]
                    th = P.emit("s", lambda e, h=h, b=b, BW=BW, par=par:
                                e.activation(
                        out=u4[par][:, 0:BW], in_=u4[par][:, 0:BW],
                        func=AF.Tanh, scale=CS,
                        bias=abias[:, h * NBB + b: h * NBB + b + 1]),
                        waits=[stt], inc="sp")
                    ex = P.emit("s", lambda e, BW=BW, par=par: e.activation(
                        out=p4b[par][:, 0:BW], in_=u4[par][:, 0:BW],
                        func=AF.Exp, scale=SOFTCAP),
                        waits=[th, prev_p4[par]], inc="sp")
                    prev_u4[par] = ex
                    if b + 1 < NB:
                        stt_toks[b + 1] = emit_stt(b + 1)
                    if b + 2 < NB:
                        qk_toks[b + 2] = emit_qk(b + 2, [stt_toks[b + 1]])
                    if 4 * b >= NSKIP:
                        mo = (4 * b - NSKIP) * TW
                        mk = P.emit("v", lambda e, mo=mo, BW=BW, par=par:
                                    e.tensor_mul(
                            p4b[par][:, 0:BW], p4b[par][:, 0:BW],
                            mbuf[:, mo: mo + BW]),
                            waits=[ex, m_tok], inc="vp")
                        last_mask = mk
                    else:
                        mk = ex
                    pv = None
                    for i in range(4):
                        s = 4 * b + i
                        pv = P.emit("t", lambda e, s=s, i=i, g=g, TW=TW, b=b,
                                    NB=NB, par=par: e.matmul(
                            pf4[:, 0:TW],
                            vtm[:, s * GW + g * 128: s * GW + g * 128 + 128],
                            p4b[par][:, i * TW:(i + 1) * TW],
                            start=(b == 0 and i == 0),
                            stop=(b == NB - 1 and i == 3)),
                            waits=[mk] + ([prev_ctx] if b == 0 and i == 0
                                          else []),
                            inc=None)
                    dn = None
                    for i in range(4):
                        dn = P.emit("t", lambda e, i=i, TW=TW, b=b, NB=NB,
                                    par=par: e.matmul(
                            pf5[0:1, 0:TW], self.sb["ones_col_sb"][:],
                            p4b[par][:, i * TW:(i + 1) * TW],
                            start=(b == 0 and i == 0),
                            stop=(b == NB - 1 and i == 3)),
                            waits=[prev_den] if b == 0 and i == 0 else [],
                            inc="tp" if i == 3 else None)
                    prev_p4[par] = dn
                    if b % 2 == 0:
                        self.emit_quant_ops(1)
                prev_stt = stt_toks[NB - 1]
                dc_ = P.emit("v", lambda e, TW=TW: e.tensor_copy(
                    out=dsb[:, 0:TW], in_=pf5[0:1, 0:TW]),
                    waits=[prev_p4[(NB - 1) % 2], prev_dsb], inc="vp")
                prev_den = dc_
                rcd = P.emit("v", lambda e, TW=TW: e.reciprocal(
                    dsb[:, 0:TW], dsb[:, 0:TW]), waits=[dc_], inc="vp")
                bc = P.emit("t", lambda e, TW=TW: e.matmul(
                    pq4[:, 0:TW], orf[:], dsb[:, 0:TW],
                    start=True, stop=True),
                    waits=[rcd, prev_stt, self.cl], inc="tp")
                ev = P.emit("s", lambda e, h=h, qoff=qoff, TW=TW: e.activation(
                    out=ctxb[:, h * OWN + qoff: h * OWN + qoff + TW],
                    in_=pf4[:, 0:TW], func=AF.Copy),
                    waits=[prev_p4[(NB - 1) % 2]], inc="sp")
                prev_ctx = ev
                dv = P.emit("v", lambda e, h=h, qoff=qoff, TW=TW:
                            e.tensor_tensor(
                    ctxb[:, h * OWN + qoff: h * OWN + qoff + TW],
                    ctxb[:, h * OWN + qoff: h * OWN + qoff + TW],
                    pq4[:, 0:TW], ALU.mult),
                    waits=[bc, ev], inc="vp")
                prev_dsb = bc
                prev_stt = dv
                self.ctx_done.append(dv)
            msl.done(last_mask)
        if self.taps:
            P.dma("y", lambda e: e.dma_start(out=self.ap["t_ctx"], in_=ctxb[:]),
                  waits=[self.ctx_done[-1]], inc="tap")
            self.tap_kv = ("tap", P.cnt["tap"])
        self.close_arena("L2")
        self.close_arena("Lqf")
        self.close_arena("Lkv")

    def wo_ln2(self):
        c, P = self.cfg, self.P
        D, DC, OWN, H = c["D"], c["DC"], c["OWN"], c["H"]
        NSLOT, NTC = c["NSLOT"], (OWN + 511) // 512
        pf = self.pf
        ctxb = self.sb["ctxb"]
        gate = [self.ctx_done[-1]]
        if self.taps and self.tap_kv:
            gate.append(self.tap_kv)

        ln2x = self.sbuf("L3b", "ln2x", [128, DC * OWN], BF16)
        x1 = self.sbuf("L3", "x1", [128, DC * OWN], F32)
        xo = self.sbuf("L3", "xw", [128, D], F32)
        wob = [self.sbuf("L3", f"wob{i}", [128, H * 128], BF16) for i in (0, 1)]
        x1b = self.sbuf("L3", "x1b", [128, 512], BF16)
        x1q = self.sbuf("L3", "x1q", [128, 512], BF16)
        smm = self.sbuf("L3", "smm", [1, 512], F32)
        smv = self.sbuf("L3", "smv", [1, 512], F32)
        smr = self.sbuf("L3", "smr", [1, 512], F32)
        tmp = self.sbuf("L3", "tmpw", [128, 512], F32)

        xsl = Slot(P, "xw0")
        wsl = [Slot(P, "wol0"), Slot(P, "wol1")]
        orf = self.sb["orf_sb"]

        prev_pf5 = None
        prev_ev = None
        for k in range(NSLOT):
            tlx = xsl.load("g", lambda e, k=k: e.dma_start(
                out=xo[:], in_=self.ap["x_own"][k * 128:(k + 1) * 128, :]),
                extra=gate if k == 0 else [])
            for j in range((DC + 3) // 4):
                dcs = list(range(4 * j, min(4 * j + 4, DC)))
                n4 = len(dcs)
                tt = None
                for ii, dc in enumerate(dcs):
                    tt = P.emit("t", lambda e, dc=dc, ii=ii: e.transpose(
                        pf[5][:, ii * 128:(ii + 1) * 128],
                        xo[:, dc * 128:(dc + 1) * 128],
                        self.sb["ident_f_sb"][:]),
                        waits=([tlx, prev_pf5, self.cl] if ii == 0 else []),
                        inc="tp" if ii == n4 - 1 else None)
                ev = P.emit("s", lambda e, k=k, j=j, n4=n4: e.copy(
                    out=x1[:].rearrange("p (dc n) -> p dc n", n=OWN)
                        [:, 4 * j:4 * j + n4, k * 128:(k + 1) * 128],
                    in_=pf[5][:].rearrange("p (q n) -> p q n", n=128)[:, 0:n4, :]),
                    waits=[tt, prev_ev], inc="sp")
                prev_pf5 = ev
                prev_ev = ev
            xsl.done(prev_ev)
        xfm_tok = prev_ev
        self.x1st = Slot(P, "x1st")
        ln2_done = []
        prev_bank = [None, None]
        prev_statmm = [None, None]
        prev_apply = None
        prev_x1bq = None
        prev_small = None
        for tc in range(NTC):
            off = tc * 512
            tw = min(512, OWN - off)
            bank = tc % 2
            stat_mm = [None, None]
            for dc in range(DC):
                par = dc % 2
                twl = wsl[par].load("g", lambda e, dc=dc, par=par: e.dma_start(
                    out=wob[par][:].rearrange("p (hc n) -> p hc n", n=128),
                    in_=self.ap["wo"][:, dc * 128:(dc + 1) * 128]
                        .rearrange("(hc p) n -> p hc n", p=128)),
                    extra=gate if (tc == 0 and dc < 2) else [])
                mmt = None
                for hc in range(H):
                    mmt = P.emit("t", lambda e, bank=bank, par=par, hc=hc, off=off,
                                 tw=tw: e.matmul(
                        pf[bank][:, 0:tw],
                        wob[par][:, hc * 128:(hc + 1) * 128],
                        ctxb[:, hc * OWN + off: hc * OWN + off + tw],
                        start=(hc == 0), stop=(hc == H - 1)),
                        waits=([twl, prev_bank[bank], xfm_tok] + gate)
                              if hc == 0 else [],
                        inc="tp" if hc == H - 1 else None)
                wsl[par].done(mmt)
                stt = P.emit("v", lambda e, bank=bank, dc=dc, off=off, tw=tw:
                             e.scalar_tensor_tensor(
                    x1[:, dc * OWN + off: dc * OWN + off + tw],
                    pf[bank][:, 0:tw], self.sb["bo_fm_sb"][:, dc:dc + 1],
                    x1[:, dc * OWN + off: dc * OWN + off + tw],
                    ALU.add, ALU.add),
                    waits=[mmt, self.cl], inc="vp")
                prev_bank[bank] = stt
                c1 = P.emit("s", lambda e, dc=dc, off=off, tw=tw: e.activation(
                    out=x1b[:, 0:tw],
                    in_=x1[:, dc * OWN + off: dc * OWN + off + tw],
                    func=AF.Copy), waits=[stt, prev_x1bq], inc="sp")
                c2 = P.emit("s", lambda e, dc=dc, off=off, tw=tw: e.activation(
                    out=x1q[:, 0:tw],
                    in_=x1[:, dc * OWN + off: dc * OWN + off + tw],
                    func=AF.Square), waits=[stt], inc="sp")
                m1 = P.emit("t", lambda e, dc=dc, tw=tw: e.matmul(
                    pf[2][0:1, 0:tw], self.sb["ones_col_sb"][:], x1b[:, 0:tw],
                    start=(dc == 0), stop=(dc == DC - 1)),
                    waits=[c1] + ([prev_statmm[0]] if dc == 0 else []), inc="tp")
                m2 = P.emit("t", lambda e, dc=dc, tw=tw: e.matmul(
                    pf[3][0:1, 0:tw], self.sb["ones_col_sb"][:], x1q[:, 0:tw],
                    start=(dc == 0), stop=(dc == DC - 1)),
                    waits=[c2] + ([prev_statmm[1]] if dc == 0 else []), inc="tp")
                stat_mm = [m1, m2]
                prev_x1bq = m2
                self.emit_quant_ops(3)
            s1 = P.emit("v", lambda e, tw=tw: e.tensor_scalar(
                smm[:, 0:tw], pf[2][0:1, 0:tw], 1.0 / D, None, ALU.mult),
                waits=[stat_mm[0], prev_small], inc="vp")
            s2 = P.emit("v", lambda e, tw=tw: e.tensor_scalar(
                smv[:, 0:tw], pf[3][0:1, 0:tw], 1.0 / D, LN_EPS,
                ALU.mult, ALU.add), waits=[stat_mm[1]], inc="vp")
            s3 = P.emit("v", lambda e, tw=tw: e.tensor_mul(
                smr[:, 0:tw], smm[:, 0:tw], smm[:, 0:tw]), waits=[s1], inc="vp")
            s4 = P.emit("v", lambda e, tw=tw: e.tensor_sub(
                smv[:, 0:tw], smv[:, 0:tw], smr[:, 0:tw]), waits=[s2, s3], inc="vp")
            s5 = P.emit("s", lambda e, tw=tw: e.activation(
                out=smr[:, 0:tw], in_=smv[:, 0:tw], func=AF.Ln),
                waits=[s4], inc="sp")
            s6 = P.emit("s", lambda e, tw=tw: e.activation(
                out=smr[:, 0:tw], in_=smr[:, 0:tw], func=AF.Exp, scale=-0.5),
                waits=[s5], inc="sp")
            b1 = P.emit("t", lambda e, tw=tw: e.matmul(
                pf[2][:, 0:tw], orf[:], smm[:, 0:tw], start=True, stop=True),
                waits=[s6, s1], inc="tp")
            b2 = P.emit("t", lambda e, tw=tw: e.matmul(
                pf[3][:, 0:tw], orf[:], smr[:, 0:tw], start=True, stop=True),
                waits=[s6], inc="tp")
            prev_small = s6
            ap_t = prev_apply
            for dc in range(DC):
                a1 = P.emit("v", lambda e, dc=dc, off=off, tw=tw: e.tensor_tensor(
                    tmp[:, 0:tw], x1[:, dc * OWN + off: dc * OWN + off + tw],
                    pf[2][:, 0:tw], ALU.subtract),
                    waits=[b1, prev_bank[0], prev_bank[1], ap_t], inc="vp")
                ap_t = P.emit("v", lambda e, dc=dc, off=off, tw=tw: e.tensor_tensor(
                    ln2x[:, dc * OWN + off: dc * OWN + off + tw],
                    tmp[:, 0:tw], pf[3][:, 0:tw], ALU.mult),
                    waits=[b2, a1], inc="vp")
                ln2_done.append(ap_t)
            prev_apply = ap_t
            prev_statmm = [ap_t, ap_t]
        x1_stores = []
        for dc in range(DC):
            tk = self.x1st.load("y", lambda e, dc=dc: e.dma_start(
                out=self.ap["x1s"][dc * 128:(dc + 1) * 128, :],
                in_=x1[:, dc * OWN:(dc + 1) * OWN]), extra=[ln2_done[-1]])
            x1_stores.append(tk)
        self.ln2_done = ln2_done
        if self.taps:
            P.sem("tap")
            P.dma("y", lambda e: e.dma_start(out=self.ap["t_ctx"], in_=ctxb[:]),
                  waits=[self.ctx_done[-1]], inc="tap")
            for dc in range(DC):
                P.dma("y", lambda e, dc=dc: e.dma_start(
                    out=self.ap["t_x1"][dc * 128:(dc + 1) * 128, :],
                    in_=x1[:, dc * OWN:(dc + 1) * OWN]),
                    waits=[ln2_done[-1]], inc="tap")
            P.dma("y", lambda e: e.dma_start(out=self.ap["t_ln2"], in_=ln2x[:]),
                  waits=[ln2_done[-1]], inc="tap")
        self.close_arena("Lctx")
        self.tap_x1_gate = ("tap", P.cnt.get("tap", 0)) if self.taps else None
        self.close_arena("L3")

    def ffn(self):
        c, P = self.cfg, self.P
        D, DC, OWN, HFF, NH2 = c["D"], c["DC"], c["OWN"], c["HFF"], c["NH2"]
        NTC = (OWN + 511) // 512
        NST = c["NSTRIP"]
        pq4 = self.pq4
        pf4, pf5 = self.pf[4], self.pf[5]
        ln2x = self.sb["ln2x"]
        orf = self.sb["orf_sb"]
        DBW = min(512, D)
        NDB = D // DBW
        assert NDB * DBW == D
        WB2 = self.qWB2
        NRS = 2 * DC

        hn = self.sbuf("L4", "hn", [128, NH2 * OWN], BF16)
        ring1 = self.sbuf("L4", "ring1", [128, DC * 512], BF16,
                          is_transpose_dest=True)
        ring2 = self.sbuf("L4c", "ring2", [128, DC * 512], BF16,
                          is_transpose_dest=True)

        def rslice(sl, w=512):
            if sl < DC:
                return ring1[:, sl * 512: sl * 512 + w]
            return ring2[:, (sl - DC) * 512: (sl - DC) * 512 + w]

        def pbank(i, tw=512):
            if i < 4:
                return pq4[:, i * 512: i * 512 + tw]
            return (pf4 if i == 4 else pf5)[:, 0:tw]
        rsl = [Slot(P, f"rg{i}") for i in range(NRS)]
        P.sem("outst0")
        P.sem("outst1")
        x1full = ("x1st", P.cnt["x1st"])
        gate0 = [x1full, self.ln2_done[-1]]
        if self.taps:
            gate0.append(self.tap_x1_gate)
        self.qstore_q = "g"

        hn_last = None
        prev_bank = [None] * 6
        rot = 0
        prev_evac = {}
        for sidx in range(NST):
            j2, half = sidx // 2, sidx % 2
            r0 = half * HFF + j2 * 512
            base = (sidx % 2) * DC
            qts = []
            for rb in range(4):
                key = ("w1h", (r0 + rb * 128) // 128)
                while key not in self.qtok:
                    self.emit_quant_ops(4)
                qts.append(self.qtok[key])
            ltoks = []
            for dc in range(DC):
                sl = base + dc
                lt = rsl[sl].load("y",
                                  lambda e, r0=r0, dc=dc, sl=sl: e.dma_start(
                    out=rslice(sl),
                    in_=self.ap["w1h"][r0:r0 + 512, dc * 128:(dc + 1) * 128],
                    transpose=True),
                    extra=qts + (gate0 if dc < 2 else []))
                ltoks.append(lt)
            strip_evacs = []
            for hc in range(4):
                hcg = j2 * 4 + hc
                for tc in range(NTC):
                    off = tc * 512
                    tw = min(512, OWN - off)
                    bank = rot % 6
                    rot += 1
                    mmt = None
                    for dc in range(DC):
                        mmt = P.emit("t", lambda e, base=base, hc=hc, dc=dc,
                                     off=off, tw=tw, bank=bank: e.matmul(
                            pbank(bank, tw),
                            rslice(base + dc)[:, hc * 128: hc * 128 + 128],
                            ln2x[:, dc * OWN + off: dc * OWN + off + tw],
                            start=(dc == 0), stop=(dc == DC - 1)),
                            waits=[ltoks[dc]] + ([prev_bank[bank],
                                    self.ln2_done[-1]] if dc == 0 else []),
                            inc="tp" if dc == DC - 1 else None)
                    if half == 0:
                        evt = P.emit("s", lambda e, hcg=hcg, off=off, tw=tw,
                                     bank=bank: e.activation(
                            out=hn[:, hcg * OWN + off: hcg * OWN + off + tw],
                            in_=pbank(bank, tw),
                            func=AF.Silu),
                            waits=[mmt, prev_evac.get((hcg, tc))], inc="sp")
                    else:
                        evt = P.emit("v", lambda e, hcg=hcg, off=off, tw=tw,
                                     bank=bank: e.tensor_mul(
                            hn[:, hcg * OWN + off: hcg * OWN + off + tw],
                            hn[:, hcg * OWN + off: hcg * OWN + off + tw],
                            pbank(bank, tw)),
                            waits=[mmt, prev_evac.get((hcg, tc))], inc="vp")
                        hn_last = evt
                    prev_bank[bank] = evt
                    prev_evac[(hcg, tc)] = evt
                    strip_evacs.append(evt)
                    self.emit_quant_ops(2)
            for dc in range(DC):
                rsl[base + dc].done(strip_evacs[-1])
        self.emit_quant(len(self.qsched))
        if self.taps and self.taps >= 3:
            P.dma("y", lambda e: e.dma_start(out=self.ap["t_hn"], in_=hn[:]),
                  waits=[hn_last], inc="tap")
        self.close_arena("L4c")

        sq0 = self.sbuf("L4b", "sq0", [128, OWN], BF16)
        sq = [sq0, sq0]
        srr = self.sbuf("L4b", "srr", [1, 512], F32)
        rbc = self.sbuf("L4b", "rbc", [128, OWN], F32)
        obuf = [self.sbuf("L4b", f"obuf{i}", [128, 512], F32) for i in (0, 1)]
        x1t = [self.sbuf("L4b", f"x1t{i}", [128, 512], F32) for i in (0, 1)]
        x1sl = [Slot(P, "x1l0"), Slot(P, "x1l1")]

        prev_sq = [None, None]
        smm = [None, None]
        for hc in range(NH2):
            par = hc % 2
            a1 = P.emit("s", lambda e, hc=hc, par=par: e.activation(
                out=sq[par][:], in_=hn[:, hc * OWN:(hc + 1) * OWN],
                func=AF.Square),
                waits=[hn_last, prev_sq[0], prev_sq[1]], inc="sp")
            for tc in range(NTC):
                off = tc * 512
                tw = min(512, OWN - off)
                pfr = pf4 if tc == 0 else pf5
                mm = P.emit("t", lambda e, par=par, off=off, tw=tw, hc=hc,
                            pfr=pfr: e.matmul(
                    pfr[0:1, 0:tw], self.sb["ones_col_sb"][:],
                    sq[par][:, off:off + tw],
                    start=(hc == 0), stop=(hc == NH2 - 1)),
                    waits=[a1] + ([prev_bank[4], prev_bank[5]]
                                  if hc == 0 else []), inc="tp")
                smm[tc] = mm
            prev_sq[0] = prev_sq[1] = smm[NTC - 1]
        rb_done = []
        prev_r = None
        for tc in range(NTC):
            off = tc * 512
            tw = min(512, OWN - off)
            pfr = pf4 if tc == 0 else pf5
            s1 = P.emit("v", lambda e, tw=tw, pfr=pfr: e.tensor_scalar(
                srr[:, 0:tw], pfr[0:1, 0:tw], 1.0 / HFF, RMS_EPS,
                ALU.mult, ALU.add), waits=[smm[tc], prev_r], inc="vp")
            s2 = P.emit("s", lambda e, tw=tw: e.activation(
                out=srr[:, 0:tw], in_=srr[:, 0:tw], func=AF.Ln),
                waits=[s1], inc="sp")
            s3 = P.emit("s", lambda e, tw=tw: e.activation(
                out=srr[:, 0:tw], in_=srr[:, 0:tw], func=AF.Exp, scale=-0.5),
                waits=[s2], inc="sp")
            b1 = P.emit("t", lambda e, tw=tw, tc=tc: e.matmul(
                pq4[:, tc * 512: tc * 512 + tw], orf[:], srr[:, 0:tw],
                start=True, stop=True),
                waits=[s3, s1, prev_bank[tc]], inc="tp")
            cp = P.emit("v", lambda e, off=off, tw=tw, tc=tc: e.tensor_copy(
                out=rbc[:, off:off + tw], in_=pq4[:, tc * 512: tc * 512 + tw]),
                waits=[b1], inc="vp")
            prev_r = cp
            rb_done.append(cp)
        prev_acc = [prev_bank[i] for i in range(4)]
        prev_ob = [None, None]
        tile_ctr = 0
        DB2 = min(256, D)
        ND2 = D // DB2
        nds = DB2 // 128
        xload = []
        for db in range(ND2):
            for ds in range(nds):
                dci = db * nds + ds
                for tc in range(NTC):
                    off = tc * 512
                    tw = min(512, OWN - off)
                    gext = ([x1full, hn_last] + prev_bank) if db == 0 \
                        else [x1full]
                    xload.append((dci, off, tw, gext))
        xtok = {}
        xst = {"cur": 0}

        def emit_xload():
            k = xst["cur"]
            if k >= len(xload):
                return
            dci, off, tw, gx = xload[k]
            xtok[k] = x1sl[k % 2].load(
                "g", lambda e, dci=dci, off=off, tw=tw, k=k: e.dma_start(
                    out=x1t[k % 2][:, 0:tw],
                    in_=self.ap["x1s"][dci * 128:(dci + 1) * 128,
                                       off:off + tw]),
                extra=gx)
            xst["cur"] = k + 1
        emit_xload()
        emit_xload()
        xpair = {"p": 0}
        for db in range(ND2):
            r0 = db * DB2
            last_mm = [None] * 4
            for hc in range(NH2):
                sl = tile_ctr % DC
                tile_ctr += 1
                qks = [self.qtok[("w2h", rb, (hc * 128) // WB2)]
                       for rb in range(r0 // 128, (r0 + DB2) // 128)]
                lt = rsl[sl].load("y",
                                  lambda e, r0=r0, hc=hc, sl=sl, DB2=DB2:
                                  e.dma_start(
                    out=rslice(sl, DB2),
                    in_=self.ap["w2h"][r0:r0 + DB2, hc * 128:(hc + 1) * 128],
                    transpose=True), extra=qks)
                for ds in range(nds):
                    for tc in range(NTC):
                        off = tc * 512
                        tw = min(512, OWN - off)
                        bank = ds * NTC + tc
                        mm = P.emit("t", lambda e, sl=sl, ds=ds, hc=hc,
                                    off=off, tw=tw, bank=bank: e.matmul(
                            pq4[:, bank * 512: bank * 512 + tw],
                            rslice(sl)[:, ds * 128: ds * 128 + 128],
                            hn[:, hc * OWN + off: hc * OWN + off + tw],
                            start=(hc == 0), stop=(hc == NH2 - 1)),
                            waits=[lt] + ([prev_acc[bank], hn_last,
                                           rb_done[-1]] if hc == 0 else []),
                            inc="tp" if (hc == NH2 - 1 or
                                         (ds == nds - 1 and tc == NTC - 1))
                                else None)
                        last_mm[bank] = mm
                rsl[sl].done(last_mm[(nds - 1) * NTC + NTC - 1])
            for ds in range(nds):
                dci = db * nds + ds
                for tc in range(NTC):
                    off = tc * 512
                    tw = min(512, OWN - off)
                    bank = ds * NTC + tc
                    opar = (dci * NTC + tc) % 2
                    p = xpair["p"]
                    xpair["p"] = p + 1
                    xpar = p % 2
                    e1 = P.emit("v", lambda e, bank=bank, off=off, tw=tw,
                                opar=opar: e.tensor_tensor(
                        obuf[opar][:, 0:tw], pq4[:, bank * 512: bank * 512 + tw],
                        rbc[:, off:off + tw], ALU.mult),
                        waits=[last_mm[bank], rb_done[-1], prev_ob[opar]],
                        inc="vp")
                    e2 = P.emit("v", lambda e, tw=tw, opar=opar,
                                xpar=xpar: e.tensor_add(
                        obuf[opar][:, 0:tw], obuf[opar][:, 0:tw],
                        x1t[xpar][:, 0:tw]),
                        waits=[xtok[p], e1], inc="vp")
                    ot = P.dma("g", lambda e, dci=dci, off=off, tw=tw,
                               opar=opar: e.dma_start(
                        out=self.ap["out_fm"][dci * 128:(dci + 1) * 128,
                                              off:off + tw],
                        in_=obuf[opar][:, 0:tw]),
                        waits=[e2], inc=f"outst{opar}")
                    prev_ob[opar] = ot
                    prev_acc[bank] = e1
                    x1sl[xpar].done(e2)
                    emit_xload()
        final_waits = [("outst0", P.cnt["outst0"]),
                       ("outst1", P.cnt["outst1"])]
        if self.taps:
            final_waits.append(("tap", P.cnt["tap"]))
        P.emit("y", lambda e: e.nop(), waits=final_waits)

def build(cfg, taps=False, stop_after=None):
    return Kern(cfg, taps=taps, stop_after=stop_after).build()


def kernel(L0_attn_hidden_states, L0_attn_Wq, L0_attn_Wk, L0_attn_Wv,
           L0_attn_Wo, L0_attn_bo, L0_ffn_w_gate_up, L0_ffn_w_proj):
    from concourse.bass_utils import run_bass_kernel_spmd
    cfg = REAL
    ins = dict(
        x=np.asarray(L0_attn_hidden_states, dtype=np.float32),
        Wq=np.asarray(L0_attn_Wq, dtype=np.float32),
        Wk=np.asarray(L0_attn_Wk, dtype=np.float32),
        Wv=np.asarray(L0_attn_Wv, dtype=np.float32),
        Wo=np.asarray(L0_attn_Wo, dtype=np.float32),
        bo=np.asarray(L0_attn_bo, dtype=np.float32),
        w_gate_up=np.asarray(L0_ffn_w_gate_up, dtype=np.float32),
        w_proj=np.asarray(L0_ffn_w_proj, dtype=np.float32),
    )
    in_maps = host_inputs(cfg, ins)
    nc = build(cfg, taps=False)
    res = run_bass_kernel_spmd(nc, in_maps, core_ids=list(range(8)))
    return assemble(cfg, res.results)
